# version 02
# baseline (speedup 1.0000x reference)
import math
import numpy as np
import ml_dtypes
import concourse.bass as bass
import concourse.mybir as mybir
from concourse.bass_utils import run_bass_kernel_spmd

F32 = mybir.dt.float32
BF16 = mybir.dt.bfloat16
I32 = mybir.dt.int32
AF = mybir.ActivationFunctionType
ALU = mybir.AluOpType
AX = mybir.AxisListType

NP = 8
B, L, D = 2, 4096, 2048
T = 1024
DFF = 8192
TC = T // 128
DC = D // 128
NHP = 16
MAGIC = float(np.float32(1.5 * 2 ** 23))
TWO_PI = 2.0 * math.pi

OFF_Q = 0
OFF_K = OFF_Q + 256 * D
OFF_V = OFF_K + 256 * D
OFF_O = OFF_V + 256 * D
OFF_W1 = OFF_O + 256 * D
OFF_W2 = OFF_W1 + 256 * DFF
R1 = OFF_W2 + 1024 * D

OFF_KV_V = T * D
R2 = 2 * T * D

UPTO = 99
DEBUG_PHASE = 0
SKIP = set()

CSEM = {"vector": "vv", "scalar": "ss", "gpsimd": "gc", "tensor": "tt"}
SYNC_LANES = ["Lx", "Lt", "Lp", "Ls", "Lr", "W0", "W1", "W2", "W3",
              "E0", "E1", "E2", "E3", "V0", "V1", "Lb", "Lh", "Lf", "Lg"]
SCAL_LANES = ["Tl", "Tw", "Tk", "Z0", "Z1", "Z2", "Z3"]
GP_LANES = ["Gd"]
LANE_ENG = {**{n: "sync" for n in SYNC_LANES},
            **{n: "scalar" for n in SCAL_LANES},
            **{n: "gpsimd" for n in GP_LANES}}
EVL = ["E0", "E1", "E2", "E3"]
WL = ["W0", "W1", "W2", "W3"]
ESEM = {"sync": SYNC_LANES, "scalar": SCAL_LANES + ["ss"], "vector": ["vv"],
        "tensor": ["tt", "mm", "mq"], "gpsimd": GP_LANES + ["gc"]}
SEM_NAMES = (SYNC_LANES + SCAL_LANES + GP_LANES +
             ["ag", "vv", "ss", "tt", "mm", "mq", "gc"])


def _cody_waite_consts():
    def chop(x):
        v = np.frombuffer(np.float32(x).tobytes(), dtype=np.uint32)[0]
        v = np.uint32(v & np.uint32(0xFFFFF000))
        return float(np.frombuffer(v.tobytes(), dtype=np.float32)[0])
    c1 = chop(TWO_PI)
    c2 = chop(TWO_PI - c1)
    c3 = float(np.float32(TWO_PI - c1 - c2))
    return c1, c2, c3


class Prog:
    ENGS = ("sync", "scalar", "vector", "tensor", "gpsimd")

    def __init__(self):
        self.ops = {e: [] for e in self.ENGS}
        self.cnt = {}
        self.max_waiter = {}
        self.gated_to = {}

    def on(self, eng, fn):
        self.ops[eng].append(fn)

    def inc(self, name, by=1):
        self.cnt[name] = self.cnt.get(name, 0) + by
        return self.cnt[name]

    def val(self, name):
        return self.cnt.get(name, 0)

    def wait(self, eng, name, val=None):
        v = self.cnt.get(name, 0) if val is None else val
        if name in LANE_ENG:
            self.max_waiter[name] = max(self.max_waiter.get(name, 0), v)
        self.on(eng, lambda e, S, n=name, v=v: e.wait_ge(S[n], v))

    def drain(self, eng):
        self.on(eng, lambda e, S: e.drain())

    def emit(self, nc, sems):
        with nc.Block() as block:
            for eng_name in self.ENGS:
                deco = getattr(block, eng_name)
                ops = self.ops[eng_name]

                def make(ops):
                    def body(e):
                        for fn in ops:
                            fn(e, sems)
                    return body
                deco(make(ops))


def build_nc():
    nc = bass.Bass("TRN2", target_bir_lowering=False, debug=False)

    x_in = nc.dram_tensor("x", [T, D], F32, kind="ExternalInput").ap()
    pos_in = nc.dram_tensor("pos", [1, T], F32, kind="ExternalInput").ap()
    w_in = {m: nc.dram_tensor(f"w{m}", [D, 256], F32, kind="ExternalInput").ap()
            for m in "qkvo"}
    w1_in = nc.dram_tensor("w1", [DFF, 256], F32, kind="ExternalInput").ap()
    w2_in = nc.dram_tensor("w2", [D, 1024], F32, kind="ExternalInput").ap()
    bo_in = nc.dram_tensor("bo", [1, D], F32, kind="ExternalInput").ap()
    lnw_in = nc.dram_tensor("lnw", [1, D], F32, kind="ExternalInput").ap()
    lnb_in = nc.dram_tensor("lnb", [1, D], F32, kind="ExternalInput").ap()
    b1_in = nc.dram_tensor("b1", [1, DFF], F32, kind="ExternalInput").ap()
    b2_in = nc.dram_tensor("b2", [1, D], F32, kind="ExternalInput").ap()
    y_out = nc.dram_tensor("y", [T, D], F32, kind="ExternalOutput").ap()

    ag1_in = nc.dram_tensor("ag1_in", [1, R1], BF16, kind="Internal").ap()
    ag1_out = nc.dram_tensor("ag1_out", [1, NP * R1], BF16, kind="Internal",
                             addr_space="Shared").ap()
    kv_in = nc.dram_tensor("kv_in", [1, R2], BF16, kind="Internal").ap()
    kv_out = nc.dram_tensor("kv_out", [1, NP * R2], BF16, kind="Internal",
                            addr_space="Shared").ap()
    s_w = {m: nc.dram_tensor(f"s_w{m}", [D, 256], BF16, kind="Internal").ap()
           for m in "qkvo"}
    s_w1 = nc.dram_tensor("s_w1", [DFF, 256], BF16, kind="Internal").ap()
    s_w2 = nc.dram_tensor("s_w2", [D, 1024], BF16, kind="Internal").ap()
    s_ln1x = nc.dram_tensor("s_ln1x", [T, D], BF16, kind="Internal").ap()
    s_qhat = nc.dram_tensor("s_qhat", [T, D], BF16, kind="Internal").ap()
    s_z = nc.dram_tensor("s_z", [T, D], F32, kind="Internal").ap()
    s_x1 = nc.dram_tensor("s_x1", [T, D], F32, kind="Internal").ap()
    s_h0 = nc.dram_tensor("s_h0", [T, D], BF16, kind="Internal").ap()
    s_hq = nc.dram_tensor("s_hq", [T, D], BF16, kind="Internal").ap()
    s_g = nc.dram_tensor("s_g", [T, DFF], BF16, kind="Internal").ap()
    s_gq = nc.dram_tensor("s_gq", [T, DFF], BF16, kind="Internal").ap()
    s_fo = nc.dram_tensor("s_fo", [T, D], BF16, kind="Internal").ap()

    dbg = {}
    if DEBUG_PHASE >= 1:
        dbg["ag1"] = nc.dram_tensor("d_ag1", [1, R1], BF16,
                                    kind="ExternalOutput").ap()
        dbg["ln1x"] = nc.dram_tensor("d_ln1x", [T, D], BF16,
                                     kind="ExternalOutput").ap()
    if DEBUG_PHASE >= 2:
        dbg["qhat"] = nc.dram_tensor("d_qhat", [T, D], BF16,
                                     kind="ExternalOutput").ap()
        dbg["kv"] = nc.dram_tensor("d_kv", [1, R2], BF16,
                                   kind="ExternalOutput").ap()
    if DEBUG_PHASE >= 3:
        dbg["z"] = nc.dram_tensor("d_z", [T, D], F32,
                                  kind="ExternalOutput").ap()
    if DEBUG_PHASE >= 4:
        dbg["x1"] = nc.dram_tensor("d_x1", [T, D], F32,
                                   kind="ExternalOutput").ap()
        dbg["hq"] = nc.dram_tensor("d_hq", [T, D], BF16,
                                   kind="ExternalOutput").ap()
    if DEBUG_PHASE >= 5:
        dbg["gq"] = nc.dram_tensor("d_gq", [T, DFF], BF16,
                                   kind="ExternalOutput").ap()

    p = Prog()

    NF32 = 15360
    NBF = 68096

    fz = {}

    def F(name, start, n):
        assert start + n <= NF32, (name, start + n)
        fz[name] = (start, n)
    F("prep_f", 0, 2048)
    F("q_v", 2048, 2048)
    F("q_r", 4096, 2048)
    F("rope_t1", 8192, 512)
    F("rope_t2", 8704, 512)
    F("cosT", 9216, 512)
    F("sinS", 9728, 512)
    F("rp_th", 10240, 64)
    F("rp_u", 10304, 64)
    F("rp_k", 10368, 64)
    F("rp_r", 10432, 64)
    F("rp_si", 10496, 64)
    F("rp_ci", 10560, 64)
    F("rp_sin", 10624, 64)
    F("invfB", 10688, 64)
    F("msk_f", 10752, 512)
    F("x_t", 11264, 2048)
    F("q2_v", 13312, 1024)
    F("q2_r", 14336, 1024)
    F("smrcp", 0, 512)
    F("z_t", 512, 2048)
    F("t_t", 2560, 2048)
    F("bo_b", 4608, 2048)
    F("lnw_b", 6656, 2048)
    F("lnb_b", 8704, 2048)
    F("bias_b", 10752, 512)
    F("y2_f", 0, 512)
    F("b1full", 512, 8192)
    F("b2full", 8704, 2048)

    bz = {}

    def Bf(name, start, n):
        assert start + n <= NBF, (name, start + n)
        bz[name] = (start, n)
    Bf("lfm", 0, 16384)
    Bf("wtr", 16384, 8192)
    Bf("prep_q", 24576, 2048)
    Bf("lnx_bf", 24576, 2048)
    Bf("ev0", 26624, 512)
    Bf("ev1", 27136, 512)
    Bf("ev2", 27648, 512)
    Bf("ev3", 28160, 512)
    Bf("wrhs0", 28672, 512)
    Bf("wrhs1", 29184, 512)
    Bf("wrhs2", 29696, 512)
    Bf("wrhs3", 30208, 512)
    Bf("qfm", 0, 16384)
    Bf("afm", 16384, 16384)
    Bf("khp", 32768, 2048)
    Bf("A0", 34816, 512)
    Bf("A1", 35328, 512)
    Bf("A2", 35840, 512)
    Bf("A3", 36352, 512)
    Bf("vp0", 36864, 128)
    Bf("vp1", 36992, 128)
    Bf("selo", 37120, 1024)
    Bf("mskA", 38144, 2048)
    Bf("wrhsB0", 40960, 512)
    Bf("wrhsB1", 41472, 512)
    Bf("wrhsB2", 41984, 512)
    Bf("wrhsB3", 42496, 512)
    Bf("h0st", 16384, 2048)
    Bf("hst", 18432, 2048)
    Bf("hqst", 20480, 2048)
    Bf("hqfm", 0, 16384)
    Bf("y1g0", 30720, 512)
    Bf("y1h0", 31232, 512)
    Bf("gqs0", 31744, 512)
    Bf("y1g1", 32256, 512)
    Bf("y1h1", 32768, 512)
    Bf("gqs1", 33280, 512)
    Bf("glhs0", 18432, 1024)
    Bf("glhs1", 19456, 1024)
    Bf("wrhsD0", 65536, 512)
    Bf("wrhsD1", 66048, 512)
    Bf("h0t0", 33792, 512)
    Bf("h0t1", 34304, 512)
    Bf("fot0", 66560, 512)
    Bf("fot1", 67072, 512)
    Bf("fo_ld", 65536, 2048)

    sm = {}

    def SMT(name, shape, dt):
        sm[name] = (shape, dt)
    SMT("ln_s1", [128, 1], F32)
    SMT("ln_s1b", [128, 1], F32)
    SMT("ln_s2", [128, 1], F32)
    SMT("ln_s2b", [128, 1], F32)
    SMT("ln_mu", [128, 1], F32)
    SMT("ln_var", [128, 1], F32)
    SMT("ln_sd", [128, 1], F32)
    SMT("ln_r", [128, 1], F32)
    SMT("ln_nmr", [128, 1], F32)
    SMT("iv_i", [128, 64], I32)
    SMT("rp_pos", [128, 1], F32)
    SMT("bm10", [128, 1], F32)
    SMT("m512", [128, 1], F32)
    SMT("onescol", [128, 1], BF16)
    SMT("onesrow", [128, 64], BF16)
    SMT("serow", [128, 1024], F32)
    SMT("wq_amax", [128, 64], F32)
    SMT("wq_amax2", [128, 64], F32)
    SMT("wq_bits", [128, 64], I32)
    SMT("wq_m", [128, 64], I32)
    SMT("wq_mf", [128, 64], F32)
    SMT("wq_mf2", [128, 64], F32)
    SMT("wq_sb", [128, 64], I32)
    SMT("wq_scale", [128, 64], F32)
    SMT("wq_inv", [128, 64], F32)
    SMT("msk_ii", [128, 512], I32)
    SMT("msk_s", [128, 512], F32)

    sb = {}
    ps = {}

    def A(name):
        s0, n = fz[name]
        return sb["AF32"][:, s0:s0 + n]

    def Abf(name):
        s0, n = bz[name]
        return sb["ABF"][:, s0:s0 + n]

    def Asm(name):
        return sb[name][:, :]

    def _lane_gate(lane):
        eng = LANE_ENG[lane]
        mw = p.max_waiter.get(lane, 0)
        if mw > p.gated_to.get(lane, 0):
            p.on(eng, lambda e, S, n=lane, v=mw: e.wait_ge(S[n], v))
            p.gated_to[lane] = mw

    def dma(lane, dst_fn, src_fn):
        eng = LANE_ENG[lane]
        _lane_gate(lane)
        v = p.inc(lane, 16)
        p.on(eng, lambda e, S, d=dst_fn, s=src_fn, sem=lane:
             e.dma_start(out=d(), in_=s()).then_inc(S[sem], 16))
        return v

    def dmaT(lane, dst_fn, src_fn):
        eng = LANE_ENG[lane]
        _lane_gate(lane)
        v = p.inc(lane, 16)
        p.on(eng, lambda e, S, d=dst_fn, s=src_fn, sem=lane:
             e.dma_start_transpose(out=d(), in_=s()).then_inc(S[sem], 16))
        return v

    def op(eng, mk, inc=False, sem=None):
        if inc:
            sem = sem or CSEM[eng]
            v = p.inc(sem)
            p.on(eng, lambda e, S, mk=mk, sem=sem:
                 mk(e, S).then_inc(S[sem], 1))
            return v
        p.on(eng, lambda e, S, mk=mk: mk(e, S))
        return None

    def ag1_tile(off_m, rowlen, d0, col0, ncol, shard_rows=256):
        r = d0 // shard_rows
        i = d0 % shard_rows
        base = r * R1 + off_m + i * rowlen
        flat = ag1_out[0, :]
        return flat[base:base + 128 * rowlen].rearrange(
            "(p f) -> p f", f=rowlen)[:, col0:col0 + ncol]

    def kvk_ap(rank, r0, r1, hp):
        base = rank * R2
        flat = kv_out[0, :]
        return flat[base:base + T * D].rearrange(
            "(t d) -> t d", d=D)[r0:r1, 128 * hp:128 * (hp + 1)]

    def kvv_ap(rank, r0, r1, hp):
        base = rank * R2 + OFF_KV_V
        flat = kv_out[0, :]
        return flat[base:base + T * D].rearrange(
            "(t d) -> t d", d=D)[r0:r1, 128 * hp:128 * (hp + 1)]

    def quant_chain(src_fn, dst_fn, nfree, vbuf="q_v", rbuf="q_r"):
        nblk = nfree // 32
        assert nblk <= 64

        def v3(ap):
            return ap.rearrange("p (nb b) -> p nb b", b=32)
        op("vector", lambda e, S: e.tensor_reduce(
            out=Asm("wq_amax")[:, :nblk], in_=v3(src_fn()), axis=AX.X,
            op=ALU.max, apply_absolute_value=True))
        p.drain("vector")
        op("vector", lambda e, S: e.tensor_scalar(
            out=Asm("wq_amax2")[:, :nblk], in0=Asm("wq_amax")[:, :nblk],
            scalar1=1e-30, scalar2=None, op0=ALU.max))
        p.drain("vector")
        op("vector", lambda e, S: e.tensor_copy(
            out=Asm("wq_bits")[:, :nblk],
            in_=Asm("wq_amax2")[:, :nblk].bitcast(I32)))
        p.drain("vector")
        a = op("vector", lambda e, S: e.tensor_scalar(
            out=Asm("wq_m")[:, :nblk], in0=Asm("wq_bits")[:, :nblk],
            scalar1=23, scalar2=None, op0=ALU.logical_shift_right), inc=True)
        p.wait("scalar", "vv", a)
        b = op("scalar", lambda e, S: e.copy(
            out=Asm("wq_mf")[:, :nblk], in_=Asm("wq_m")[:, :nblk]), inc=True)
        p.wait("vector", "ss", b)
        c = op("vector", lambda e, S: e.tensor_scalar(
            out=Asm("wq_mf2")[:, :nblk], in0=Asm("wq_mf")[:, :nblk],
            scalar1=float(1 << 23), scalar2=-float(6 << 23),
            op0=ALU.mult, op1=ALU.add), inc=True)
        p.wait("scalar", "vv", c)
        d = op("scalar", lambda e, S: e.copy(
            out=Asm("wq_sb")[:, :nblk], in_=Asm("wq_mf2")[:, :nblk]), inc=True)
        p.wait("vector", "ss", d)
        op("vector", lambda e, S: e.tensor_copy(
            out=Asm("wq_scale")[:, :nblk],
            in_=Asm("wq_sb")[:, :nblk].bitcast(F32)))
        p.drain("vector")
        op("vector", lambda e, S: e.reciprocal(
            out=Asm("wq_inv")[:, :nblk], in_=Asm("wq_scale")[:, :nblk]))
        p.drain("vector")
        op("vector", lambda e, S: e.tensor_tensor(
            out=v3(A(vbuf)[:, :nfree]), in0=v3(src_fn()),
            in1=Asm("wq_inv")[:, :nblk].unsqueeze(2).broadcast_to(
                [128, nblk, 32]), op=ALU.mult))
        p.drain("vector")
        op("vector", lambda e, S: e.tensor_scalar(
            out=A(rbuf)[:, :nfree], in0=A(vbuf)[:, :nfree],
            scalar1=MAGIC, scalar2=MAGIC, op0=ALU.add, op1=ALU.subtract))
        p.drain("vector")
        op("vector", lambda e, S: e.tensor_scalar(
            out=A(vbuf)[:, :nfree], in0=A(rbuf)[:, :nfree],
            scalar1=127.0, scalar2=-128.0, op0=ALU.min, op1=ALU.max))
        p.drain("vector")
        fin = op("vector", lambda e, S: e.tensor_tensor(
            out=v3(dst_fn()), in0=v3(A(vbuf)[:, :nfree]),
            in1=Asm("wq_scale")[:, :nblk].unsqueeze(2).broadcast_to(
                [128, nblk, 32]), op=ALU.mult), inc=True)
        return fin

    def ln_stats(src_fn):
        op("scalar", lambda e, S: e.activation(
            out=A("q2_v"), in_=src_fn()[:, 0:1024], func=AF.Square,
            accum_out=Asm("ln_s2")))
        h1 = op("scalar", lambda e, S: e.activation(
            out=A("q2_r"), in_=src_fn()[:, 1024:2048], func=AF.Square,
            accum_out=Asm("ln_s2b")), inc=True)
        p.wait("scalar", "ss", h1)
        op("scalar", lambda e, S: e.activation(
            out=A("q2_v"), in_=src_fn()[:, 0:1024], func=AF.Identity,
            accum_out=Asm("ln_s1")))
        a = op("scalar", lambda e, S: e.activation(
            out=A("q2_r"), in_=src_fn()[:, 1024:2048], func=AF.Identity,
            accum_out=Asm("ln_s1b")), inc=True)
        p.wait("vector", "ss", a)
        op("vector", lambda e, S: e.tensor_tensor(
            out=Asm("ln_s1"), in0=Asm("ln_s1"), in1=Asm("ln_s1b"),
            op=ALU.add))
        op("vector", lambda e, S: e.tensor_tensor(
            out=Asm("ln_s2"), in0=Asm("ln_s2"), in1=Asm("ln_s2b"),
            op=ALU.add))
        p.drain("vector")
        op("vector", lambda e, S: e.tensor_scalar(
            out=Asm("ln_mu"), in0=Asm("ln_s1"), scalar1=1.0 / D,
            scalar2=None, op0=ALU.mult))
        p.drain("vector")
        op("vector", lambda e, S: e.tensor_tensor(
            out=Asm("ln_var"), in0=Asm("ln_mu"), in1=Asm("ln_mu"),
            op=ALU.mult))
        p.drain("vector")
        op("vector", lambda e, S: e.scalar_tensor_tensor(
            out=Asm("ln_var"), in0=Asm("ln_s2"), scalar=1.0 / D,
            in1=Asm("ln_var"), op0=ALU.mult, op1=ALU.subtract))
        p.drain("vector")
        b = op("vector", lambda e, S: e.tensor_scalar(
            out=Asm("ln_sd"), in0=Asm("ln_var"), scalar1=1e-5,
            scalar2=None, op0=ALU.add), inc=True)
        p.wait("scalar", "vv", b)
        c = op("scalar", lambda e, S: e.sqrt(
            out=Asm("ln_sd"), in_=Asm("ln_sd")), inc=True)
        p.wait("vector", "ss", c)
        op("vector", lambda e, S: e.reciprocal(
            out=Asm("ln_r"), in_=Asm("ln_sd")))
        p.drain("vector")
        d = op("vector", lambda e, S: e.scalar_tensor_tensor(
            out=Asm("ln_nmr"), in0=Asm("ln_mu"), scalar=-1.0,
            in1=Asm("ln_r"), op0=ALU.mult, op1=ALU.mult), inc=True)
        p.wait("scalar", "vv", d)
        return d


    prev_apply = None
    prev_lnstore = None
    for tc in range(TC):
        if prev_apply is not None:
            p.wait("sync", "ss", prev_apply)
        t = dma("Lx", lambda: A("x_t"),
                lambda tc=tc: x_in[128 * tc:128 * (tc + 1), :])
        p.wait("scalar", "Lx", t)
        ln_stats(lambda: A("x_t"))
        if prev_lnstore is not None:
            p.wait("scalar", "Ls", prev_lnstore)
        ap_ = op("scalar", lambda e, S: e.activation(
            out=Abf("lnx_bf"), in_=A("x_t"), func=AF.Identity,
            scale=Asm("ln_r"), bias=Asm("ln_nmr")), inc=True)
        prev_apply = ap_
        p.wait("sync", "ss", ap_)
        prev_lnstore = dma(
            "Ls", lambda tc=tc: s_ln1x[128 * tc:128 * (tc + 1), :],
            lambda: Abf("lnx_bf"))
    ln1_last_store = p.val("Ls")
    p.wait("scalar", "Ls", ln1_last_store)
    for dc in range(DC):
        dmaT("Tl",
             lambda dc=dc: Abf("lfm").rearrange(
                 "p (c f) -> p c f", f=1024)[:, dc, :],
             lambda dc=dc: s_ln1x[:, 128 * dc:128 * (dc + 1)])
    lfm_done = p.val("Tl")

    prep_state = {"fin": None, "fin_quant": True, "store": None}

    def prep_rows(src, dst_dram, nrow_chunk, nchunks, ncol, quant):
        prev_fin = prep_state["fin"]
        prev_quant = prep_state["fin_quant"]
        prev_store = prep_state["store"]
        for ch in range(nchunks):
            r0 = nrow_chunk * ch
            nsub = nrow_chunk // 128
            nfree = nsub * ncol
            if prev_fin is not None:
                p.wait("sync", "vv" if prev_quant else "ss", prev_fin)
            prev_quant = quant
            t0 = dma("Lp",
                     lambda nsub=nsub, ncol=ncol: A("prep_f").rearrange(
                         "p (c f) -> p c f", f=ncol)[:, 0:nsub, :],
                     lambda src=src, r0=r0, nrow_chunk=nrow_chunk:
                     src[r0:r0 + nrow_chunk, :].rearrange(
                         "(c p) f -> p c f", p=128))
            eng2 = "vector" if quant else "scalar"
            p.wait(eng2, "Lp", t0)
            if prev_store is not None:
                p.wait(eng2, "Ls", prev_store)
            if quant:
                fin = quant_chain(
                    lambda nfree=nfree: A("prep_f")[:, :nfree],
                    lambda nfree=nfree: Abf("prep_q")[:, :nfree], nfree)
            else:
                fin = op("scalar", lambda e, S, nfree=nfree: e.copy(
                    out=Abf("prep_q")[:, :nfree],
                    in_=A("prep_f")[:, :nfree]), inc=True)
            p.wait("sync", "vv" if quant else "ss", fin)
            prev_store = dma(
                "Ls",
                lambda dst_dram=dst_dram, r0=r0, nrow_chunk=nrow_chunk:
                dst_dram[r0:r0 + nrow_chunk, :].rearrange(
                    "(c p) f -> p c f", p=128),
                lambda nsub=nsub, ncol=ncol: Abf("prep_q").rearrange(
                    "p (c f) -> p c f", f=ncol)[:, 0:nsub, :])
            prev_fin = fin
        prep_state["fin"] = prev_fin
        prep_state["fin_quant"] = quant
        prep_state["store"] = prev_store
        return prev_store

    prep_rows(w1_in, s_w1, 1024, 8, 256, True)
    prep_rows(w2_in, s_w2, 256, 8, 1024, True)
    for m in "qkvo":
        prep_rows(w_in[m], s_w[m], 1024, 2, 256, False)
    wprep_stored = p.val("Ls")
    p.wait("scalar", "Ls", wprep_stored)

    def prep_transpose(src_dram, ncols_src, off_dst, nrows_src):
        for i in range(ncols_src // 128):
            tW = dmaT("Tw",
                      lambda nrows_src=nrows_src: Abf("wtr")[:, :nrows_src],
                      lambda src_dram=src_dram, i=i:
                      src_dram[:, 128 * i:128 * (i + 1)])
            p.wait("scalar", "Tw", tW)
            stW = dma("Tw",
                      lambda off_dst=off_dst, i=i, nrows_src=nrows_src:
                      ag1_in[0, off_dst + 128 * i * nrows_src:
                             off_dst + 128 * (i + 1) * nrows_src].rearrange(
                          "(p f) -> p f", p=128),
                      lambda nrows_src=nrows_src: Abf("wtr")[:, :nrows_src])
            p.wait("scalar", "Tw", stW)

    prep_transpose(s_w1, 256, OFF_W1, DFF)
    prep_transpose(s_w2, 1024, OFF_W2, D)
    for m, off in (("q", OFF_Q), ("k", OFF_K), ("v", OFF_V), ("o", OFF_O)):
        prep_transpose(s_w[m], 256, off, D)
    wprep_done = p.val("Tw")

    p.wait("gpsimd", "Tw", wprep_done)
    ag1_done = p.inc("ag")
    p.on("gpsimd", lambda e, S: e.collective_compute(
        "AllGather", ALU.bypass, replica_groups=[list(range(NP))],
        ins=[ag1_in[:, :]], outs=[ag1_out[:, :]]).then_inc(S["ag"], 1))

    if DEBUG_PHASE >= 1:
        p.wait("gpsimd", "ag", ag1_done)
        dma("Gd", lambda: dbg["ag1"][:, :], lambda: ag1_in[:, :])
        dma("Gd", lambda: dbg["ln1x"][:, :], lambda: s_ln1x[:, :])

    c1, c2, c3 = _cody_waite_consts()
    do_rope = "rope" not in SKIP
    iv = op("gpsimd", lambda e, S: e.iota(
        out=Asm("iv_i")[:, :], pattern=[[0, 2], [1, 32]], base=0,
        channel_multiplier=0), inc=True)
    mi = None
    mv = 0
    for j in ([] if "masks" in SKIP else range(4)):
        mi = op("gpsimd", lambda e, S, j=j: e.iota(
            out=Asm("msk_ii")[:, :], pattern=[[1, 512]], base=-128 * j,
            channel_multiplier=-1), inc=True)
        p.wait("scalar", "gc", mi)
        mc = op("scalar", lambda e, S: e.copy(
            out=Asm("msk_s"), in_=Asm("msk_ii")), inc=True)
        p.wait("vector", "ss", mc)
        mv = op("vector", lambda e, S, j=j: e.tensor_scalar(
            out=Abf("mskA")[:, 512 * j:512 * (j + 1)], in0=Asm("msk_s"),
            scalar1=0.0, scalar2=-1e30, op0=ALU.is_lt, op1=ALU.mult),
            inc=True)
        if j < 3:
            p.wait("gpsimd", "vv", mv)
    masks_done = mv
    p.wait("scalar", "gc", iv)
    s1v = op("scalar", lambda e, S: e.activation(
        out=A("invfB"), in_=Asm("iv_i")[:, :], func=AF.Exp,
        scale=-math.log(10000.0) / 32.0), inc=True)
    p.wait("vector", "ss", s1v)
    op("vector", lambda e, S: e.memset(Asm("bm10"), -10.0))
    op("vector", lambda e, S: e.memset(Asm("m512"), 512.0))
    op("vector", lambda e, S: e.memset(Asm("onescol"), 1.0))
    consts_v = op("vector", lambda e, S: e.memset(Asm("onesrow"), 1.0),
                  inc=True)
    for tc in (range(TC) if do_rope else []):
        ldp = dma("Lr", lambda: Asm("rp_pos"),
                  lambda tc=tc: pos_in[0, 128 * tc:128 * (tc + 1)].rearrange(
                      "(p one) -> p one", one=1))
        p.wait("vector", "Lr", ldp)
        thv = op("vector", lambda e, S: e.tensor_scalar(
            out=A("rp_th"), in0=A("invfB"), scalar1=Asm("rp_pos"),
            scalar2=None, op0=ALU.mult), inc=True)
        p.wait("sync", "vv", thv)
        p.drain("vector")
        op("vector", lambda e, S: e.tensor_scalar(
            out=A("rp_u"), in0=A("rp_th"), scalar1=1.0 / TWO_PI,
            scalar2=None, op0=ALU.mult))
        p.drain("vector")
        op("vector", lambda e, S: e.tensor_scalar(
            out=A("rp_k"), in0=A("rp_u"), scalar1=MAGIC, scalar2=MAGIC,
            op0=ALU.add, op1=ALU.subtract))
        p.drain("vector")
        op("vector", lambda e, S: e.scalar_tensor_tensor(
            out=A("rp_r"), in0=A("rp_k"), scalar=-c1, in1=A("rp_th"),
            op0=ALU.mult, op1=ALU.add))
        p.drain("vector")
        op("vector", lambda e, S: e.scalar_tensor_tensor(
            out=A("rp_u"), in0=A("rp_k"), scalar=-c2, in1=A("rp_r"),
            op0=ALU.mult, op1=ALU.add))
        p.drain("vector")
        op("vector", lambda e, S: e.scalar_tensor_tensor(
            out=A("rp_r"), in0=A("rp_k"), scalar=-c3, in1=A("rp_u"),
            op0=ALU.mult, op1=ALU.add))
        p.drain("vector")
        op("vector", lambda e, S: e.tensor_scalar(
            out=A("rp_si"), in0=A("rp_r"), scalar1=math.pi, scalar2=None,
            op0=ALU.is_gt))
        p.drain("vector")
        op("vector", lambda e, S: e.tensor_scalar(
            out=A("rp_u"), in0=A("rp_r"), scalar1=-math.pi, scalar2=None,
            op0=ALU.is_lt))
        p.drain("vector")
        op("vector", lambda e, S: e.tensor_tensor(
            out=A("rp_u"), in0=A("rp_u"), in1=A("rp_si"), op=ALU.subtract))
        p.drain("vector")
        op("vector", lambda e, S: e.scalar_tensor_tensor(
            out=A("rp_si"), in0=A("rp_u"), scalar=TWO_PI, in1=A("rp_r"),
            op0=ALU.mult, op1=ALU.add))
        p.drain("vector")
        op("vector", lambda e, S: e.tensor_scalar(
            out=A("rp_ci"), in0=A("rp_r"), scalar1=math.pi / 2, scalar2=None,
            op0=ALU.add))
        p.drain("vector")
        op("vector", lambda e, S: e.tensor_scalar(
            out=A("rp_u"), in0=A("rp_ci"), scalar1=math.pi, scalar2=None,
            op0=ALU.is_gt))
        p.drain("vector")
        rw = op("vector", lambda e, S: e.scalar_tensor_tensor(
            out=A("rp_ci"), in0=A("rp_u"), scalar=-TWO_PI, in1=A("rp_ci"),
            op0=ALU.mult, op1=ALU.add), inc=True)
        p.wait("scalar", "vv", rw)
        op("scalar", lambda e, S: e.activation(
            out=A("rp_sin"), in_=A("rp_si"), func=AF.Sin))
        sc = op("scalar", lambda e, S, tc=tc: e.activation(
            out=A("cosT")[:, 64 * tc:64 * (tc + 1)], in_=A("rp_ci"),
            func=AF.Sin), inc=True)
        p.wait("vector", "ss", sc)
        op("vector", lambda e, S, tc=tc: e.tensor_scalar(
            out=A("sinS")[:, 64 * tc:64 * tc + 32], in0=A("rp_sin")[:, 0:32],
            scalar1=-1.0, scalar2=None, op0=ALU.mult))
        op("vector", lambda e, S, tc=tc: e.tensor_copy(
            out=A("sinS")[:, 64 * tc + 32:64 * (tc + 1)],
            in_=A("rp_sin")[:, 32:64]), inc=True)
    rope_done = p.val("vv")

    WBUFS = ["wrhs0", "wrhs1", "wrhs2", "wrhs3"]
    WLANE = {"wrhs0": "W0", "wrhs1": "W1", "wrhs2": "W2", "wrhs3": "W3",
             "wrhsB0": "W0", "wrhsB1": "W1", "wrhsB2": "W2", "wrhsB3": "W3",
             "wrhsD0": "W0", "wrhsD1": "W1"}
    wb_state = {"i": 0, "guard": {}}

    def stream_w(tile_fn, bufs=None):
        bufs = bufs if bufs is not None else WBUFS
        buf = bufs[wb_state["i"] % len(bufs)]
        wb_state["i"] += 1
        lane = WLANE[buf]
        g = wb_state["guard"].get(buf)
        if g is not None:
            p.wait("sync", g[0], g[1])
        t = dma(lane, lambda buf=buf: Abf(buf)[:, :512], tile_fn)
        return buf, t, lane

    def mm_group8(lhs_fn, tile_fn, nK, evac_fn, psum_guards=None,
                  bufs=None, pre_kc=None):
        for kc in range(nK):
            buf, t, lane = stream_w(lambda kc=kc: tile_fn(kc), bufs=bufs)
            if pre_kc is not None:
                pre_kc(kc)
            p.wait("tensor", lane, t)
            for tc in range(TC):
                if kc == 0 and psum_guards and psum_guards[tc] is not None:
                    p.wait("tensor", psum_guards[tc][0], psum_guards[tc][1])
                last = (kc == nK - 1)
                if last:
                    p.inc("mm")
                    p.on("tensor", lambda e, S, kc=kc, tc=tc, buf=buf:
                         e.matmul(ps[f"ps{tc}"][:, :], lhs_fn(kc, tc),
                                  Abf(buf)[:, :512], start=(kc == 0),
                                  stop=True).then_inc(S["mm"], 1))
                else:
                    p.inc("mq")
                    p.on("tensor", lambda e, S, kc=kc, tc=tc, buf=buf:
                         e.matmul(ps[f"ps{tc}"][:, :], lhs_fn(kc, tc),
                                  Abf(buf)[:, :512], start=(kc == 0),
                                  stop=False).then_inc(S["mq"], 1))
            wb_state["guard"][buf] = \
                ("mm", p.val("mm")) if kc == nK - 1 else ("mq", p.val("mq"))
        base = p.val("mm") - TC
        frees = []
        for tc in range(TC):
            frees.append(evac_fn(tc, base + tc + 1))
        return frees

    def lfm_lhs(kc, tc):
        return Abf("lfm").rearrange("p (c f) -> p c f", f=1024)[
            :, kc, 128 * tc:128 * (tc + 1)]

    EVB = ["ev0", "ev1", "ev2", "ev3"]
    EVLANE = {"ev0": "E0", "ev1": "E1", "ev2": "E2", "ev3": "E3"}
    ev_state = {"i": 0, "guard": {}}

    def qkv_evac(kind, pc):
        def f(tc, mm_target):
            buf = EVB[ev_state["i"] % 4]
            ev_state["i"] += 1
            weng = "scalar" if kind == "v" else "vector"
            g = ev_state["guard"].get(buf)
            if g is not None:
                p.wait(weng, EVLANE[buf], g)
            if kind == "v":
                p.wait("scalar", "mm", mm_target)
                evv = op("scalar", lambda e, S, buf=buf, tc=tc: e.activation(
                    out=Abf(buf)[:, :512], in_=ps[f"ps{tc}"][:, :],
                    func=AF.Identity), inc=True)
                p.wait("sync", "ss", evv)
                free = ("ss", evv)
            else:
                p.wait("vector", "mm", mm_target)

                def psv(tc2):
                    return ps[f"ps{tc2}"][:, :].rearrange(
                        "p (h d) -> p h d", d=64)

                def cosb(tc2):
                    return A("cosT")[:, 64 * tc2:64 * (tc2 + 1)].unsqueeze(1)\
                        .broadcast_to([128, 8, 64])

                def sinb(tc2, half):
                    s0 = 64 * tc2 + 32 * half
                    return A("sinS")[:, s0:s0 + 32].unsqueeze(1)\
                        .broadcast_to([128, 8, 32])
                op("vector", lambda e, S, tc=tc: e.tensor_tensor(
                    out=A("rope_t1").rearrange("p (h d) -> p h d", d=64),
                    in0=psv(tc), in1=cosb(tc), op=ALU.mult))
                op("vector", lambda e, S, tc=tc: e.tensor_tensor(
                    out=A("rope_t2").rearrange(
                        "p (h d) -> p h d", d=64)[:, :, 0:32],
                    in0=psv(tc)[:, :, 32:64], in1=sinb(tc, 0), op=ALU.mult))
                pf = op("vector", lambda e, S, tc=tc: e.tensor_tensor(
                    out=A("rope_t2").rearrange(
                        "p (h d) -> p h d", d=64)[:, :, 32:64],
                    in0=psv(tc)[:, :, 0:32], in1=sinb(tc, 1), op=ALU.mult),
                    inc=True)
                p.drain("vector")
                evv = op("vector", lambda e, S, buf=buf: e.tensor_tensor(
                    out=Abf(buf)[:, :512], in0=A("rope_t1"),
                    in1=A("rope_t2"), op=ALU.add), inc=True)
                p.drain("vector")
                p.wait("sync", "vv", evv)
                free = ("vv", pf)
            if kind == "q":
                dst = lambda tc=tc, pc=pc: s_qhat[
                    128 * tc:128 * (tc + 1), 512 * pc:512 * (pc + 1)]
            elif kind == "k":
                dst = lambda tc=tc, pc=pc: kv_in[0, 0:OFF_KV_V].rearrange(
                    "(t d) -> t d", d=D)[128 * tc:128 * (tc + 1),
                                         512 * pc:512 * (pc + 1)]
            else:
                dst = lambda tc=tc, pc=pc: kv_in[0, OFF_KV_V:].rearrange(
                    "(t d) -> t d", d=D)[128 * tc:128 * (tc + 1),
                                         512 * pc:512 * (pc + 1)]
            g = dma(EVLANE[buf], dst, lambda buf=buf: Abf(buf)[:, :512])
            ev_state["guard"][buf] = g
            return free
        return f

    p.wait("tensor", "ag", ag1_done)
    p.wait("tensor", "Tl", lfm_done)
    p.wait("sync", "ag", ag1_done)
    p.wait("vector", "vv", rope_done)
    pg = None
    for kind, off_m in ([] if "qkv" in SKIP else
                        (("q", OFF_Q), ("k", OFF_K), ("v", OFF_V))):
        for pc in range(4):
            pg = mm_group8(lfm_lhs,
                           lambda kc, off_m=off_m, pc=pc:
                           ag1_tile(off_m, D, 128 * kc, 512 * pc, 512),
                           DC, qkv_evac(kind, pc), psum_guards=pg)
    for _ln in EVL:
        p.wait("gpsimd", _ln, p.val(_ln))
    ag2_done = p.inc("ag")
    p.on("gpsimd", lambda e, S: e.collective_compute(
        "AllGather", ALU.bypass, replica_groups=[list(range(NP))],
        ins=[kv_in[:, :]], outs=[kv_out[:, :]]).then_inc(S["ag"], 1))

    if DEBUG_PHASE >= 2:
        p.wait("gpsimd", "ag", ag2_done)
        dma("Gd", lambda: dbg["qhat"][:, :], lambda: s_qhat[:, :])
        dma("Gd", lambda: dbg["kv"][:, :], lambda: kv_in[:, :])

    p.wait("scalar", "mm", p.val("mm"))
    for hp in range(NHP):
        dmaT("Tl",
             lambda hp=hp: Abf("qfm").rearrange(
                 "p (c f) -> p c f", f=1024)[:, hp, :],
             lambda hp=hp: s_qhat[:, 128 * hp:128 * (hp + 1)])
    qfm_done = p.val("Tl")

    def emit_scores_arm(s):
        b, c = divmod(s, 4)
        B0 = 1024 * c
        st = {"vp_i": 0, "vp_guard": {}, "A_guard": {}, "S_guard": {}}

        def khp_load(hp, lo, hi):
            if st.get("last_smm") is not None:
                p.wait("scalar", "tt", st["last_smm"])
            off = 0
            v = None
            t0 = lo
            while t0 < hi:
                r = 4 * b + t0 // 1024
                r0 = t0 % 1024
                r1 = min(1024, r0 + (hi - t0))
                n = r1 - r0
                v = dmaT("Tk",
                         lambda off=off, n=n: Abf("khp")[:, off:off + n],
                         lambda r=r, r0=r0, r1=r1, hp=hp: kvk_ap(r, r0, r1, hp))
                off += n
                t0 += n
            return v

        def do_pair(hp, qp, kcs, first, last_flags):
            Q0 = B0 + 512 * qp
            for i, (ktok, kind) in enumerate(kcs):
                par = (ktok // 128) % 2
                sA, sB = f"ps{2 * par}", f"ps{2 * par + 1}"
                vb = ["vp0", "vp1"][st["vp_i"] % 2]
                st["vp_i"] += 1
                gv = st["vp_guard"].get(vb)
                if gv is not None:
                    p.wait("sync", gv[0], gv[1])
                r = 4 * b + ktok // 1024
                r0 = ktok % 1024
                vld = dma("V0" if vb == "vp0" else "V1",
                          lambda vb=vb: Abf(vb)[:, :],
                          lambda r=r, r0=r0, hp=hp: kvv_ap(r, r0, r0 + 128, hp))
                gS = st["S_guard"].get(par)
                if gS is not None:
                    p.wait("tensor", gS[0], gS[1])
                koff = None
                smm = None
                kofs = st["koff"][ktok]
                op("tensor", lambda e, S, sA=sA, kofs=kofs, hp=hp, qp=qp:
                   e.matmul(ps[sA][:, :],
                            Abf("khp")[0:64, kofs:kofs + 128],
                            Abf("qfm").rearrange(
                                "p (c f) -> p c f", f=1024)[
                                0:64, hp, 512 * qp:512 * (qp + 1)],
                            start=True, stop=True, tile_position=(0, 0)),
                   inc=True, sem="tt")
                smm = op("tensor", lambda e, S, sB=sB, kofs=kofs, hp=hp, qp=qp:
                         e.matmul(ps[sB][:, :],
                                  Abf("khp")[64:128, kofs:kofs + 128],
                                  Abf("qfm").rearrange(
                                      "p (c f) -> p c f", f=1024)[
                                      64:128, hp, 512 * qp:512 * (qp + 1)],
                                  start=True, stop=True,
                                  tile_position=(64, 0)), inc=True, sem="tt")
                aA, aB = f"A{2 * par}", f"A{2 * par + 1}"
                for head, (sX, aX) in enumerate(((sA, aA), (sB, aB))):
                    gA = st["A_guard"].get(aX)
                    if gA is not None:
                        p.wait("scalar", gA[0], gA[1])
                    p.wait("scalar", "tt", smm)
                    if kind == "diag":
                        j = (ktok - Q0) // 128
                        assert 0 <= j < 4
                        p.wait("vector", "tt", smm)
                        if st.get("mskf_free") is not None:
                            p.wait("vector", "ss", st["mskf_free"])
                        mk = op("vector", lambda e, S, sX=sX, j=j:
                                e.tensor_tensor(
                                    out=A("msk_f"), in0=ps[sX][:, :],
                                    in1=Abf("mskA")[:, 512 * j:512 * (j + 1)],
                                    op=ALU.add), inc=True)
                        p.wait("scalar", "vv", mk)
                        ex = op("scalar", lambda e, S, aX=aX: e.activation(
                            out=Abf(aX)[:, :], in_=A("msk_f"), func=AF.Exp,
                            scale=0.125, bias=Asm("bm10")), inc=True)
                        st["mskf_free"] = ex
                    else:
                        ex = op("scalar", lambda e, S, sX=sX, aX=aX:
                                e.activation(
                                    out=Abf(aX)[:, :], in_=ps[sX][:, :],
                                    func=AF.Exp, scale=0.125,
                                    bias=Asm("bm10")), inc=True)
                    st[f"exp_{head}"] = ex
                st["S_guard"][par] = ("ss", st["exp_1"])
                st["last_smm"] = smm
                exA, exB = st["exp_0"], st["exp_1"]
                p.wait("tensor", "ss", exB)
                p.wait("tensor", "V0" if vb == "vp0" else "V1", vld)
                stt = (first and i == 0)
                stp = (last_flags and i == len(kcs) - 1)
                op("tensor", lambda e, S, vb=vb, aA=aA, stt=stt, stp=stp:
                   e.matmul(ps["ps4"][0:64, :], Abf(vb)[:, 0:64],
                            Abf(aA)[:, :], start=stt, stop=stp,
                            tile_position=(0, 0)), inc=True, sem="tt")
                op("tensor", lambda e, S, vb=vb, aB=aB, stt=stt, stp=stp:
                   e.matmul(ps["ps5"][64:128, :], Abf(vb)[:, 64:128],
                            Abf(aB)[:, :], start=stt, stop=stp,
                            tile_position=(0, 64)), inc=True, sem="tt")
                op("tensor", lambda e, S, aA=aA, stt=stt, stp=stp:
                   e.matmul(ps["ps6"][0:1, :], Asm("onescol"),
                            Abf(aA)[:, :], start=stt, stop=stp,
                            tile_position=(0, 0)), inc=True, sem="tt")
                avv = op("tensor", lambda e, S, aB=aB, stt=stt, stp=stp:
                         e.matmul(ps["ps7"][64:65, :], Asm("onescol"),
                                  Abf(aB)[:, :], start=stt, stop=stp,
                                  tile_position=(0, 64)), inc=True, sem="tt")
                st["vp_guard"][vb] = ("tt", avv)
                st["A_guard"][aA] = ("tt", avv)
                st["A_guard"][aB] = ("tt", avv)
            return avv

        def finish_pair(hp, qp, av_done):
            p.wait("scalar", "tt", av_done)
            se1 = op("scalar", lambda e, S: e.copy(
                out=Asm("serow")[0:1, 0:512], in_=ps["ps6"][0:1, :]),
                inc=True)
            se2 = op("scalar", lambda e, S: e.copy(
                out=Asm("serow")[64:65, 0:512], in_=ps["ps7"][64:65, :]),
                inc=True)
            p.wait("vector", "ss", se2)
            op("vector", lambda e, S: e.reciprocal(
                out=Asm("serow")[0:1, 512:1024], in_=Asm("serow")[0:1, 0:512]))
            op("vector", lambda e, S: e.reciprocal(
                out=Asm("serow")[64:65, 512:1024],
                in_=Asm("serow")[64:65, 0:512]))
            p.drain("vector")
            op("vector", lambda e, S: e.tensor_copy(
                out=Abf("selo")[0:1, 0:512], in_=Asm("serow")[0:1, 512:1024]))
            op("vector", lambda e, S: e.tensor_copy(
                out=Abf("selo")[64:65, 0:512],
                in_=Asm("serow")[64:65, 512:1024]))
            p.drain("vector")
            op("vector", lambda e, S: e.tensor_tensor(
                out=Asm("serow")[0:1, 0:512],
                in0=Asm("serow")[0:1, 512:1024], in1=Abf("selo")[0:1, 0:512],
                op=ALU.subtract))
            op("vector", lambda e, S: e.tensor_tensor(
                out=Asm("serow")[64:65, 0:512],
                in0=Asm("serow")[64:65, 512:1024],
                in1=Abf("selo")[64:65, 0:512], op=ALU.subtract))
            p.drain("vector")
            op("vector", lambda e, S: e.tensor_copy(
                out=Abf("selo")[0:1, 512:1024], in_=Asm("serow")[0:1, 0:512]))
            lov = op("vector", lambda e, S: e.tensor_copy(
                out=Abf("selo")[64:65, 512:1024],
                in_=Asm("serow")[64:65, 0:512]), inc=True)
            p.wait("tensor", "vv", lov)
            op("tensor", lambda e, S: e.matmul(
                ps["ps6"][0:64, :], Asm("onesrow")[0:1, :],
                Abf("selo")[0:1, 0:512], start=True, stop=False,
                tile_position=(0, 0)), inc=True, sem="tt")
            op("tensor", lambda e, S: e.matmul(
                ps["ps6"][0:64, :], Asm("onesrow")[0:1, :],
                Abf("selo")[0:1, 512:1024], start=False, stop=True,
                tile_position=(0, 0)), inc=True, sem="tt")
            op("tensor", lambda e, S: e.matmul(
                ps["ps7"][64:128, :], Asm("onesrow")[64:65, :],
                Abf("selo")[64:65, 0:512], start=True, stop=False,
                tile_position=(64, 64)), inc=True, sem="tt")
            bcv2 = op("tensor", lambda e, S: e.matmul(
                ps["ps7"][64:128, :], Asm("onesrow")[64:65, :],
                Abf("selo")[64:65, 512:1024], start=False, stop=True,
                tile_position=(64, 64)), inc=True, sem="tt")
            p.wait("scalar", "tt", bcv2)
            r1v = op("scalar", lambda e, S: e.copy(
                out=A("smrcp")[0:64, :], in_=ps["ps6"][0:64, :]), inc=True)
            r2v = op("scalar", lambda e, S: e.copy(
                out=A("smrcp")[64:128, :], in_=ps["ps7"][64:128, :]),
                inc=True)
            p.wait("vector", "ss", r2v)
            op("vector", lambda e, S, hp=hp, qp=qp: e.tensor_tensor(
                out=Abf("afm").rearrange("p (c f) -> p c f", f=1024)[
                    0:64, hp, 512 * qp:512 * (qp + 1)],
                in0=ps["ps4"][0:64, :], in1=A("smrcp")[0:64, :],
                op=ALU.mult))
            fin = op("vector", lambda e, S, hp=hp, qp=qp: e.tensor_tensor(
                out=Abf("afm").rearrange("p (c f) -> p c f", f=1024)[
                    64:128, hp, 512 * qp:512 * (qp + 1)],
                in0=ps["ps5"][64:128, :], in1=A("smrcp")[64:128, :],
                op=ALU.mult), inc=True)
            p.wait("tensor", "vv", fin)
            return fin

        for hp in range(15):
            S0 = max(0, B0 - 512)
            span_end = B0 + 1024
            st["koff"] = {tok: tok - S0 for tok in range(S0, span_end, 128)}
            kl = khp_load(hp, S0, span_end)
            p.wait("tensor", "Tk", kl)
            for qp in range(2):
                Q0 = B0 + 512 * qp
                kcs = []
                for ktok in range(S0, span_end, 128):
                    if ktok >= Q0 + 512:
                        continue
                    kcs.append((ktok, "diag" if ktok + 127 > Q0 else "full"))
                av = do_pair(hp, qp, kcs, True, True)
                finish_pair(hp, qp, av)
        hp = 15
        for qp in range(2):
            Q0 = B0 + 512 * qp
            hi_all = Q0 + 512
            scs = [(lo, min(lo + 2048, hi_all))
                   for lo in range(0, hi_all, 2048)]
            first = True
            for si, (lo, hi) in enumerate(scs):
                st["koff"] = {tok: tok - lo for tok in range(lo, hi, 128)}
                kl = khp_load(hp, lo, hi)
                p.wait("tensor", "Tk", kl)
                kcs = []
                for ktok in range(lo, hi, 128):
                    kcs.append((ktok,
                                "diag" if ktok + 127 > Q0 else "full"))
                av = do_pair(hp, qp, kcs, first, si == len(scs) - 1)
                first = False
            finish_pair(hp, qp, av)

    if UPTO >= 3:
        p.wait("tensor", "ag", p.val("ag"))
        p.wait("tensor", "Tl", qfm_done)
        p.wait("tensor", "vv", consts_v)
        p.wait("scalar", "Tl", qfm_done)
        p.wait("scalar", "ag", p.val("ag"))
        p.wait("vector", "ag", p.val("ag"))
        for _ln in EVL:
            p.wait("vector", _ln, p.val(_ln))
            p.wait("scalar", _ln, p.val(_ln))
        p.wait("sync", "ag", p.val("ag"))

        sw_engines = ("tensor", "vector", "scalar", "sync")
        snap = dict(p.cnt)
        snap_mw = dict(p.max_waiter)
        snap_gt = dict(p.gated_to)
        collected = {a: {eng: [] for eng in sw_engines} for a in range(8)}
        finals = []
        fin_mw, fin_gt = [], []
        for arm in range(8):
            p.cnt = dict(snap)
            p.max_waiter = dict(snap_mw)
            p.gated_to = dict(snap_gt)
            saved = {eng: p.ops[eng] for eng in sw_engines}
            for eng in sw_engines:
                p.ops[eng] = collected[arm][eng]
            emit_scores_arm(arm)
            for eng in sw_engines:
                p.ops[eng] = saved[eng]
            finals.append(dict(p.cnt))
            fin_mw.append(dict(p.max_waiter))
            fin_gt.append(dict(p.gated_to))
        canon = {}
        for f in finals + [snap]:
            for k, v in f.items():
                canon[k] = max(canon.get(k, 0), v)
        for arm in range(8):
            fin = finals[arm]
            for eng in sw_engines:
                tail = []
                for semn in ESEM[eng]:
                    fv = fin.get(semn, 0)
                    cv = canon.get(semn, 0)
                    tail.append((semn, fv, cv - fv))
                ops_list = collected[arm][eng]
                for semn, fv, pad in tail:
                    if fv > 0:
                        ops_list.append(
                            lambda e, S, n=semn, v=fv: e.wait_ge(S[n], v))
                    if pad > 0:
                        ops_list.append(
                            lambda e, S, n=semn, d=pad: e.sem_inc(S[n], d))
        p.cnt = canon
        p.max_waiter = {k: max(canon.get(k, 0),
                               max((m.get(k, 0) for m in fin_mw), default=0))
                        for k in set(canon) | set(snap_mw)}
        p.gated_to = {}
        for eng in sw_engines:
            def mk_switch(eng):
                arms_ops = [collected[a][eng] for a in range(8)]

                def fn(e, S):
                    pid = e.partition_id()
                    for arm in e.Switch(pid, 8):
                        for f in arms_ops[arm]:
                            f(e, S)
                return fn
            p.on(eng, mk_switch(eng))

        if DEBUG_PHASE >= 3:
            pass

    if UPTO >= 4:
        def afm_lhs(kc, tc):
            return Abf("afm").rearrange("p (c f) -> p c f", f=1024)[
                :, kc, 128 * tc:128 * (tc + 1)]

        ZEV = [("Z0", 0), ("Z1", 512), ("Z2", 1024), ("Z3", 1536)]
        zev_state = {"i": 0, "guard": {}}

        def wo_evac(pc):
            def f(tc, mm_target):
                zlane, off = ZEV[zev_state["i"] % 4]
                zev_state["i"] += 1
                g = zev_state["guard"].get(off)
                if g is not None:
                    p.wait("scalar", zlane, g)
                p.wait("scalar", "mm", mm_target)
                evv = op("scalar", lambda e, S, off=off, tc=tc: e.activation(
                    out=A("z_t")[:, off:off + 512], in_=ps[f"ps{tc}"][:, :],
                    func=AF.Identity), inc=True)
                p.wait("scalar", "ss", evv)
                st2 = dma(zlane,
                          lambda tc=tc, pc=pc: s_z[128 * tc:128 * (tc + 1),
                                                   512 * pc:512 * (pc + 1)],
                          lambda off=off: A("z_t")[:, off:off + 512])
                zev_state["guard"][off] = st2
                return ("ss", evv)
            return f

        pg2 = pg
        pg2 = [("vv", p.val("vv"))] * TC
        for pc in range(4):
            pg2 = mm_group8(afm_lhs,
                            lambda kc, pc=pc:
                            ag1_tile(OFF_O, D, 128 * kc, 512 * pc, 512),
                            DC, wo_evac(pc), psum_guards=pg2,
                            bufs=["wrhsB0", "wrhsB1", "wrhsB2", "wrhsB3"])
        wo_done = {zl: p.val(zl) for zl in ("Z0", "Z1", "Z2", "Z3")}

        if DEBUG_PHASE >= 3:
            for zl, zv in wo_done.items():
                p.wait("gpsimd", zl, zv)
            dma("Gd", lambda: dbg["z"][:, :], lambda: s_z[:, :])

        for nm, src in (("bo_b", bo_in), ("lnw_b", lnw_in), ("lnb_b", lnb_in)):
            t = dma("Gd", lambda nm=nm: A(nm),
                    lambda src=src: src[0:1, :].broadcast_to([128, D]))
        bco = p.val("Gd")
        p.wait("vector", "Gd", bco)
        p.wait("scalar", "Gd", bco)

        for zl, zv in wo_done.items():
            p.wait("sync", zl, zv)
        for tc in range(TC):
            tz = dma("Lt", lambda: A("t_t"),
                     lambda tc=tc: s_z[128 * tc:128 * (tc + 1), :])
            tx = dma("Lx", lambda: A("x_t"),
                     lambda tc=tc: x_in[128 * tc:128 * (tc + 1), :])
            p.wait("vector", "Lt", tz)
            p.wait("vector", "Lx", tx)
            zb = op("vector", lambda e, S: e.tensor_tensor(
                out=A("z_t"), in0=A("t_t"), in1=A("bo_b"), op=ALU.add),
                inc=True)
            p.wait("scalar", "vv", zb)
            ln_stats(lambda: A("z_t"))
            ap2 = op("scalar", lambda e, S: e.activation(
                out=A("t_t"), in_=A("z_t"), func=AF.Identity,
                scale=Asm("ln_r"), bias=Asm("ln_nmr")), inc=True)
            p.wait("vector", "ss", ap2)
            x1v = op("vector", lambda e, S: e.tensor_tensor(
                out=A("z_t"), in0=A("t_t"), in1=A("x_t"), op=ALU.add),
                inc=True)
            p.wait("sync", "vv", x1v)
            sx1 = dma("Ls", lambda tc=tc: s_x1[128 * tc:128 * (tc + 1), :],
                      lambda: A("z_t"))
            p.wait("scalar", "vv", x1v)
            ln_stats(lambda: A("z_t"))
            h0v = op("scalar", lambda e, S: e.activation(
                out=Abf("h0st"), in_=A("z_t"), func=AF.Identity,
                scale=Asm("ln_r"), bias=Asm("ln_nmr")), inc=True)
            p.wait("sync", "ss", h0v)
            dma("Ls", lambda tc=tc: s_h0[128 * tc:128 * (tc + 1), :],
                lambda: Abf("h0st"))
            p.wait("scalar", "ss", h0v)
            ln_stats(lambda: Abf("h0st"))
            t2v = op("scalar", lambda e, S: e.activation(
                out=A("t_t"), in_=Abf("h0st"), func=AF.Identity,
                scale=Asm("ln_r"), bias=Asm("ln_nmr")), inc=True)
            p.wait("vector", "ss", t2v)
            p.wait("vector", "Ls")
            op("vector", lambda e, S: e.tensor_tensor(
                out=A("z_t"), in0=A("t_t"), in1=A("lnw_b"), op=ALU.mult))
            p.drain("vector")
            hv = op("vector", lambda e, S: e.tensor_tensor(
                out=Abf("hst"), in0=A("z_t"), in1=A("lnb_b"), op=ALU.add),
                inc=True)
            p.wait("vector", "vv", hv)
            for half in range(2):
                qf = quant_chain(
                    lambda half=half: Abf("hst")[:, 1024 * half:1024 * (half + 1)],
                    lambda half=half: Abf("hqst")[:, 1024 * half:1024 * (half + 1)],
                    1024, vbuf="q2_v", rbuf="q2_r")
            p.wait("sync", "vv", qf)
            dma("Ls", lambda tc=tc: s_hq[128 * tc:128 * (tc + 1), :],
                lambda: Abf("hqst"))
            p.wait("sync", "vv", qf)
        hq_stored = p.val("Ls")

        if DEBUG_PHASE >= 4:
            p.wait("gpsimd", "Ls", hq_stored)
            dma("Gd", lambda: dbg["x1"][:, :], lambda: s_x1[:, :])
            dma("Gd", lambda: dbg["hq"][:, :], lambda: s_hq[:, :])

    if UPTO >= 5:
        p.wait("scalar", "Ls", hq_stored)
        for dc in range(DC):
            dmaT("Tl",
                 lambda dc=dc: Abf("hqfm").rearrange(
                     "p (c f) -> p c f", f=1024)[:, dc, :],
                 lambda dc=dc: s_hq[:, 128 * dc:128 * (dc + 1)])
        hqfm_done = p.val("Tl")

        def hq_lhs(kc, tc):
            return Abf("hqfm").rearrange("p (c f) -> p c f", f=1024)[
                :, kc, 128 * tc:128 * (tc + 1)]

        p.wait("gpsimd", "vv", p.val("vv"))
        p.wait("gpsimd", "ss", p.val("ss"))
        dma("Gd", lambda: A("b1full"),
            lambda: b1_in[0:1, :].broadcast_to([128, DFF]))
        tb2 = dma("Gd", lambda: A("b2full"),
                  lambda: b2_in[0:1, :].broadcast_to([128, D]))

        w1st = {"i": 0, "g_y1g": [None, None], "g_y1h": [None, None],
                "g_gqs": [None, None], "grp_st": []}

        def w1_evac(jc):
            def f(tc, mm_target):
                sl = w1st["i"] & 1
                el = EVL[w1st["i"] & 3]
                w1st["i"] += 1
                p.wait("vector", "mm", mm_target)
                if w1st["g_y1g"][sl] is not None:
                    p.wait("vector", "ss", w1st["g_y1g"][sl])
                yv = op("vector", lambda e, S, sl=sl, tc=tc, jc=jc:
                        e.tensor_tensor(
                            out=Abf(f"y1g{sl}")[:, :], in0=ps[f"ps{tc}"][:, :],
                            in1=A("b1full")[:, 512 * jc:512 * (jc + 1)],
                            op=ALU.add), inc=True)
                p.wait("scalar", "vv", yv)
                if w1st["g_y1h"][sl] is not None:
                    p.wait("scalar", "vv", w1st["g_y1h"][sl])
                gv = op("scalar", lambda e, S, sl=sl: e.activation(
                    out=Abf(f"y1h{sl}")[:, :], in_=Abf(f"y1g{sl}")[:, :],
                    func=AF.Gelu_apprx_tanh), inc=True)
                w1st["g_y1g"][sl] = gv
                p.wait("vector", "ss", gv)
                if w1st["g_gqs"][sl] is not None:
                    gl, gvv = w1st["g_gqs"][sl]
                    p.wait("vector", gl, gvv)
                qv = quant_chain(lambda sl=sl: Abf(f"y1h{sl}")[:, :],
                                 lambda sl=sl: Abf(f"gqs{sl}")[:, :],
                                 512, vbuf="q2_v", rbuf="q2_r")
                w1st["g_y1h"][sl] = qv
                p.wait("sync", "vv", qv)
                stv = dma(el,
                          lambda tc=tc, jc=jc: s_gq[128 * tc:128 * (tc + 1),
                                                    512 * jc:512 * (jc + 1)],
                          lambda sl=sl: Abf(f"gqs{sl}")[:, :])
                w1st["g_gqs"][sl] = (el, stv)
                w1st["grp_st"].append((el, stv))
                return ("vv", yv)
            return f

        pg3 = [("ss", p.val("ss"))] * TC
        p.wait("tensor", "Tl", hqfm_done)
        p.wait("vector", "Gd", tb2)
        grp_done = []
        for jc in range(16):
            w1st["grp_st"] = []
            pg3 = mm_group8(hq_lhs,
                            lambda kc, jc=jc:
                            ag1_tile(OFF_W1, DFF, 128 * kc, 512 * jc, 512),
                            DC, w1_evac(jc), psum_guards=pg3)
            lanemax = {}
            for el, sv in w1st["grp_st"]:
                lanemax[el] = max(lanemax.get(el, 0), sv)
            grp_done.append(lanemax)
        w1_mm_done = p.val("mm")

        if DEBUG_PHASE >= 5:
            for el, sv in grp_done[-1].items():
                p.wait("gpsimd", el, sv)
            for lm in grp_done:
                for el, sv in lm.items():
                    p.wait("gpsimd", el, sv)
            dma("Gd", lambda: dbg["gq"][:, :], lambda: s_gq[:, :])

        w2st = {"guard": [None, None]}

        def w2_pre(kc):
            sl = kc & 1
            g = w2st["guard"][sl]
            if g is not None:
                p.wait("scalar", g[0], g[1])
            for el, sv in grp_done[kc // 4].items():
                p.wait("scalar", el, sv)
            t = dmaT("Tl",
                     lambda sl=sl: Abf(f"glhs{sl}")[:, :],
                     lambda kc=kc: s_gq[:, 128 * kc:128 * (kc + 1)])
            if kc == 63:
                w2st["guard"][sl] = ("mm", p.val("mm") + TC)
            else:
                w2st["guard"][sl] = ("mq", p.val("mq") + TC)
            p.wait("tensor", "Tl", t)

        def gq_lhs(kc, tc):
            return Abf(f"glhs{kc & 1}")[:, 128 * tc:128 * (tc + 1)]

        h0_state = {"g": [None, None], "fg": [None, None], "pend": {}}

        def w2_load_h0(oc, tc):
            sl = tc & 1
            g = h0_state["g"][sl]
            if g is not None:
                p.wait("sync", "vv", g)
            th = dma("Lh", lambda sl=sl: Abf(f"h0t{sl}")[:, :],
                     lambda tc=tc, oc=oc: s_h0[128 * tc:128 * (tc + 1),
                                               512 * oc:512 * (oc + 1)])
            h0_state["pend"][(oc, tc)] = th

        def w2_evac(oc):
            def f(tc, mm_target):
                sl = tc & 1
                fsl = w1st["i"] & 1
                w1st["i"] += 1
                if (oc, tc) not in h0_state["pend"]:
                    w2_load_h0(oc, tc)
                th = h0_state["pend"].pop((oc, tc))
                nxt = (oc, tc + 1) if tc < TC - 1 else                     ((oc + 1, 0) if oc < 3 else None)
                if nxt is not None:
                    w2_load_h0(*nxt)
                p.wait("vector", "mm", mm_target)
                p.wait("vector", "Lh", p.val("Lh"))
                if h0_state.get("pfv") is not None:
                    p.wait("vector", "vv", h0_state["pfv"])
                yv = op("vector", lambda e, S, tc=tc, oc=oc: e.tensor_tensor(
                    out=A("y2_f"), in0=ps[f"ps{tc}"][:, :],
                    in1=A("b2full")[:, 512 * oc:512 * (oc + 1)],
                    op=ALU.add), inc=True)
                p.drain("vector")
                if h0_state["fg"][fsl] is not None:
                    p.wait("vector", "Lf", p.val("Lf"))
                fv = op("vector", lambda e, S, sl=sl, fsl=fsl:
                        e.tensor_tensor(
                            out=Abf(f"fot{fsl}")[:, :], in0=A("y2_f"),
                            in1=Abf(f"h0t{sl}")[:, :], op=ALU.add), inc=True)
                h0_state["g"][sl] = fv
                h0_state["pfv"] = fv
                p.wait("sync", "vv", fv)
                sv = dma("Lf",
                         lambda tc=tc, oc=oc: s_fo[128 * tc:128 * (tc + 1),
                                                   512 * oc:512 * (oc + 1)],
                         lambda fsl=fsl: Abf(f"fot{fsl}")[:, :])
                h0_state["fg"][fsl] = sv
                return ("vv", yv)
            return f

        pg4 = [("vv", p.val("vv"))] * TC
        for oc in range(4):
            pg4 = mm_group8(gq_lhs,
                            lambda kc, oc=oc:
                            ag1_tile(OFF_W2, D, 128 * kc, 512 * oc, 512, 1024),
                            64, w2_evac(oc), psum_guards=pg4,
                            bufs=["wrhsD0", "wrhsD1"], pre_kc=w2_pre)
        fo_stored = p.val("Lf")

        p.wait("sync", "Lf", fo_stored)
        p.wait("scalar", "mm", p.val("mm"))
        fprev = None
        for tc in range(TC):
            if fprev is not None:
                p.wait("sync", "vv", fprev)
            tf = dma("Lg", lambda: Abf("fo_ld"),
                     lambda tc=tc: s_fo[128 * tc:128 * (tc + 1), :])
            tx1 = dma("Lt", lambda: A("t_t"),
                      lambda tc=tc: s_x1[128 * tc:128 * (tc + 1), :])
            p.wait("scalar", "Lg", tf)
            p.wait("scalar", "Lt", tx1)
            p.wait("vector", "Lt", tx1)
            ln_stats(lambda: Abf("fo_ld"))
            apf = op("scalar", lambda e, S: e.activation(
                out=A("z_t"), in_=Abf("fo_ld"), func=AF.Identity,
                scale=Asm("ln_r"), bias=Asm("ln_nmr")), inc=True)
            p.wait("vector", "ss", apf)
            if fprev is not None:
                p.wait("vector", "Ls", syp)
            yv = op("vector", lambda e, S: e.tensor_tensor(
                out=A("x_t"), in0=A("z_t"), in1=A("t_t"), op=ALU.add),
                inc=True)
            p.wait("sync", "vv", yv)
            syp = dma("Ls", lambda tc=tc: y_out[128 * tc:128 * (tc + 1), :],
                      lambda: A("x_t"))
            fprev = yv
    else:
        pv = None
        for tc in range(TC):
            if pv is not None:
                p.wait("sync", "Ls", pv)
            tl0 = dma("Lx", lambda: A("x_t"),
                      lambda tc=tc: x_in[128 * tc:128 * (tc + 1), :])
            p.wait("sync", "Lx", tl0)
            pv = dma("Ls", lambda tc=tc: y_out[128 * tc:128 * (tc + 1), :],
                     lambda: A("x_t"))

    for _ln in SYNC_LANES + SCAL_LANES + GP_LANES:
        if p.val(_ln) > 0:
            p.wait("gpsimd", _ln, p.val(_ln))

    import contextlib
    with contextlib.ExitStack() as stack:
        sb["AF32"] = stack.enter_context(
            nc.sbuf_tensor("AF32", [128, NF32], F32))
        sb["ABF"] = stack.enter_context(
            nc.sbuf_tensor("ABF", [128, NBF], BF16))
        for name, (shape, dt) in sm.items():
            sb[name] = stack.enter_context(nc.sbuf_tensor(name, shape, dt))
        for i in range(8):
            ps[f"ps{i}"] = stack.enter_context(
                nc.psum_tensor(f"ps{i}", [128, 512], F32))
        S = {n: stack.enter_context(nc.semaphore(n)) for n in SEM_NAMES}
        p.emit(nc, S)
    return nc


def kernel(L0_attn_hidden_states, L0_attn_Wq, L0_attn_Wk, L0_attn_Wv,
           L0_attn_Wo, L0_attn_bo, L0_ffn_ln_w, L0_ffn_ln_b, L0_ffn_W1,
           L0_ffn_b1, L0_ffn_W2, L0_ffn_b2):
    x = np.ascontiguousarray(np.asarray(L0_attn_hidden_states, np.float32))
    Wq = np.asarray(L0_attn_Wq, np.float32)
    Wk = np.asarray(L0_attn_Wk, np.float32)
    Wv = np.asarray(L0_attn_Wv, np.float32)
    Wo = np.asarray(L0_attn_Wo, np.float32)
    W1 = np.asarray(L0_ffn_W1, np.float32)
    W2 = np.asarray(L0_ffn_W2, np.float32)
    in_maps = []
    for s in range(NP):
        b, c = divmod(s, 4)
        in_maps.append({
            "x": np.ascontiguousarray(x[b, 1024 * c:1024 * (c + 1), :]),
            "pos": np.arange(1024 * c, 1024 * (c + 1),
                             dtype=np.float32)[None, :],
            "wq": np.ascontiguousarray(Wq[:, 256 * s:256 * (s + 1)]),
            "wk": np.ascontiguousarray(Wk[:, 256 * s:256 * (s + 1)]),
            "wv": np.ascontiguousarray(Wv[:, 256 * s:256 * (s + 1)]),
            "wo": np.ascontiguousarray(Wo[:, 256 * s:256 * (s + 1)]),
            "w1": np.ascontiguousarray(W1[:, 256 * s:256 * (s + 1)]),
            "w2": np.ascontiguousarray(W2[:, 1024 * s:1024 * (s + 1)]),
            "bo": np.asarray(L0_attn_bo, np.float32)[None, :],
            "lnw": np.asarray(L0_ffn_ln_w, np.float32)[None, :],
            "lnb": np.asarray(L0_ffn_ln_b, np.float32)[None, :],
            "b1": np.asarray(L0_ffn_b1, np.float32)[None, :],
            "b2": np.asarray(L0_ffn_b2, np.float32)[None, :],
        })
    nc = build_nc()
    res = run_bass_kernel_spmd(nc, in_maps, core_ids=list(range(NP)))
    out = np.empty((B, L, D), np.float32)
    for s in range(NP):
        b, c = divmod(s, 4)
        out[b, 1024 * c:1024 * (c + 1), :] = res.results[s]["y"]
    return out


# version 03
# speedup vs baseline: 1.0093x; 1.0093x over previous
import math
import numpy as np
import ml_dtypes
import concourse.bass as bass
import concourse.mybir as mybir
from concourse.bass_utils import run_bass_kernel_spmd

F32 = mybir.dt.float32
BF16 = mybir.dt.bfloat16
I32 = mybir.dt.int32
AF = mybir.ActivationFunctionType
ALU = mybir.AluOpType
AX = mybir.AxisListType

NP = 8
B, L, D = 2, 4096, 2048
T = 1024
DFF = 8192
TC = T // 128
DC = D // 128
NHP = 16
MAGIC = float(np.float32(1.5 * 2 ** 23))
TWO_PI = 2.0 * math.pi

OFF_Q = 0
OFF_K = OFF_Q + 256 * D
OFF_V = OFF_K + 256 * D
OFF_O = OFF_V + 256 * D
OFF_W1 = OFF_O + 256 * D
OFF_W2 = OFF_W1 + 256 * DFF
R1 = OFF_W2 + 1024 * D

OFF_KV_V = T * D
R2 = 2 * T * D

UPTO = 99
DEBUG_PHASE = 0
SKIP = set()

CSEM = {"vector": "vv", "scalar": "ss", "gpsimd": "gc", "tensor": "tt"}
SYNC_LANES = ["Lx", "Lt", "Lp", "Ls", "Lr", "W0", "W1", "W2", "W3",
              "E0", "E1", "E2", "E3", "V0", "V1", "Lb", "Lh", "Lf", "Lg"]
SCAL_LANES = ["Tl", "Tw", "Tk", "Z0", "Z1", "Z2", "Z3"]
GP_LANES = ["Gd"]
LANE_ENG = {**{n: "sync" for n in SYNC_LANES},
            **{n: "scalar" for n in SCAL_LANES},
            **{n: "gpsimd" for n in GP_LANES}}
EVL = ["E0", "E1", "E2", "E3"]
WL = ["W0", "W1", "W2", "W3"]
ESEM = {"sync": SYNC_LANES, "scalar": SCAL_LANES + ["ss"], "vector": ["vv"],
        "tensor": ["tt", "mm", "mq"], "gpsimd": GP_LANES + ["gc"]}
SEM_NAMES = (SYNC_LANES + SCAL_LANES + GP_LANES +
             ["ag", "vv", "ss", "tt", "mm", "mq", "gc"])


def _cody_waite_consts():
    def chop(x):
        v = np.frombuffer(np.float32(x).tobytes(), dtype=np.uint32)[0]
        v = np.uint32(v & np.uint32(0xFFFFF000))
        return float(np.frombuffer(v.tobytes(), dtype=np.float32)[0])
    c1 = chop(TWO_PI)
    c2 = chop(TWO_PI - c1)
    c3 = float(np.float32(TWO_PI - c1 - c2))
    return c1, c2, c3


class Prog:
    ENGS = ("sync", "scalar", "vector", "tensor", "gpsimd")

    def __init__(self):
        self.ops = {e: [] for e in self.ENGS}
        self.cnt = {}
        self.max_waiter = {}
        self.gated_to = {}

    def on(self, eng, fn):
        self.ops[eng].append(fn)

    def inc(self, name, by=1):
        self.cnt[name] = self.cnt.get(name, 0) + by
        return self.cnt[name]

    def val(self, name):
        return self.cnt.get(name, 0)

    def wait(self, eng, name, val=None):
        v = self.cnt.get(name, 0) if val is None else val
        if name in LANE_ENG:
            self.max_waiter[name] = max(self.max_waiter.get(name, 0), v)
        self.on(eng, lambda e, S, n=name, v=v: e.wait_ge(S[n], v))

    def drain(self, eng):
        self.on(eng, lambda e, S: e.drain())

    def emit(self, nc, sems):
        with nc.Block() as block:
            for eng_name in self.ENGS:
                deco = getattr(block, eng_name)
                ops = self.ops[eng_name]

                def make(ops):
                    def body(e):
                        for fn in ops:
                            fn(e, sems)
                    return body
                deco(make(ops))


def build_nc():
    nc = bass.Bass("TRN2", target_bir_lowering=False, debug=False)

    x_in = nc.dram_tensor("x", [T, D], F32, kind="ExternalInput").ap()
    pos_in = nc.dram_tensor("pos", [1, T], F32, kind="ExternalInput").ap()
    w_in = {m: nc.dram_tensor(f"w{m}", [D, 256], F32, kind="ExternalInput").ap()
            for m in "qkvo"}
    w1_in = nc.dram_tensor("w1", [DFF, 256], F32, kind="ExternalInput").ap()
    w2_in = nc.dram_tensor("w2", [D, 1024], F32, kind="ExternalInput").ap()
    bo_in = nc.dram_tensor("bo", [1, D], F32, kind="ExternalInput").ap()
    lnw_in = nc.dram_tensor("lnw", [1, D], F32, kind="ExternalInput").ap()
    lnb_in = nc.dram_tensor("lnb", [1, D], F32, kind="ExternalInput").ap()
    b1_in = nc.dram_tensor("b1", [1, DFF], F32, kind="ExternalInput").ap()
    b2_in = nc.dram_tensor("b2", [1, D], F32, kind="ExternalInput").ap()
    y_out = nc.dram_tensor("y", [T, D], F32, kind="ExternalOutput").ap()

    ag1_in = nc.dram_tensor("ag1_in", [1, R1], BF16, kind="Internal").ap()
    ag1_out = nc.dram_tensor("ag1_out", [1, NP * R1], BF16, kind="Internal",
                             addr_space="Shared").ap()
    kv_in = nc.dram_tensor("kv_in", [1, R2], BF16, kind="Internal").ap()
    kv_out = nc.dram_tensor("kv_out", [1, NP * R2], BF16, kind="Internal",
                            addr_space="Shared").ap()
    s_w = {m: nc.dram_tensor(f"s_w{m}", [D, 256], BF16, kind="Internal").ap()
           for m in "qkvo"}
    s_w1 = nc.dram_tensor("s_w1", [DFF, 256], BF16, kind="Internal").ap()
    s_w2 = nc.dram_tensor("s_w2", [D, 1024], BF16, kind="Internal").ap()
    s_ln1x = nc.dram_tensor("s_ln1x", [T, D], BF16, kind="Internal").ap()
    s_qhat = nc.dram_tensor("s_qhat", [T, D], BF16, kind="Internal").ap()
    s_z = nc.dram_tensor("s_z", [T, D], F32, kind="Internal").ap()
    s_x1 = nc.dram_tensor("s_x1", [T, D], F32, kind="Internal").ap()
    s_h0 = nc.dram_tensor("s_h0", [T, D], BF16, kind="Internal").ap()
    s_hq = nc.dram_tensor("s_hq", [T, D], BF16, kind="Internal").ap()
    s_g = nc.dram_tensor("s_g", [T, DFF], BF16, kind="Internal").ap()
    s_gq = nc.dram_tensor("s_gq", [T, DFF], BF16, kind="Internal").ap()
    s_fo = nc.dram_tensor("s_fo", [T, D], BF16, kind="Internal").ap()

    dbg = {}
    if DEBUG_PHASE >= 1:
        dbg["ag1"] = nc.dram_tensor("d_ag1", [1, R1], BF16,
                                    kind="ExternalOutput").ap()
        dbg["ln1x"] = nc.dram_tensor("d_ln1x", [T, D], BF16,
                                     kind="ExternalOutput").ap()
    if DEBUG_PHASE >= 2:
        dbg["qhat"] = nc.dram_tensor("d_qhat", [T, D], BF16,
                                     kind="ExternalOutput").ap()
        dbg["kv"] = nc.dram_tensor("d_kv", [1, R2], BF16,
                                   kind="ExternalOutput").ap()
    if DEBUG_PHASE >= 3:
        dbg["z"] = nc.dram_tensor("d_z", [T, D], F32,
                                  kind="ExternalOutput").ap()
    if DEBUG_PHASE >= 4:
        dbg["x1"] = nc.dram_tensor("d_x1", [T, D], F32,
                                   kind="ExternalOutput").ap()
        dbg["hq"] = nc.dram_tensor("d_hq", [T, D], BF16,
                                   kind="ExternalOutput").ap()
    if DEBUG_PHASE >= 5:
        dbg["gq"] = nc.dram_tensor("d_gq", [T, DFF], BF16,
                                   kind="ExternalOutput").ap()

    p = Prog()

    NF32 = 15360
    NBF = 68096

    fz = {}

    def F(name, start, n):
        assert start + n <= NF32, (name, start + n)
        fz[name] = (start, n)
    F("prep_f", 0, 2048)
    F("q_v", 2048, 2048)
    F("q_r", 4096, 2048)
    F("rope_t1", 8192, 512)
    F("rope_t2", 8704, 512)
    F("cosT", 9216, 512)
    F("sinS", 9728, 512)
    F("rp_th", 10240, 64)
    F("rp_u", 10304, 64)
    F("rp_k", 10368, 64)
    F("rp_r", 10432, 64)
    F("rp_si", 10496, 64)
    F("rp_ci", 10560, 64)
    F("rp_sin", 10624, 64)
    F("invfB", 10688, 64)
    F("msk_f", 10752, 512)
    F("x_t", 11264, 2048)
    F("q2_v", 13312, 1024)
    F("q2_r", 14336, 1024)
    F("smrcp", 0, 512)
    F("z_t", 512, 2048)
    F("t_t", 2560, 2048)
    F("bo_b", 4608, 2048)
    F("lnw_b", 6656, 2048)
    F("lnb_b", 8704, 2048)
    F("bias_b", 10752, 512)
    F("y2_f", 0, 512)
    F("b1full", 512, 8192)
    F("b2full", 8704, 2048)
    F("y1p0", 10752, 512)
    F("y1p1", 11264, 512)
    F("y1p2", 11776, 512)
    F("y1p3", 12288, 512)

    bz = {}

    def Bf(name, start, n):
        assert start + n <= NBF, (name, start + n)
        bz[name] = (start, n)
    Bf("lfm", 0, 16384)
    Bf("wtr", 16384, 8192)
    Bf("prep_q", 24576, 2048)
    Bf("lnx_bf", 24576, 2048)
    Bf("ev0", 26624, 512)
    Bf("ev1", 27136, 512)
    Bf("ev2", 27648, 512)
    Bf("ev3", 28160, 512)
    Bf("wrhs0", 28672, 512)
    Bf("wrhs1", 29184, 512)
    Bf("wrhs2", 29696, 512)
    Bf("wrhs3", 30208, 512)
    Bf("qfm", 0, 16384)
    Bf("afm", 16384, 16384)
    Bf("khp", 32768, 2048)
    Bf("A0", 34816, 512)
    Bf("A1", 35328, 512)
    Bf("A2", 35840, 512)
    Bf("A3", 36352, 512)
    Bf("vp0", 36864, 128)
    Bf("vp1", 36992, 128)
    Bf("selo", 37120, 1024)
    Bf("mskA", 38144, 2048)
    Bf("wrhsB0", 40960, 512)
    Bf("wrhsB1", 41472, 512)
    Bf("wrhsB2", 41984, 512)
    Bf("wrhsB3", 42496, 512)
    Bf("h0st", 16384, 2048)
    Bf("hst", 18432, 2048)
    Bf("hqst", 20480, 2048)
    Bf("hqfm", 0, 16384)
    Bf("y1g0", 30720, 512)
    Bf("y1h0", 31232, 512)
    Bf("gqs0", 31744, 512)
    Bf("y1g1", 32256, 512)
    Bf("y1h1", 32768, 512)
    Bf("gqs1", 33280, 512)
    Bf("glhs0", 18432, 1024)
    Bf("glhs1", 19456, 1024)
    Bf("wrhsD0", 65536, 512)
    Bf("wrhsD1", 66048, 512)
    Bf("h0t0", 33792, 512)
    Bf("h0t1", 34304, 512)
    Bf("fot0", 66560, 512)
    Bf("fot1", 67072, 512)
    Bf("fo_ld", 65536, 2048)

    sm = {}

    def SMT(name, shape, dt):
        sm[name] = (shape, dt)
    SMT("ln_s1", [128, 1], F32)
    SMT("ln_s1b", [128, 1], F32)
    SMT("ln_s2", [128, 1], F32)
    SMT("ln_s2b", [128, 1], F32)
    SMT("ln_mu", [128, 1], F32)
    SMT("ln_var", [128, 1], F32)
    SMT("ln_sd", [128, 1], F32)
    SMT("ln_r", [128, 1], F32)
    SMT("ln_nmr", [128, 1], F32)
    SMT("iv_i", [128, 64], I32)
    SMT("rp_pos", [128, 1], F32)
    SMT("bm10", [128, 1], F32)
    SMT("m512", [128, 1], F32)
    SMT("onescol", [128, 1], BF16)
    SMT("onesrow", [128, 64], BF16)
    SMT("serow", [128, 1024], F32)
    SMT("wq_amax", [128, 64], F32)
    SMT("wq_amax2", [128, 64], F32)
    SMT("wq_bits", [128, 64], I32)
    SMT("wq_m", [128, 64], I32)
    SMT("wq_mf", [128, 64], F32)
    SMT("wq_mf2", [128, 64], F32)
    SMT("wq_sb", [128, 64], I32)
    SMT("wq_scale", [128, 64], F32)
    SMT("wq_inv", [128, 64], F32)
    SMT("msk_ii", [128, 512], I32)
    SMT("msk_s", [128, 512], F32)

    sb = {}
    ps = {}

    def A(name):
        s0, n = fz[name]
        return sb["AF32"][:, s0:s0 + n]

    def Abf(name):
        s0, n = bz[name]
        return sb["ABF"][:, s0:s0 + n]

    def Asm(name):
        return sb[name][:, :]

    def _lane_gate(lane):
        eng = LANE_ENG[lane]
        mw = p.max_waiter.get(lane, 0)
        if mw > p.gated_to.get(lane, 0):
            p.on(eng, lambda e, S, n=lane, v=mw: e.wait_ge(S[n], v))
            p.gated_to[lane] = mw

    def dma(lane, dst_fn, src_fn):
        eng = LANE_ENG[lane]
        _lane_gate(lane)
        v = p.inc(lane, 16)
        p.on(eng, lambda e, S, d=dst_fn, s=src_fn, sem=lane:
             e.dma_start(out=d(), in_=s()).then_inc(S[sem], 16))
        return v

    def dmaT(lane, dst_fn, src_fn):
        eng = LANE_ENG[lane]
        _lane_gate(lane)
        v = p.inc(lane, 16)
        p.on(eng, lambda e, S, d=dst_fn, s=src_fn, sem=lane:
             e.dma_start_transpose(out=d(), in_=s()).then_inc(S[sem], 16))
        return v

    def op(eng, mk, inc=False, sem=None):
        if inc:
            sem = sem or CSEM[eng]
            v = p.inc(sem)
            p.on(eng, lambda e, S, mk=mk, sem=sem:
                 mk(e, S).then_inc(S[sem], 1))
            return v
        p.on(eng, lambda e, S, mk=mk: mk(e, S))
        return None

    def ag1_tile(off_m, rowlen, d0, col0, ncol, shard_rows=256):
        r = d0 // shard_rows
        i = d0 % shard_rows
        base = r * R1 + off_m + i * rowlen
        flat = ag1_out[0, :]
        return flat[base:base + 128 * rowlen].rearrange(
            "(p f) -> p f", f=rowlen)[:, col0:col0 + ncol]

    def kvk_ap(rank, r0, r1, hp):
        base = rank * R2
        flat = kv_out[0, :]
        return flat[base:base + T * D].rearrange(
            "(t d) -> t d", d=D)[r0:r1, 128 * hp:128 * (hp + 1)]

    def kvv_ap(rank, r0, r1, hp):
        base = rank * R2 + OFF_KV_V
        flat = kv_out[0, :]
        return flat[base:base + T * D].rearrange(
            "(t d) -> t d", d=D)[r0:r1, 128 * hp:128 * (hp + 1)]

    def quant_chain(src_fn, dst_fn, nfree, vbuf="q_v", rbuf="q_r"):
        nblk = nfree // 32
        assert nblk <= 64

        def v3(ap):
            return ap.rearrange("p (nb b) -> p nb b", b=32)
        op("vector", lambda e, S: e.tensor_reduce(
            out=Asm("wq_amax")[:, :nblk], in_=v3(src_fn()), axis=AX.X,
            op=ALU.max, apply_absolute_value=True))
        p.drain("vector")
        op("vector", lambda e, S: e.tensor_scalar(
            out=Asm("wq_amax2")[:, :nblk], in0=Asm("wq_amax")[:, :nblk],
            scalar1=1e-30, scalar2=None, op0=ALU.max))
        p.drain("vector")
        op("vector", lambda e, S: e.tensor_copy(
            out=Asm("wq_bits")[:, :nblk],
            in_=Asm("wq_amax2")[:, :nblk].bitcast(I32)))
        p.drain("vector")
        a = op("vector", lambda e, S: e.tensor_scalar(
            out=Asm("wq_m")[:, :nblk], in0=Asm("wq_bits")[:, :nblk],
            scalar1=23, scalar2=None, op0=ALU.logical_shift_right), inc=True)
        p.wait("scalar", "vv", a)
        b = op("scalar", lambda e, S: e.copy(
            out=Asm("wq_mf")[:, :nblk], in_=Asm("wq_m")[:, :nblk]), inc=True)
        p.wait("vector", "ss", b)
        c = op("vector", lambda e, S: e.tensor_scalar(
            out=Asm("wq_mf2")[:, :nblk], in0=Asm("wq_mf")[:, :nblk],
            scalar1=float(1 << 23), scalar2=-float(6 << 23),
            op0=ALU.mult, op1=ALU.add), inc=True)
        p.wait("scalar", "vv", c)
        d = op("scalar", lambda e, S: e.copy(
            out=Asm("wq_sb")[:, :nblk], in_=Asm("wq_mf2")[:, :nblk]), inc=True)
        p.wait("vector", "ss", d)
        op("vector", lambda e, S: e.tensor_copy(
            out=Asm("wq_scale")[:, :nblk],
            in_=Asm("wq_sb")[:, :nblk].bitcast(F32)))
        p.drain("vector")
        op("vector", lambda e, S: e.reciprocal(
            out=Asm("wq_inv")[:, :nblk], in_=Asm("wq_scale")[:, :nblk]))
        p.drain("vector")
        op("vector", lambda e, S: e.tensor_tensor(
            out=v3(A(vbuf)[:, :nfree]), in0=v3(src_fn()),
            in1=Asm("wq_inv")[:, :nblk].unsqueeze(2).broadcast_to(
                [128, nblk, 32]), op=ALU.mult))
        p.drain("vector")
        op("vector", lambda e, S: e.tensor_scalar(
            out=A(rbuf)[:, :nfree], in0=A(vbuf)[:, :nfree],
            scalar1=MAGIC, scalar2=MAGIC, op0=ALU.add, op1=ALU.subtract))
        p.drain("vector")
        op("vector", lambda e, S: e.tensor_scalar(
            out=A(vbuf)[:, :nfree], in0=A(rbuf)[:, :nfree],
            scalar1=127.0, scalar2=-128.0, op0=ALU.min, op1=ALU.max))
        p.drain("vector")
        fin = op("vector", lambda e, S: e.tensor_tensor(
            out=v3(dst_fn()), in0=v3(A(vbuf)[:, :nfree]),
            in1=Asm("wq_scale")[:, :nblk].unsqueeze(2).broadcast_to(
                [128, nblk, 32]), op=ALU.mult), inc=True)
        return fin

    def ln_stats(src_fn):
        op("scalar", lambda e, S: e.activation(
            out=A("q2_v"), in_=src_fn()[:, 0:1024], func=AF.Square,
            accum_out=Asm("ln_s2")))
        h1 = op("scalar", lambda e, S: e.activation(
            out=A("q2_r"), in_=src_fn()[:, 1024:2048], func=AF.Square,
            accum_out=Asm("ln_s2b")), inc=True)
        p.wait("scalar", "ss", h1)
        op("scalar", lambda e, S: e.activation(
            out=A("q2_v"), in_=src_fn()[:, 0:1024], func=AF.Identity,
            accum_out=Asm("ln_s1")))
        a = op("scalar", lambda e, S: e.activation(
            out=A("q2_r"), in_=src_fn()[:, 1024:2048], func=AF.Identity,
            accum_out=Asm("ln_s1b")), inc=True)
        p.wait("vector", "ss", a)
        op("vector", lambda e, S: e.tensor_tensor(
            out=Asm("ln_s1"), in0=Asm("ln_s1"), in1=Asm("ln_s1b"),
            op=ALU.add))
        op("vector", lambda e, S: e.tensor_tensor(
            out=Asm("ln_s2"), in0=Asm("ln_s2"), in1=Asm("ln_s2b"),
            op=ALU.add))
        p.drain("vector")
        op("vector", lambda e, S: e.tensor_scalar(
            out=Asm("ln_mu"), in0=Asm("ln_s1"), scalar1=1.0 / D,
            scalar2=None, op0=ALU.mult))
        p.drain("vector")
        op("vector", lambda e, S: e.tensor_tensor(
            out=Asm("ln_var"), in0=Asm("ln_mu"), in1=Asm("ln_mu"),
            op=ALU.mult))
        p.drain("vector")
        op("vector", lambda e, S: e.scalar_tensor_tensor(
            out=Asm("ln_var"), in0=Asm("ln_s2"), scalar=1.0 / D,
            in1=Asm("ln_var"), op0=ALU.mult, op1=ALU.subtract))
        p.drain("vector")
        b = op("vector", lambda e, S: e.tensor_scalar(
            out=Asm("ln_sd"), in0=Asm("ln_var"), scalar1=1e-5,
            scalar2=None, op0=ALU.add), inc=True)
        p.wait("scalar", "vv", b)
        c = op("scalar", lambda e, S: e.sqrt(
            out=Asm("ln_sd"), in_=Asm("ln_sd")), inc=True)
        p.wait("vector", "ss", c)
        op("vector", lambda e, S: e.reciprocal(
            out=Asm("ln_r"), in_=Asm("ln_sd")))
        p.drain("vector")
        d = op("vector", lambda e, S: e.scalar_tensor_tensor(
            out=Asm("ln_nmr"), in0=Asm("ln_mu"), scalar=-1.0,
            in1=Asm("ln_r"), op0=ALU.mult, op1=ALU.mult), inc=True)
        p.wait("scalar", "vv", d)
        return d


    prev_apply = None
    prev_lnstore = None
    for tc in range(TC):
        if prev_apply is not None:
            p.wait("sync", "ss", prev_apply)
        t = dma("Lx", lambda: A("x_t"),
                lambda tc=tc: x_in[128 * tc:128 * (tc + 1), :])
        p.wait("scalar", "Lx", t)
        ln_stats(lambda: A("x_t"))
        if prev_lnstore is not None:
            p.wait("scalar", "Ls", prev_lnstore)
        ap_ = op("scalar", lambda e, S: e.activation(
            out=Abf("lnx_bf"), in_=A("x_t"), func=AF.Identity,
            scale=Asm("ln_r"), bias=Asm("ln_nmr")), inc=True)
        prev_apply = ap_
        p.wait("sync", "ss", ap_)
        prev_lnstore = dma(
            "Ls", lambda tc=tc: s_ln1x[128 * tc:128 * (tc + 1), :],
            lambda: Abf("lnx_bf"))
    ln1_last_store = p.val("Ls")
    p.wait("scalar", "Ls", ln1_last_store)
    for dc in range(DC):
        dmaT("Tl",
             lambda dc=dc: Abf("lfm").rearrange(
                 "p (c f) -> p c f", f=1024)[:, dc, :],
             lambda dc=dc: s_ln1x[:, 128 * dc:128 * (dc + 1)])
    lfm_done = p.val("Tl")

    prep_state = {"fin": None, "fin_quant": True, "store": None}

    def prep_rows(src, dst_dram, nrow_chunk, nchunks, ncol, quant):
        prev_fin = prep_state["fin"]
        prev_quant = prep_state["fin_quant"]
        prev_store = prep_state["store"]
        for ch in range(nchunks):
            r0 = nrow_chunk * ch
            nsub = nrow_chunk // 128
            nfree = nsub * ncol
            if prev_fin is not None:
                p.wait("sync", "vv" if prev_quant else "ss", prev_fin)
            prev_quant = quant
            t0 = dma("Lp",
                     lambda nsub=nsub, ncol=ncol: A("prep_f").rearrange(
                         "p (c f) -> p c f", f=ncol)[:, 0:nsub, :],
                     lambda src=src, r0=r0, nrow_chunk=nrow_chunk:
                     src[r0:r0 + nrow_chunk, :].rearrange(
                         "(c p) f -> p c f", p=128))
            eng2 = "vector" if quant else "scalar"
            p.wait(eng2, "Lp", t0)
            if prev_store is not None:
                p.wait(eng2, "Ls", prev_store)
            if quant:
                fin = quant_chain(
                    lambda nfree=nfree: A("prep_f")[:, :nfree],
                    lambda nfree=nfree: Abf("prep_q")[:, :nfree], nfree)
            else:
                fin = op("scalar", lambda e, S, nfree=nfree: e.copy(
                    out=Abf("prep_q")[:, :nfree],
                    in_=A("prep_f")[:, :nfree]), inc=True)
            p.wait("sync", "vv" if quant else "ss", fin)
            prev_store = dma(
                "Ls",
                lambda dst_dram=dst_dram, r0=r0, nrow_chunk=nrow_chunk:
                dst_dram[r0:r0 + nrow_chunk, :].rearrange(
                    "(c p) f -> p c f", p=128),
                lambda nsub=nsub, ncol=ncol: Abf("prep_q").rearrange(
                    "p (c f) -> p c f", f=ncol)[:, 0:nsub, :])
            prev_fin = fin
        prep_state["fin"] = prev_fin
        prep_state["fin_quant"] = quant
        prep_state["store"] = prev_store
        return prev_store

    prep_rows(w1_in, s_w1, 1024, 8, 256, True)
    prep_rows(w2_in, s_w2, 256, 8, 1024, True)
    for m in "qkvo":
        prep_rows(w_in[m], s_w[m], 1024, 2, 256, False)
    wprep_stored = p.val("Ls")
    p.wait("scalar", "Ls", wprep_stored)

    def prep_transpose(src_dram, ncols_src, off_dst, nrows_src):
        for i in range(ncols_src // 128):
            tW = dmaT("Tw",
                      lambda nrows_src=nrows_src: Abf("wtr")[:, :nrows_src],
                      lambda src_dram=src_dram, i=i:
                      src_dram[:, 128 * i:128 * (i + 1)])
            p.wait("scalar", "Tw", tW)
            stW = dma("Tw",
                      lambda off_dst=off_dst, i=i, nrows_src=nrows_src:
                      ag1_in[0, off_dst + 128 * i * nrows_src:
                             off_dst + 128 * (i + 1) * nrows_src].rearrange(
                          "(p f) -> p f", p=128),
                      lambda nrows_src=nrows_src: Abf("wtr")[:, :nrows_src])
            p.wait("scalar", "Tw", stW)

    prep_transpose(s_w1, 256, OFF_W1, DFF)
    prep_transpose(s_w2, 1024, OFF_W2, D)
    for m, off in (("q", OFF_Q), ("k", OFF_K), ("v", OFF_V), ("o", OFF_O)):
        prep_transpose(s_w[m], 256, off, D)
    wprep_done = p.val("Tw")

    p.wait("gpsimd", "Tw", wprep_done)
    ag1_done = p.inc("ag")
    p.on("gpsimd", lambda e, S: e.collective_compute(
        "AllGather", ALU.bypass, replica_groups=[list(range(NP))],
        ins=[ag1_in[:, :]], outs=[ag1_out[:, :]]).then_inc(S["ag"], 1))

    if DEBUG_PHASE >= 1:
        p.wait("gpsimd", "ag", ag1_done)
        dma("Gd", lambda: dbg["ag1"][:, :], lambda: ag1_in[:, :])
        dma("Gd", lambda: dbg["ln1x"][:, :], lambda: s_ln1x[:, :])

    c1, c2, c3 = _cody_waite_consts()
    do_rope = "rope" not in SKIP
    iv = op("gpsimd", lambda e, S: e.iota(
        out=Asm("iv_i")[:, :], pattern=[[0, 2], [1, 32]], base=0,
        channel_multiplier=0), inc=True)
    mi = None
    mv = 0
    for j in ([] if "masks" in SKIP else range(4)):
        mi = op("gpsimd", lambda e, S, j=j: e.iota(
            out=Asm("msk_ii")[:, :], pattern=[[1, 512]], base=-128 * j,
            channel_multiplier=-1), inc=True)
        p.wait("scalar", "gc", mi)
        mc = op("scalar", lambda e, S: e.copy(
            out=Asm("msk_s"), in_=Asm("msk_ii")), inc=True)
        p.wait("vector", "ss", mc)
        mv = op("vector", lambda e, S, j=j: e.tensor_scalar(
            out=Abf("mskA")[:, 512 * j:512 * (j + 1)], in0=Asm("msk_s"),
            scalar1=0.0, scalar2=-1e30, op0=ALU.is_lt, op1=ALU.mult),
            inc=True)
        if j < 3:
            p.wait("gpsimd", "vv", mv)
    masks_done = mv
    p.wait("scalar", "gc", iv)
    s1v = op("scalar", lambda e, S: e.activation(
        out=A("invfB"), in_=Asm("iv_i")[:, :], func=AF.Exp,
        scale=-math.log(10000.0) / 32.0), inc=True)
    p.wait("vector", "ss", s1v)
    op("vector", lambda e, S: e.memset(Asm("bm10"), -10.0))
    op("vector", lambda e, S: e.memset(Asm("m512"), 512.0))
    op("vector", lambda e, S: e.memset(Asm("onescol"), 1.0))
    consts_v = op("vector", lambda e, S: e.memset(Asm("onesrow"), 1.0),
                  inc=True)
    for tc in (range(TC) if do_rope else []):
        ldp = dma("Lr", lambda: Asm("rp_pos"),
                  lambda tc=tc: pos_in[0, 128 * tc:128 * (tc + 1)].rearrange(
                      "(p one) -> p one", one=1))
        p.wait("vector", "Lr", ldp)
        thv = op("vector", lambda e, S: e.tensor_scalar(
            out=A("rp_th"), in0=A("invfB"), scalar1=Asm("rp_pos"),
            scalar2=None, op0=ALU.mult), inc=True)
        p.wait("sync", "vv", thv)
        p.drain("vector")
        op("vector", lambda e, S: e.tensor_scalar(
            out=A("rp_u"), in0=A("rp_th"), scalar1=1.0 / TWO_PI,
            scalar2=None, op0=ALU.mult))
        p.drain("vector")
        op("vector", lambda e, S: e.tensor_scalar(
            out=A("rp_k"), in0=A("rp_u"), scalar1=MAGIC, scalar2=MAGIC,
            op0=ALU.add, op1=ALU.subtract))
        p.drain("vector")
        op("vector", lambda e, S: e.scalar_tensor_tensor(
            out=A("rp_r"), in0=A("rp_k"), scalar=-c1, in1=A("rp_th"),
            op0=ALU.mult, op1=ALU.add))
        p.drain("vector")
        op("vector", lambda e, S: e.scalar_tensor_tensor(
            out=A("rp_u"), in0=A("rp_k"), scalar=-c2, in1=A("rp_r"),
            op0=ALU.mult, op1=ALU.add))
        p.drain("vector")
        op("vector", lambda e, S: e.scalar_tensor_tensor(
            out=A("rp_r"), in0=A("rp_k"), scalar=-c3, in1=A("rp_u"),
            op0=ALU.mult, op1=ALU.add))
        p.drain("vector")
        op("vector", lambda e, S: e.tensor_scalar(
            out=A("rp_si"), in0=A("rp_r"), scalar1=math.pi, scalar2=None,
            op0=ALU.is_gt))
        p.drain("vector")
        op("vector", lambda e, S: e.tensor_scalar(
            out=A("rp_u"), in0=A("rp_r"), scalar1=-math.pi, scalar2=None,
            op0=ALU.is_lt))
        p.drain("vector")
        op("vector", lambda e, S: e.tensor_tensor(
            out=A("rp_u"), in0=A("rp_u"), in1=A("rp_si"), op=ALU.subtract))
        p.drain("vector")
        op("vector", lambda e, S: e.scalar_tensor_tensor(
            out=A("rp_si"), in0=A("rp_u"), scalar=TWO_PI, in1=A("rp_r"),
            op0=ALU.mult, op1=ALU.add))
        p.drain("vector")
        op("vector", lambda e, S: e.tensor_scalar(
            out=A("rp_ci"), in0=A("rp_r"), scalar1=math.pi / 2, scalar2=None,
            op0=ALU.add))
        p.drain("vector")
        op("vector", lambda e, S: e.tensor_scalar(
            out=A("rp_u"), in0=A("rp_ci"), scalar1=math.pi, scalar2=None,
            op0=ALU.is_gt))
        p.drain("vector")
        rw = op("vector", lambda e, S: e.scalar_tensor_tensor(
            out=A("rp_ci"), in0=A("rp_u"), scalar=-TWO_PI, in1=A("rp_ci"),
            op0=ALU.mult, op1=ALU.add), inc=True)
        p.wait("scalar", "vv", rw)
        op("scalar", lambda e, S: e.activation(
            out=A("rp_sin"), in_=A("rp_si"), func=AF.Sin))
        sc = op("scalar", lambda e, S, tc=tc: e.activation(
            out=A("cosT")[:, 64 * tc:64 * (tc + 1)], in_=A("rp_ci"),
            func=AF.Sin), inc=True)
        p.wait("vector", "ss", sc)
        op("vector", lambda e, S, tc=tc: e.tensor_scalar(
            out=A("sinS")[:, 64 * tc:64 * tc + 32], in0=A("rp_sin")[:, 0:32],
            scalar1=-1.0, scalar2=None, op0=ALU.mult))
        op("vector", lambda e, S, tc=tc: e.tensor_copy(
            out=A("sinS")[:, 64 * tc + 32:64 * (tc + 1)],
            in_=A("rp_sin")[:, 32:64]), inc=True)
    rope_done = p.val("vv")

    WBUFS = ["wrhs0", "wrhs1", "wrhs2", "wrhs3"]
    WLANE = {"wrhs0": "W0", "wrhs1": "W1", "wrhs2": "W2", "wrhs3": "W3",
             "wrhsB0": "W0", "wrhsB1": "W1", "wrhsB2": "W2", "wrhsB3": "W3",
             "wrhsD0": "W0", "wrhsD1": "W1"}
    wb_state = {"i": 0, "guard": {}}

    def stream_w(tile_fn, bufs=None):
        bufs = bufs if bufs is not None else WBUFS
        buf = bufs[wb_state["i"] % len(bufs)]
        wb_state["i"] += 1
        lane = WLANE[buf]
        g = wb_state["guard"].get(buf)
        if g is not None:
            p.wait("sync", g[0], g[1])
        t = dma(lane, lambda buf=buf: Abf(buf)[:, :512], tile_fn)
        return buf, t, lane

    def mm_group8(lhs_fn, tile_fn, nK, evac_fn, psum_guards=None,
                  bufs=None, pre_kc=None):
        for kc in range(nK):
            buf, t, lane = stream_w(lambda kc=kc: tile_fn(kc), bufs=bufs)
            if pre_kc is not None:
                pre_kc(kc)
            p.wait("tensor", lane, t)
            for tc in range(TC):
                if kc == 0 and psum_guards and psum_guards[tc] is not None:
                    p.wait("tensor", psum_guards[tc][0], psum_guards[tc][1])
                last = (kc == nK - 1)
                if last:
                    p.inc("mm")
                    p.on("tensor", lambda e, S, kc=kc, tc=tc, buf=buf:
                         e.matmul(ps[f"ps{tc}"][:, :], lhs_fn(kc, tc),
                                  Abf(buf)[:, :512], start=(kc == 0),
                                  stop=True).then_inc(S["mm"], 1))
                else:
                    p.inc("mq")
                    p.on("tensor", lambda e, S, kc=kc, tc=tc, buf=buf:
                         e.matmul(ps[f"ps{tc}"][:, :], lhs_fn(kc, tc),
                                  Abf(buf)[:, :512], start=(kc == 0),
                                  stop=False).then_inc(S["mq"], 1))
            wb_state["guard"][buf] = \
                ("mm", p.val("mm")) if kc == nK - 1 else ("mq", p.val("mq"))
        base = p.val("mm") - TC
        frees = []
        for tc in range(TC):
            frees.append(evac_fn(tc, base + tc + 1))
        return frees

    def lfm_lhs(kc, tc):
        return Abf("lfm").rearrange("p (c f) -> p c f", f=1024)[
            :, kc, 128 * tc:128 * (tc + 1)]

    EVB = ["ev0", "ev1", "ev2", "ev3"]
    EVLANE = {"ev0": "E0", "ev1": "E1", "ev2": "E2", "ev3": "E3"}
    ev_state = {"i": 0, "guard": {}}

    def qkv_evac(kind, pc):
        def f(tc, mm_target):
            buf = EVB[ev_state["i"] % 4]
            ev_state["i"] += 1
            weng = "scalar" if kind == "v" else "vector"
            g = ev_state["guard"].get(buf)
            if g is not None:
                p.wait(weng, EVLANE[buf], g)
            if kind == "v":
                p.wait("scalar", "mm", mm_target)
                evv = op("scalar", lambda e, S, buf=buf, tc=tc: e.activation(
                    out=Abf(buf)[:, :512], in_=ps[f"ps{tc}"][:, :],
                    func=AF.Identity), inc=True)
                p.wait("sync", "ss", evv)
                free = ("ss", evv)
            else:
                p.wait("vector", "mm", mm_target)

                def psv(tc2):
                    return ps[f"ps{tc2}"][:, :].rearrange(
                        "p (h d) -> p h d", d=64)

                def cosb(tc2):
                    return A("cosT")[:, 64 * tc2:64 * (tc2 + 1)].unsqueeze(1)\
                        .broadcast_to([128, 8, 64])

                def sinb(tc2, half):
                    s0 = 64 * tc2 + 32 * half
                    return A("sinS")[:, s0:s0 + 32].unsqueeze(1)\
                        .broadcast_to([128, 8, 32])
                op("vector", lambda e, S, tc=tc: e.tensor_tensor(
                    out=A("rope_t1").rearrange("p (h d) -> p h d", d=64),
                    in0=psv(tc), in1=cosb(tc), op=ALU.mult))
                op("vector", lambda e, S, tc=tc: e.tensor_tensor(
                    out=A("rope_t2").rearrange(
                        "p (h d) -> p h d", d=64)[:, :, 0:32],
                    in0=psv(tc)[:, :, 32:64], in1=sinb(tc, 0), op=ALU.mult))
                pf = op("vector", lambda e, S, tc=tc: e.tensor_tensor(
                    out=A("rope_t2").rearrange(
                        "p (h d) -> p h d", d=64)[:, :, 32:64],
                    in0=psv(tc)[:, :, 0:32], in1=sinb(tc, 1), op=ALU.mult),
                    inc=True)
                p.drain("vector")
                evv = op("vector", lambda e, S, buf=buf: e.tensor_tensor(
                    out=Abf(buf)[:, :512], in0=A("rope_t1"),
                    in1=A("rope_t2"), op=ALU.add), inc=True)
                p.drain("vector")
                p.wait("sync", "vv", evv)
                free = ("vv", pf)
            if kind == "q":
                dst = lambda tc=tc, pc=pc: s_qhat[
                    128 * tc:128 * (tc + 1), 512 * pc:512 * (pc + 1)]
            elif kind == "k":
                dst = lambda tc=tc, pc=pc: kv_in[0, 0:OFF_KV_V].rearrange(
                    "(t d) -> t d", d=D)[128 * tc:128 * (tc + 1),
                                         512 * pc:512 * (pc + 1)]
            else:
                dst = lambda tc=tc, pc=pc: kv_in[0, OFF_KV_V:].rearrange(
                    "(t d) -> t d", d=D)[128 * tc:128 * (tc + 1),
                                         512 * pc:512 * (pc + 1)]
            g = dma(EVLANE[buf], dst, lambda buf=buf: Abf(buf)[:, :512])
            ev_state["guard"][buf] = g
            return free
        return f

    p.wait("tensor", "ag", ag1_done)
    p.wait("tensor", "Tl", lfm_done)
    p.wait("sync", "ag", ag1_done)
    p.wait("vector", "vv", rope_done)
    pg = None
    for kind, off_m in ([] if "qkv" in SKIP else
                        (("q", OFF_Q), ("k", OFF_K), ("v", OFF_V))):
        for pc in range(4):
            pg = mm_group8(lfm_lhs,
                           lambda kc, off_m=off_m, pc=pc:
                           ag1_tile(off_m, D, 128 * kc, 512 * pc, 512),
                           DC, qkv_evac(kind, pc), psum_guards=pg)
    for _ln in EVL:
        p.wait("gpsimd", _ln, p.val(_ln))
    ag2_done = p.inc("ag")
    p.on("gpsimd", lambda e, S: e.collective_compute(
        "AllGather", ALU.bypass, replica_groups=[list(range(NP))],
        ins=[kv_in[:, :]], outs=[kv_out[:, :]]).then_inc(S["ag"], 1))

    if DEBUG_PHASE >= 2:
        p.wait("gpsimd", "ag", ag2_done)
        dma("Gd", lambda: dbg["qhat"][:, :], lambda: s_qhat[:, :])
        dma("Gd", lambda: dbg["kv"][:, :], lambda: kv_in[:, :])

    p.wait("scalar", "mm", p.val("mm"))
    for hp in range(NHP):
        dmaT("Tl",
             lambda hp=hp: Abf("qfm").rearrange(
                 "p (c f) -> p c f", f=1024)[:, hp, :],
             lambda hp=hp: s_qhat[:, 128 * hp:128 * (hp + 1)])
    qfm_done = p.val("Tl")

    def emit_scores_arm(s):
        b, c = divmod(s, 4)
        B0 = 1024 * c
        st = {"vp_i": 0, "vp_guard": {}, "A_guard": {}, "S_guard": {}}

        def khp_load(hp, lo, hi):
            if st.get("last_smm") is not None:
                p.wait("scalar", "tt", st["last_smm"])
            off = 0
            v = None
            t0 = lo
            while t0 < hi:
                r = 4 * b + t0 // 1024
                r0 = t0 % 1024
                r1 = min(1024, r0 + (hi - t0))
                n = r1 - r0
                v = dmaT("Tk",
                         lambda off=off, n=n: Abf("khp")[:, off:off + n],
                         lambda r=r, r0=r0, r1=r1, hp=hp: kvk_ap(r, r0, r1, hp))
                off += n
                t0 += n
            return v

        def do_pair(hp, qp, kcs, first, last_flags):
            Q0 = B0 + 512 * qp
            for i, (ktok, kind) in enumerate(kcs):
                par = (ktok // 128) % 2
                sA, sB = f"ps{2 * par}", f"ps{2 * par + 1}"
                vb = ["vp0", "vp1"][st["vp_i"] % 2]
                st["vp_i"] += 1
                gv = st["vp_guard"].get(vb)
                if gv is not None:
                    p.wait("sync", gv[0], gv[1])
                r = 4 * b + ktok // 1024
                r0 = ktok % 1024
                vld = dma("V0" if vb == "vp0" else "V1",
                          lambda vb=vb: Abf(vb)[:, :],
                          lambda r=r, r0=r0, hp=hp: kvv_ap(r, r0, r0 + 128, hp))
                gS = st["S_guard"].get(par)
                if gS is not None:
                    p.wait("tensor", gS[0], gS[1])
                koff = None
                smm = None
                kofs = st["koff"][ktok]
                op("tensor", lambda e, S, sA=sA, kofs=kofs, hp=hp, qp=qp:
                   e.matmul(ps[sA][:, :],
                            Abf("khp")[0:64, kofs:kofs + 128],
                            Abf("qfm").rearrange(
                                "p (c f) -> p c f", f=1024)[
                                0:64, hp, 512 * qp:512 * (qp + 1)],
                            start=True, stop=True, tile_position=(0, 0)),
                   inc=True, sem="tt")
                smm = op("tensor", lambda e, S, sB=sB, kofs=kofs, hp=hp, qp=qp:
                         e.matmul(ps[sB][:, :],
                                  Abf("khp")[64:128, kofs:kofs + 128],
                                  Abf("qfm").rearrange(
                                      "p (c f) -> p c f", f=1024)[
                                      64:128, hp, 512 * qp:512 * (qp + 1)],
                                  start=True, stop=True,
                                  tile_position=(64, 0)), inc=True, sem="tt")
                aA, aB = f"A{2 * par}", f"A{2 * par + 1}"
                for head, (sX, aX) in enumerate(((sA, aA), (sB, aB))):
                    gA = st["A_guard"].get(aX)
                    if gA is not None:
                        p.wait("scalar", gA[0], gA[1])
                    p.wait("scalar", "tt", smm)
                    if kind == "diag":
                        j = (ktok - Q0) // 128
                        assert 0 <= j < 4
                        p.wait("vector", "tt", smm)
                        if st.get("mskf_free") is not None:
                            p.wait("vector", "ss", st["mskf_free"])
                        mk = op("vector", lambda e, S, sX=sX, j=j:
                                e.tensor_tensor(
                                    out=A("msk_f"), in0=ps[sX][:, :],
                                    in1=Abf("mskA")[:, 512 * j:512 * (j + 1)],
                                    op=ALU.add), inc=True)
                        p.wait("scalar", "vv", mk)
                        ex = op("scalar", lambda e, S, aX=aX: e.activation(
                            out=Abf(aX)[:, :], in_=A("msk_f"), func=AF.Exp,
                            scale=0.125, bias=Asm("bm10")), inc=True)
                        st["mskf_free"] = ex
                    else:
                        ex = op("scalar", lambda e, S, sX=sX, aX=aX:
                                e.activation(
                                    out=Abf(aX)[:, :], in_=ps[sX][:, :],
                                    func=AF.Exp, scale=0.125,
                                    bias=Asm("bm10")), inc=True)
                    st[f"exp_{head}"] = ex
                st["S_guard"][par] = ("ss", st["exp_1"])
                st["last_smm"] = smm
                exA, exB = st["exp_0"], st["exp_1"]
                p.wait("tensor", "ss", exB)
                p.wait("tensor", "V0" if vb == "vp0" else "V1", vld)
                stt = (first and i == 0)
                stp = (last_flags and i == len(kcs) - 1)
                op("tensor", lambda e, S, vb=vb, aA=aA, stt=stt, stp=stp:
                   e.matmul(ps["ps4"][0:64, :], Abf(vb)[:, 0:64],
                            Abf(aA)[:, :], start=stt, stop=stp,
                            tile_position=(0, 0)), inc=True, sem="tt")
                op("tensor", lambda e, S, vb=vb, aB=aB, stt=stt, stp=stp:
                   e.matmul(ps["ps5"][64:128, :], Abf(vb)[:, 64:128],
                            Abf(aB)[:, :], start=stt, stop=stp,
                            tile_position=(0, 64)), inc=True, sem="tt")
                op("tensor", lambda e, S, aA=aA, stt=stt, stp=stp:
                   e.matmul(ps["ps6"][0:1, :], Asm("onescol"),
                            Abf(aA)[:, :], start=stt, stop=stp,
                            tile_position=(0, 0)), inc=True, sem="tt")
                avv = op("tensor", lambda e, S, aB=aB, stt=stt, stp=stp:
                         e.matmul(ps["ps7"][64:65, :], Asm("onescol"),
                                  Abf(aB)[:, :], start=stt, stop=stp,
                                  tile_position=(0, 64)), inc=True, sem="tt")
                st["vp_guard"][vb] = ("tt", avv)
                st["A_guard"][aA] = ("tt", avv)
                st["A_guard"][aB] = ("tt", avv)
            return avv

        def finish_pair(hp, qp, av_done):
            p.wait("scalar", "tt", av_done)
            se1 = op("scalar", lambda e, S: e.copy(
                out=Asm("serow")[0:1, 0:512], in_=ps["ps6"][0:1, :]),
                inc=True)
            se2 = op("scalar", lambda e, S: e.copy(
                out=Asm("serow")[64:65, 0:512], in_=ps["ps7"][64:65, :]),
                inc=True)
            p.wait("vector", "ss", se2)
            op("vector", lambda e, S: e.reciprocal(
                out=Asm("serow")[0:1, 512:1024], in_=Asm("serow")[0:1, 0:512]))
            op("vector", lambda e, S: e.reciprocal(
                out=Asm("serow")[64:65, 512:1024],
                in_=Asm("serow")[64:65, 0:512]))
            p.drain("vector")
            op("vector", lambda e, S: e.tensor_copy(
                out=Abf("selo")[0:1, 0:512], in_=Asm("serow")[0:1, 512:1024]))
            op("vector", lambda e, S: e.tensor_copy(
                out=Abf("selo")[64:65, 0:512],
                in_=Asm("serow")[64:65, 512:1024]))
            p.drain("vector")
            op("vector", lambda e, S: e.tensor_tensor(
                out=Asm("serow")[0:1, 0:512],
                in0=Asm("serow")[0:1, 512:1024], in1=Abf("selo")[0:1, 0:512],
                op=ALU.subtract))
            op("vector", lambda e, S: e.tensor_tensor(
                out=Asm("serow")[64:65, 0:512],
                in0=Asm("serow")[64:65, 512:1024],
                in1=Abf("selo")[64:65, 0:512], op=ALU.subtract))
            p.drain("vector")
            op("vector", lambda e, S: e.tensor_copy(
                out=Abf("selo")[0:1, 512:1024], in_=Asm("serow")[0:1, 0:512]))
            lov = op("vector", lambda e, S: e.tensor_copy(
                out=Abf("selo")[64:65, 512:1024],
                in_=Asm("serow")[64:65, 0:512]), inc=True)
            p.wait("tensor", "vv", lov)
            op("tensor", lambda e, S: e.matmul(
                ps["ps6"][0:64, :], Asm("onesrow")[0:1, :],
                Abf("selo")[0:1, 0:512], start=True, stop=False,
                tile_position=(0, 0)), inc=True, sem="tt")
            op("tensor", lambda e, S: e.matmul(
                ps["ps6"][0:64, :], Asm("onesrow")[0:1, :],
                Abf("selo")[0:1, 512:1024], start=False, stop=True,
                tile_position=(0, 0)), inc=True, sem="tt")
            op("tensor", lambda e, S: e.matmul(
                ps["ps7"][64:128, :], Asm("onesrow")[64:65, :],
                Abf("selo")[64:65, 0:512], start=True, stop=False,
                tile_position=(64, 64)), inc=True, sem="tt")
            bcv2 = op("tensor", lambda e, S: e.matmul(
                ps["ps7"][64:128, :], Asm("onesrow")[64:65, :],
                Abf("selo")[64:65, 512:1024], start=False, stop=True,
                tile_position=(64, 64)), inc=True, sem="tt")
            p.wait("scalar", "tt", bcv2)
            r1v = op("scalar", lambda e, S: e.copy(
                out=A("smrcp")[0:64, :], in_=ps["ps6"][0:64, :]), inc=True)
            r2v = op("scalar", lambda e, S: e.copy(
                out=A("smrcp")[64:128, :], in_=ps["ps7"][64:128, :]),
                inc=True)
            p.wait("vector", "ss", r2v)
            op("vector", lambda e, S, hp=hp, qp=qp: e.tensor_tensor(
                out=Abf("afm").rearrange("p (c f) -> p c f", f=1024)[
                    0:64, hp, 512 * qp:512 * (qp + 1)],
                in0=ps["ps4"][0:64, :], in1=A("smrcp")[0:64, :],
                op=ALU.mult))
            fin = op("vector", lambda e, S, hp=hp, qp=qp: e.tensor_tensor(
                out=Abf("afm").rearrange("p (c f) -> p c f", f=1024)[
                    64:128, hp, 512 * qp:512 * (qp + 1)],
                in0=ps["ps5"][64:128, :], in1=A("smrcp")[64:128, :],
                op=ALU.mult), inc=True)
            p.wait("tensor", "vv", fin)
            return fin

        for hp in range(15):
            S0 = max(0, B0 - 512)
            span_end = B0 + 1024
            st["koff"] = {tok: tok - S0 for tok in range(S0, span_end, 128)}
            kl = khp_load(hp, S0, span_end)
            p.wait("tensor", "Tk", kl)
            for qp in range(2):
                Q0 = B0 + 512 * qp
                kcs = []
                for ktok in range(S0, span_end, 128):
                    if ktok >= Q0 + 512:
                        continue
                    kcs.append((ktok, "diag" if ktok + 127 > Q0 else "full"))
                av = do_pair(hp, qp, kcs, True, True)
                finish_pair(hp, qp, av)
        hp = 15
        for qp in range(2):
            Q0 = B0 + 512 * qp
            hi_all = Q0 + 512
            scs = [(lo, min(lo + 2048, hi_all))
                   for lo in range(0, hi_all, 2048)]
            first = True
            for si, (lo, hi) in enumerate(scs):
                st["koff"] = {tok: tok - lo for tok in range(lo, hi, 128)}
                kl = khp_load(hp, lo, hi)
                p.wait("tensor", "Tk", kl)
                kcs = []
                for ktok in range(lo, hi, 128):
                    kcs.append((ktok,
                                "diag" if ktok + 127 > Q0 else "full"))
                av = do_pair(hp, qp, kcs, first, si == len(scs) - 1)
                first = False
            finish_pair(hp, qp, av)

    if UPTO >= 3:
        p.wait("tensor", "ag", p.val("ag"))
        p.wait("tensor", "Tl", qfm_done)
        p.wait("tensor", "vv", consts_v)
        p.wait("scalar", "Tl", qfm_done)
        p.wait("scalar", "ag", p.val("ag"))
        p.wait("vector", "ag", p.val("ag"))
        for _ln in EVL:
            p.wait("vector", _ln, p.val(_ln))
            p.wait("scalar", _ln, p.val(_ln))
        p.wait("sync", "ag", p.val("ag"))

        sw_engines = ("tensor", "vector", "scalar", "sync")
        snap = dict(p.cnt)
        snap_mw = dict(p.max_waiter)
        snap_gt = dict(p.gated_to)
        collected = {a: {eng: [] for eng in sw_engines} for a in range(8)}
        finals = []
        fin_mw, fin_gt = [], []
        for arm in range(8):
            p.cnt = dict(snap)
            p.max_waiter = dict(snap_mw)
            p.gated_to = dict(snap_gt)
            saved = {eng: p.ops[eng] for eng in sw_engines}
            for eng in sw_engines:
                p.ops[eng] = collected[arm][eng]
            emit_scores_arm(arm)
            for eng in sw_engines:
                p.ops[eng] = saved[eng]
            finals.append(dict(p.cnt))
            fin_mw.append(dict(p.max_waiter))
            fin_gt.append(dict(p.gated_to))
        canon = {}
        for f in finals + [snap]:
            for k, v in f.items():
                canon[k] = max(canon.get(k, 0), v)
        for arm in range(8):
            fin = finals[arm]
            for eng in sw_engines:
                tail = []
                for semn in ESEM[eng]:
                    fv = fin.get(semn, 0)
                    cv = canon.get(semn, 0)
                    tail.append((semn, fv, cv - fv))
                ops_list = collected[arm][eng]
                for semn, fv, pad in tail:
                    if fv > 0:
                        ops_list.append(
                            lambda e, S, n=semn, v=fv: e.wait_ge(S[n], v))
                    if pad > 0:
                        ops_list.append(
                            lambda e, S, n=semn, d=pad: e.sem_inc(S[n], d))
        p.cnt = canon
        p.max_waiter = {k: max(canon.get(k, 0),
                               max((m.get(k, 0) for m in fin_mw), default=0))
                        for k in set(canon) | set(snap_mw)}
        p.gated_to = {}
        for eng in sw_engines:
            def mk_switch(eng):
                arms_ops = [collected[a][eng] for a in range(8)]

                def fn(e, S):
                    pid = e.partition_id()
                    for arm in e.Switch(pid, 8):
                        for f in arms_ops[arm]:
                            f(e, S)
                return fn
            p.on(eng, mk_switch(eng))

        if DEBUG_PHASE >= 3:
            pass

    if UPTO >= 4:
        def afm_lhs(kc, tc):
            return Abf("afm").rearrange("p (c f) -> p c f", f=1024)[
                :, kc, 128 * tc:128 * (tc + 1)]

        ZEV = [("Z0", 0), ("Z1", 512), ("Z2", 1024), ("Z3", 1536)]
        zev_state = {"i": 0, "guard": {}}

        def wo_evac(pc):
            def f(tc, mm_target):
                zlane, off = ZEV[zev_state["i"] % 4]
                zev_state["i"] += 1
                g = zev_state["guard"].get(off)
                if g is not None:
                    p.wait("scalar", zlane, g)
                p.wait("scalar", "mm", mm_target)
                evv = op("scalar", lambda e, S, off=off, tc=tc: e.activation(
                    out=A("z_t")[:, off:off + 512], in_=ps[f"ps{tc}"][:, :],
                    func=AF.Identity), inc=True)
                p.wait("scalar", "ss", evv)
                st2 = dma(zlane,
                          lambda tc=tc, pc=pc: s_z[128 * tc:128 * (tc + 1),
                                                   512 * pc:512 * (pc + 1)],
                          lambda off=off: A("z_t")[:, off:off + 512])
                zev_state["guard"][off] = st2
                return ("ss", evv)
            return f

        pg2 = pg
        pg2 = [("vv", p.val("vv"))] * TC
        for pc in range(4):
            pg2 = mm_group8(afm_lhs,
                            lambda kc, pc=pc:
                            ag1_tile(OFF_O, D, 128 * kc, 512 * pc, 512),
                            DC, wo_evac(pc), psum_guards=pg2,
                            bufs=["wrhsB0", "wrhsB1", "wrhsB2", "wrhsB3"])
        wo_done = {zl: p.val(zl) for zl in ("Z0", "Z1", "Z2", "Z3")}

        if DEBUG_PHASE >= 3:
            for zl, zv in wo_done.items():
                p.wait("gpsimd", zl, zv)
            dma("Gd", lambda: dbg["z"][:, :], lambda: s_z[:, :])

        for nm, src in (("bo_b", bo_in), ("lnw_b", lnw_in), ("lnb_b", lnb_in)):
            t = dma("Gd", lambda nm=nm: A(nm),
                    lambda src=src: src[0:1, :].broadcast_to([128, D]))
        bco = p.val("Gd")
        p.wait("vector", "Gd", bco)
        p.wait("scalar", "Gd", bco)

        for zl, zv in wo_done.items():
            p.wait("sync", zl, zv)
        for tc in range(TC):
            tz = dma("Lt", lambda: A("t_t"),
                     lambda tc=tc: s_z[128 * tc:128 * (tc + 1), :])
            tx = dma("Lx", lambda: A("x_t"),
                     lambda tc=tc: x_in[128 * tc:128 * (tc + 1), :])
            p.wait("vector", "Lt", tz)
            p.wait("vector", "Lx", tx)
            zb = op("vector", lambda e, S: e.tensor_tensor(
                out=A("z_t"), in0=A("t_t"), in1=A("bo_b"), op=ALU.add),
                inc=True)
            p.wait("scalar", "vv", zb)
            ln_stats(lambda: A("z_t"))
            ap2 = op("scalar", lambda e, S: e.activation(
                out=A("t_t"), in_=A("z_t"), func=AF.Identity,
                scale=Asm("ln_r"), bias=Asm("ln_nmr")), inc=True)
            p.wait("vector", "ss", ap2)
            x1v = op("vector", lambda e, S: e.tensor_tensor(
                out=A("z_t"), in0=A("t_t"), in1=A("x_t"), op=ALU.add),
                inc=True)
            p.wait("sync", "vv", x1v)
            sx1 = dma("Ls", lambda tc=tc: s_x1[128 * tc:128 * (tc + 1), :],
                      lambda: A("z_t"))
            p.wait("scalar", "vv", x1v)
            ln_stats(lambda: A("z_t"))
            h0v = op("scalar", lambda e, S: e.activation(
                out=Abf("h0st"), in_=A("z_t"), func=AF.Identity,
                scale=Asm("ln_r"), bias=Asm("ln_nmr")), inc=True)
            p.wait("sync", "ss", h0v)
            dma("Ls", lambda tc=tc: s_h0[128 * tc:128 * (tc + 1), :],
                lambda: Abf("h0st"))
            p.wait("scalar", "ss", h0v)
            ln_stats(lambda: Abf("h0st"))
            t2v = op("scalar", lambda e, S: e.activation(
                out=A("t_t"), in_=Abf("h0st"), func=AF.Identity,
                scale=Asm("ln_r"), bias=Asm("ln_nmr")), inc=True)
            p.wait("vector", "ss", t2v)
            p.wait("vector", "Ls")
            op("vector", lambda e, S: e.tensor_tensor(
                out=A("z_t"), in0=A("t_t"), in1=A("lnw_b"), op=ALU.mult))
            p.drain("vector")
            hv = op("vector", lambda e, S: e.tensor_tensor(
                out=Abf("hst"), in0=A("z_t"), in1=A("lnb_b"), op=ALU.add),
                inc=True)
            p.wait("vector", "vv", hv)
            for half in range(2):
                qf = quant_chain(
                    lambda half=half: Abf("hst")[:, 1024 * half:1024 * (half + 1)],
                    lambda half=half: Abf("hqst")[:, 1024 * half:1024 * (half + 1)],
                    1024, vbuf="q2_v", rbuf="q2_r")
            p.wait("sync", "vv", qf)
            dma("Ls", lambda tc=tc: s_hq[128 * tc:128 * (tc + 1), :],
                lambda: Abf("hqst"))
            p.wait("sync", "vv", qf)
        hq_stored = p.val("Ls")

        if DEBUG_PHASE >= 4:
            p.wait("gpsimd", "Ls", hq_stored)
            dma("Gd", lambda: dbg["x1"][:, :], lambda: s_x1[:, :])
            dma("Gd", lambda: dbg["hq"][:, :], lambda: s_hq[:, :])

    if UPTO >= 5:
        p.wait("scalar", "Ls", hq_stored)
        for dc in range(DC):
            dmaT("Tl",
                 lambda dc=dc: Abf("hqfm").rearrange(
                     "p (c f) -> p c f", f=1024)[:, dc, :],
                 lambda dc=dc: s_hq[:, 128 * dc:128 * (dc + 1)])
        hqfm_done = p.val("Tl")

        def hq_lhs(kc, tc):
            return Abf("hqfm").rearrange("p (c f) -> p c f", f=1024)[
                :, kc, 128 * tc:128 * (tc + 1)]

        p.wait("gpsimd", "vv", p.val("vv"))
        p.wait("gpsimd", "ss", p.val("ss"))
        dma("Gd", lambda: A("b1full"),
            lambda: b1_in[0:1, :].broadcast_to([128, DFF]))
        tb2 = dma("Gd", lambda: A("b2full"),
                  lambda: b2_in[0:1, :].broadcast_to([128, D]))

        w1st = {"i": 0, "g_y1p": [None] * 4, "g_y1g": [None, None],
                "g_y1h": [None, None], "g_gqs": [None, None],
                "grp_st": [], "stash": []}

        def w1_flush():
            for (tc, sl4, evv, jc) in w1st["stash"]:
                sl = w1st["i"] & 1
                el = EVL[w1st["i"] & 3]
                w1st["i"] += 1
                p.wait("vector", "ss", evv)
                if w1st["g_y1g"][sl] is not None:
                    p.wait("vector", "ss", w1st["g_y1g"][sl])
                yv = op("vector", lambda e, S, sl=sl, sl4=sl4, jc=jc:
                        e.tensor_tensor(
                            out=Abf(f"y1g{sl}")[:, :], in0=A(f"y1p{sl4}"),
                            in1=A("b1full")[:, 512 * jc:512 * (jc + 1)],
                            op=ALU.add), inc=True)
                w1st["g_y1p"][sl4] = yv
                p.wait("scalar", "vv", yv)
                if w1st["g_y1h"][sl] is not None:
                    p.wait("scalar", "vv", w1st["g_y1h"][sl])
                gv = op("scalar", lambda e, S, sl=sl: e.activation(
                    out=Abf(f"y1h{sl}")[:, :], in_=Abf(f"y1g{sl}")[:, :],
                    func=AF.Gelu_apprx_tanh), inc=True)
                w1st["g_y1g"][sl] = gv
                p.wait("vector", "ss", gv)
                if w1st["g_gqs"][sl] is not None:
                    gl, gvv = w1st["g_gqs"][sl]
                    p.wait("vector", gl, gvv)
                qv = quant_chain(lambda sl=sl: Abf(f"y1h{sl}")[:, :],
                                 lambda sl=sl: Abf(f"gqs{sl}")[:, :],
                                 512, vbuf="q2_v", rbuf="q2_r")
                w1st["g_y1h"][sl] = qv
                p.wait("sync", "vv", qv)
                stv = dma(el,
                          lambda tc=tc, jc=jc: s_gq[128 * tc:128 * (tc + 1),
                                                    512 * jc:512 * (jc + 1)],
                          lambda sl=sl: Abf(f"gqs{sl}")[:, :])
                w1st["g_gqs"][sl] = (el, stv)
                w1st["grp_st"].append((el, stv))
            w1st["stash"] = []

        def w1_evac(jc):
            def f(tc, mm_target):
                sl4 = tc & 3
                p.wait("scalar", "mm", mm_target)
                if w1st["g_y1p"][sl4] is not None:
                    p.wait("scalar", "vv", w1st["g_y1p"][sl4])
                evv = op("scalar", lambda e, S, sl4=sl4, tc=tc: e.activation(
                    out=A(f"y1p{sl4}"), in_=ps[f"ps{tc}"][:, :],
                    func=AF.Identity), inc=True)
                w1st["stash"].append((tc, sl4, evv, jc))
                if tc == 3 or tc == TC - 1:
                    w1_flush()
                return ("ss", evv)
            return f

        pg3 = [("ss", p.val("ss"))] * TC
        p.wait("tensor", "Tl", hqfm_done)
        p.wait("vector", "Gd", tb2)
        grp_done = []
        for jc in range(16):
            w1st["grp_st"] = []
            pg3 = mm_group8(hq_lhs,
                            lambda kc, jc=jc:
                            ag1_tile(OFF_W1, DFF, 128 * kc, 512 * jc, 512),
                            DC, w1_evac(jc), psum_guards=pg3)
            lanemax = {}
            for el, sv in w1st["grp_st"]:
                lanemax[el] = max(lanemax.get(el, 0), sv)
            grp_done.append(lanemax)
        w1_mm_done = p.val("mm")

        if DEBUG_PHASE >= 5:
            for el, sv in grp_done[-1].items():
                p.wait("gpsimd", el, sv)
            for lm in grp_done:
                for el, sv in lm.items():
                    p.wait("gpsimd", el, sv)
            dma("Gd", lambda: dbg["gq"][:, :], lambda: s_gq[:, :])

        w2st = {"guard": [None, None]}

        def w2_pre(kc):
            sl = kc & 1
            g = w2st["guard"][sl]
            if g is not None:
                p.wait("scalar", g[0], g[1])
            for el, sv in grp_done[kc // 4].items():
                p.wait("scalar", el, sv)
            t = dmaT("Tl",
                     lambda sl=sl: Abf(f"glhs{sl}")[:, :],
                     lambda kc=kc: s_gq[:, 128 * kc:128 * (kc + 1)])
            if kc == 63:
                w2st["guard"][sl] = ("mm", p.val("mm") + TC)
            else:
                w2st["guard"][sl] = ("mq", p.val("mq") + TC)
            p.wait("tensor", "Tl", t)

        def gq_lhs(kc, tc):
            return Abf(f"glhs{kc & 1}")[:, 128 * tc:128 * (tc + 1)]

        h0_state = {"g": [None, None], "fg": [None, None], "pend": {}}

        def w2_load_h0(oc, tc):
            sl = tc & 1
            g = h0_state["g"][sl]
            if g is not None:
                p.wait("sync", "vv", g)
            th = dma("Lh", lambda sl=sl: Abf(f"h0t{sl}")[:, :],
                     lambda tc=tc, oc=oc: s_h0[128 * tc:128 * (tc + 1),
                                               512 * oc:512 * (oc + 1)])
            h0_state["pend"][(oc, tc)] = th

        def w2_evac(oc):
            def f(tc, mm_target):
                sl = tc & 1
                fsl = w1st["i"] & 1
                w1st["i"] += 1
                if (oc, tc) not in h0_state["pend"]:
                    w2_load_h0(oc, tc)
                th = h0_state["pend"].pop((oc, tc))
                nxt = (oc, tc + 1) if tc < TC - 1 else                     ((oc + 1, 0) if oc < 3 else None)
                if nxt is not None:
                    w2_load_h0(*nxt)
                p.wait("vector", "mm", mm_target)
                p.wait("vector", "Lh", p.val("Lh"))
                if h0_state.get("pfv") is not None:
                    p.wait("vector", "vv", h0_state["pfv"])
                yv = op("vector", lambda e, S, tc=tc, oc=oc: e.tensor_tensor(
                    out=A("y2_f"), in0=ps[f"ps{tc}"][:, :],
                    in1=A("b2full")[:, 512 * oc:512 * (oc + 1)],
                    op=ALU.add), inc=True)
                p.drain("vector")
                if h0_state["fg"][fsl] is not None:
                    p.wait("vector", "Lf", p.val("Lf"))
                fv = op("vector", lambda e, S, sl=sl, fsl=fsl:
                        e.tensor_tensor(
                            out=Abf(f"fot{fsl}")[:, :], in0=A("y2_f"),
                            in1=Abf(f"h0t{sl}")[:, :], op=ALU.add), inc=True)
                h0_state["g"][sl] = fv
                h0_state["pfv"] = fv
                p.wait("sync", "vv", fv)
                sv = dma("Lf",
                         lambda tc=tc, oc=oc: s_fo[128 * tc:128 * (tc + 1),
                                                   512 * oc:512 * (oc + 1)],
                         lambda fsl=fsl: Abf(f"fot{fsl}")[:, :])
                h0_state["fg"][fsl] = sv
                return ("vv", yv)
            return f

        pg4 = pg3
        for oc in range(4):
            pg4 = mm_group8(gq_lhs,
                            lambda kc, oc=oc:
                            ag1_tile(OFF_W2, D, 128 * kc, 512 * oc, 512, 1024),
                            64, w2_evac(oc), psum_guards=pg4,
                            bufs=["wrhsD0", "wrhsD1"], pre_kc=w2_pre)
        fo_stored = p.val("Lf")

        p.wait("sync", "Lf", fo_stored)
        p.wait("scalar", "mm", p.val("mm"))
        fprev = None
        for tc in range(TC):
            if fprev is not None:
                p.wait("sync", "vv", fprev)
            tf = dma("Lg", lambda: Abf("fo_ld"),
                     lambda tc=tc: s_fo[128 * tc:128 * (tc + 1), :])
            tx1 = dma("Lt", lambda: A("t_t"),
                      lambda tc=tc: s_x1[128 * tc:128 * (tc + 1), :])
            p.wait("scalar", "Lg", tf)
            p.wait("scalar", "Lt", tx1)
            p.wait("vector", "Lt", tx1)
            ln_stats(lambda: Abf("fo_ld"))
            apf = op("scalar", lambda e, S: e.activation(
                out=A("z_t"), in_=Abf("fo_ld"), func=AF.Identity,
                scale=Asm("ln_r"), bias=Asm("ln_nmr")), inc=True)
            p.wait("vector", "ss", apf)
            if fprev is not None:
                p.wait("vector", "Ls", syp)
            yv = op("vector", lambda e, S: e.tensor_tensor(
                out=A("x_t"), in0=A("z_t"), in1=A("t_t"), op=ALU.add),
                inc=True)
            p.wait("sync", "vv", yv)
            syp = dma("Ls", lambda tc=tc: y_out[128 * tc:128 * (tc + 1), :],
                      lambda: A("x_t"))
            fprev = yv
    else:
        pv = None
        for tc in range(TC):
            if pv is not None:
                p.wait("sync", "Ls", pv)
            tl0 = dma("Lx", lambda: A("x_t"),
                      lambda tc=tc: x_in[128 * tc:128 * (tc + 1), :])
            p.wait("sync", "Lx", tl0)
            pv = dma("Ls", lambda tc=tc: y_out[128 * tc:128 * (tc + 1), :],
                     lambda: A("x_t"))

    for _ln in SYNC_LANES + SCAL_LANES + GP_LANES:
        if p.val(_ln) > 0:
            p.wait("gpsimd", _ln, p.val(_ln))

    import contextlib
    with contextlib.ExitStack() as stack:
        sb["AF32"] = stack.enter_context(
            nc.sbuf_tensor("AF32", [128, NF32], F32))
        sb["ABF"] = stack.enter_context(
            nc.sbuf_tensor("ABF", [128, NBF], BF16))
        for name, (shape, dt) in sm.items():
            sb[name] = stack.enter_context(nc.sbuf_tensor(name, shape, dt))
        for i in range(8):
            ps[f"ps{i}"] = stack.enter_context(
                nc.psum_tensor(f"ps{i}", [128, 512], F32))
        S = {n: stack.enter_context(nc.semaphore(n)) for n in SEM_NAMES}
        p.emit(nc, S)
    return nc


def kernel(L0_attn_hidden_states, L0_attn_Wq, L0_attn_Wk, L0_attn_Wv,
           L0_attn_Wo, L0_attn_bo, L0_ffn_ln_w, L0_ffn_ln_b, L0_ffn_W1,
           L0_ffn_b1, L0_ffn_W2, L0_ffn_b2):
    x = np.ascontiguousarray(np.asarray(L0_attn_hidden_states, np.float32))
    Wq = np.asarray(L0_attn_Wq, np.float32)
    Wk = np.asarray(L0_attn_Wk, np.float32)
    Wv = np.asarray(L0_attn_Wv, np.float32)
    Wo = np.asarray(L0_attn_Wo, np.float32)
    W1 = np.asarray(L0_ffn_W1, np.float32)
    W2 = np.asarray(L0_ffn_W2, np.float32)
    in_maps = []
    for s in range(NP):
        b, c = divmod(s, 4)
        in_maps.append({
            "x": np.ascontiguousarray(x[b, 1024 * c:1024 * (c + 1), :]),
            "pos": np.arange(1024 * c, 1024 * (c + 1),
                             dtype=np.float32)[None, :],
            "wq": np.ascontiguousarray(Wq[:, 256 * s:256 * (s + 1)]),
            "wk": np.ascontiguousarray(Wk[:, 256 * s:256 * (s + 1)]),
            "wv": np.ascontiguousarray(Wv[:, 256 * s:256 * (s + 1)]),
            "wo": np.ascontiguousarray(Wo[:, 256 * s:256 * (s + 1)]),
            "w1": np.ascontiguousarray(W1[:, 256 * s:256 * (s + 1)]),
            "w2": np.ascontiguousarray(W2[:, 1024 * s:1024 * (s + 1)]),
            "bo": np.asarray(L0_attn_bo, np.float32)[None, :],
            "lnw": np.asarray(L0_ffn_ln_w, np.float32)[None, :],
            "lnb": np.asarray(L0_ffn_ln_b, np.float32)[None, :],
            "b1": np.asarray(L0_ffn_b1, np.float32)[None, :],
            "b2": np.asarray(L0_ffn_b2, np.float32)[None, :],
        })
    nc = build_nc()
    res = run_bass_kernel_spmd(nc, in_maps, core_ids=list(range(NP)))
    out = np.empty((B, L, D), np.float32)
    for s in range(NP):
        b, c = divmod(s, 4)
        out[b, 1024 * c:1024 * (c + 1), :] = res.results[s]["y"]
    return out


# version 04
# speedup vs baseline: 1.0383x; 1.0288x over previous
import math
import numpy as np
import ml_dtypes
import concourse.bass as bass
import concourse.mybir as mybir
from concourse.bass_utils import run_bass_kernel_spmd

F32 = mybir.dt.float32
BF16 = mybir.dt.bfloat16
I32 = mybir.dt.int32
AF = mybir.ActivationFunctionType
ALU = mybir.AluOpType
AX = mybir.AxisListType

NP = 8
B, L, D = 2, 4096, 2048
T = 1024
DFF = 8192
TC = T // 128
DC = D // 128
NHP = 16
MAGIC = float(np.float32(1.5 * 2 ** 23))
TWO_PI = 2.0 * math.pi

OFF_Q = 0
OFF_K = OFF_Q + 256 * D
OFF_V = OFF_K + 256 * D
OFF_O = OFF_V + 256 * D
OFF_W1 = OFF_O + 256 * D
OFF_W2 = OFF_W1 + 256 * DFF
R1 = OFF_W2 + 1024 * D

OFF_KV_V = T * D
R2 = 2 * T * D

UPTO = 99
DEBUG_PHASE = 0
SKIP = set()

CSEM = {"vector": "vv", "scalar": "ss", "gpsimd": "gc", "tensor": "tt"}
SYNC_LANES = ["Lx", "Lt", "Lp", "Ls", "Lr", "W0", "W1", "W2", "W3",
              "E0", "E1", "E2", "E3", "V0", "V1", "Lb", "Lh", "Lf", "Lg"]
SCAL_LANES = ["Tl", "Tw", "Tk", "Z0", "Z1", "Z2", "Z3"]
GP_LANES = ["Gd"]
LANE_ENG = {**{n: "sync" for n in SYNC_LANES},
            **{n: "scalar" for n in SCAL_LANES},
            **{n: "gpsimd" for n in GP_LANES}}
EVL = ["E0", "E1", "E2", "E3"]
WL = ["W0", "W1", "W2", "W3"]
ESEM = {"sync": SYNC_LANES, "scalar": SCAL_LANES + ["ss"], "vector": ["vv"],
        "tensor": ["tt", "mm", "mq"], "gpsimd": GP_LANES + ["gc"]}
SEM_NAMES = (SYNC_LANES + SCAL_LANES + GP_LANES +
             ["ag", "vv", "ss", "tt", "mm", "mq", "gc"])


def _cody_waite_consts():
    def chop(x):
        v = np.frombuffer(np.float32(x).tobytes(), dtype=np.uint32)[0]
        v = np.uint32(v & np.uint32(0xFFFFF000))
        return float(np.frombuffer(v.tobytes(), dtype=np.float32)[0])
    c1 = chop(TWO_PI)
    c2 = chop(TWO_PI - c1)
    c3 = float(np.float32(TWO_PI - c1 - c2))
    return c1, c2, c3


class Prog:
    ENGS = ("sync", "scalar", "vector", "tensor", "gpsimd")

    def __init__(self):
        self.ops = {e: [] for e in self.ENGS}
        self.cnt = {}
        self.max_waiter = {}
        self.gated_to = {}

    def on(self, eng, fn):
        self.ops[eng].append(fn)

    def inc(self, name, by=1):
        self.cnt[name] = self.cnt.get(name, 0) + by
        return self.cnt[name]

    def val(self, name):
        return self.cnt.get(name, 0)

    def wait(self, eng, name, val=None):
        v = self.cnt.get(name, 0) if val is None else val
        if name in LANE_ENG:
            self.max_waiter[name] = max(self.max_waiter.get(name, 0), v)
        self.on(eng, lambda e, S, n=name, v=v: e.wait_ge(S[n], v))

    def drain(self, eng):
        self.on(eng, lambda e, S: e.drain())

    def emit(self, nc, sems):
        with nc.Block() as block:
            for eng_name in self.ENGS:
                deco = getattr(block, eng_name)
                ops = self.ops[eng_name]

                def make(ops):
                    def body(e):
                        for fn in ops:
                            fn(e, sems)
                    return body
                deco(make(ops))


def build_nc():
    nc = bass.Bass("TRN2", target_bir_lowering=False, debug=False)

    x_in = nc.dram_tensor("x", [T, D], F32, kind="ExternalInput").ap()
    pos_in = nc.dram_tensor("pos", [1, T], F32, kind="ExternalInput").ap()
    w_in = {m: nc.dram_tensor(f"w{m}", [D, 256], F32, kind="ExternalInput").ap()
            for m in "qkvo"}
    w1_in = nc.dram_tensor("w1", [DFF, 256], F32, kind="ExternalInput").ap()
    w2_in = nc.dram_tensor("w2", [D, 1024], F32, kind="ExternalInput").ap()
    bo_in = nc.dram_tensor("bo", [1, D], F32, kind="ExternalInput").ap()
    lnw_in = nc.dram_tensor("lnw", [1, D], F32, kind="ExternalInput").ap()
    lnb_in = nc.dram_tensor("lnb", [1, D], F32, kind="ExternalInput").ap()
    b1_in = nc.dram_tensor("b1", [1, DFF], F32, kind="ExternalInput").ap()
    b2_in = nc.dram_tensor("b2", [1, D], F32, kind="ExternalInput").ap()
    y_out = nc.dram_tensor("y", [T, D], F32, kind="ExternalOutput").ap()

    ag1_in = nc.dram_tensor("ag1_in", [1, R1], BF16, kind="Internal").ap()
    ag1_out = nc.dram_tensor("ag1_out", [1, NP * R1], BF16, kind="Internal",
                             addr_space="Shared").ap()
    kv_in = nc.dram_tensor("kv_in", [1, R2], BF16, kind="Internal").ap()
    kv_out = nc.dram_tensor("kv_out", [1, NP * R2], BF16, kind="Internal",
                            addr_space="Shared").ap()
    s_w = {m: nc.dram_tensor(f"s_w{m}", [D, 256], BF16, kind="Internal").ap()
           for m in "qkvo"}
    s_w1 = nc.dram_tensor("s_w1", [DFF, 256], BF16, kind="Internal").ap()
    s_w2 = nc.dram_tensor("s_w2", [D, 1024], BF16, kind="Internal").ap()
    s_ln1x = nc.dram_tensor("s_ln1x", [T, D], BF16, kind="Internal").ap()
    s_qhat = nc.dram_tensor("s_qhat", [T, D], BF16, kind="Internal").ap()
    s_z = nc.dram_tensor("s_z", [T, D], F32, kind="Internal").ap()
    s_x1 = nc.dram_tensor("s_x1", [T, D], F32, kind="Internal").ap()
    s_h0 = nc.dram_tensor("s_h0", [T, D], BF16, kind="Internal").ap()
    s_hq = nc.dram_tensor("s_hq", [T, D], BF16, kind="Internal").ap()
    s_g = nc.dram_tensor("s_g", [T, DFF], BF16, kind="Internal").ap()
    s_gq = nc.dram_tensor("s_gq", [T, DFF], BF16, kind="Internal").ap()
    s_fo = nc.dram_tensor("s_fo", [T, D], BF16, kind="Internal").ap()

    dbg = {}
    if DEBUG_PHASE >= 1:
        dbg["ag1"] = nc.dram_tensor("d_ag1", [1, R1], BF16,
                                    kind="ExternalOutput").ap()
        dbg["ln1x"] = nc.dram_tensor("d_ln1x", [T, D], BF16,
                                     kind="ExternalOutput").ap()
    if DEBUG_PHASE >= 2:
        dbg["qhat"] = nc.dram_tensor("d_qhat", [T, D], BF16,
                                     kind="ExternalOutput").ap()
        dbg["kv"] = nc.dram_tensor("d_kv", [1, R2], BF16,
                                   kind="ExternalOutput").ap()
    if DEBUG_PHASE >= 3:
        dbg["z"] = nc.dram_tensor("d_z", [T, D], F32,
                                  kind="ExternalOutput").ap()
    if DEBUG_PHASE >= 4:
        dbg["x1"] = nc.dram_tensor("d_x1", [T, D], F32,
                                   kind="ExternalOutput").ap()
        dbg["hq"] = nc.dram_tensor("d_hq", [T, D], BF16,
                                   kind="ExternalOutput").ap()
    if DEBUG_PHASE >= 5:
        dbg["gq"] = nc.dram_tensor("d_gq", [T, DFF], BF16,
                                   kind="ExternalOutput").ap()

    p = Prog()

    NF32 = 15360
    NBF = 68096

    fz = {}

    def F(name, start, n):
        assert start + n <= NF32, (name, start + n)
        fz[name] = (start, n)
    F("prep_f", 0, 2048)
    F("q_v", 2048, 2048)
    F("q_r", 4096, 2048)
    F("rope_t1", 8192, 512)
    F("rope_t2", 8704, 512)
    F("cosT", 9216, 512)
    F("sinS", 9728, 512)
    F("rp_th", 10240, 64)
    F("rp_u", 10304, 64)
    F("rp_k", 10368, 64)
    F("rp_r", 10432, 64)
    F("rp_si", 10496, 64)
    F("rp_ci", 10560, 64)
    F("rp_sin", 10624, 64)
    F("invfB", 10688, 64)
    F("msk_f", 10752, 512)
    F("x_t", 11264, 2048)
    F("q2_v", 13312, 1024)
    F("q2_r", 14336, 1024)
    F("smrcp", 0, 512)
    F("z_t", 512, 2048)
    F("t_t", 2560, 2048)
    F("bo_b", 4608, 2048)
    F("lnw_b", 6656, 2048)
    F("lnb_b", 8704, 2048)
    F("bias_b", 10752, 512)
    F("y2_f", 0, 512)
    F("b1full", 512, 8192)
    F("b2full", 8704, 2048)
    F("y1p0", 10752, 512)
    F("y1p1", 11264, 512)
    F("y1p2", 11776, 512)
    F("y1p3", 12288, 512)
    F("y1p4", 12800, 512)
    F("y1p5", 0, 512)

    bz = {}

    def Bf(name, start, n):
        assert start + n <= NBF, (name, start + n)
        bz[name] = (start, n)
    Bf("lfm", 0, 16384)
    Bf("wtr", 16384, 8192)
    Bf("prep_q", 24576, 2048)
    Bf("lnx_bf", 24576, 2048)
    Bf("ev0", 26624, 512)
    Bf("ev1", 27136, 512)
    Bf("ev2", 27648, 512)
    Bf("ev3", 28160, 512)
    Bf("wrhs0", 28672, 512)
    Bf("wrhs1", 29184, 512)
    Bf("wrhs2", 29696, 512)
    Bf("wrhs3", 30208, 512)
    Bf("qfm", 0, 16384)
    Bf("afm", 16384, 16384)
    Bf("khp", 32768, 2048)
    Bf("A0", 34816, 512)
    Bf("A1", 35328, 512)
    Bf("A2", 35840, 512)
    Bf("A3", 36352, 512)
    Bf("vp0", 36864, 128)
    Bf("vp1", 36992, 128)
    Bf("selo", 37120, 1024)
    Bf("mskA", 38144, 2048)
    Bf("wrhsB0", 40960, 512)
    Bf("wrhsB1", 41472, 512)
    Bf("wrhsB2", 41984, 512)
    Bf("wrhsB3", 42496, 512)
    Bf("h0st", 16384, 2048)
    Bf("hst", 18432, 2048)
    Bf("hqst", 20480, 2048)
    Bf("hqfm", 0, 16384)
    for _i in range(6):
        Bf(f"y1g{_i}", 35840 + 512 * _i, 512)
        Bf(f"y1h{_i}", 38912 + 512 * _i, 512)
        Bf(f"gqs{_i}", 41984 + 512 * _i, 512)
    Bf("glhs0", 18432, 1024)
    Bf("glhs1", 19456, 1024)
    Bf("wrhsD0", 65536, 512)
    Bf("wrhsD1", 66048, 512)
    Bf("h0t0", 33792, 512)
    Bf("h0t1", 34304, 512)
    Bf("fot0", 66560, 512)
    Bf("fot1", 67072, 512)
    Bf("fo_ld", 65536, 2048)

    sm = {}

    def SMT(name, shape, dt):
        sm[name] = (shape, dt)
    SMT("ln_s1", [128, 1], F32)
    SMT("ln_s1b", [128, 1], F32)
    SMT("ln_s2", [128, 1], F32)
    SMT("ln_s2b", [128, 1], F32)
    SMT("ln_mu", [128, 1], F32)
    SMT("ln_var", [128, 1], F32)
    SMT("ln_sd", [128, 1], F32)
    SMT("ln_r", [128, 1], F32)
    SMT("ln_nmr", [128, 1], F32)
    SMT("iv_i", [128, 64], I32)
    SMT("rp_pos", [128, 1], F32)
    SMT("bm10", [128, 1], F32)
    SMT("m512", [128, 1], F32)
    SMT("onescol", [128, 1], BF16)
    SMT("onesrow", [128, 64], BF16)
    SMT("serow", [128, 1024], F32)
    SMT("wq_amax", [128, 64], F32)
    SMT("wq_amax2", [128, 64], F32)
    SMT("wq_bits", [128, 64], I32)
    SMT("wq_m", [128, 64], I32)
    SMT("wq_mf", [128, 64], F32)
    SMT("wq_mf2", [128, 64], F32)
    SMT("wq_sb", [128, 64], I32)
    SMT("wq_scale", [128, 64], F32)
    SMT("wq_inv", [128, 64], F32)
    SMT("msk_ii", [128, 512], I32)
    SMT("msk_s", [128, 512], F32)

    sb = {}
    ps = {}

    def A(name):
        s0, n = fz[name]
        return sb["AF32"][:, s0:s0 + n]

    def Abf(name):
        s0, n = bz[name]
        return sb["ABF"][:, s0:s0 + n]

    def Asm(name):
        return sb[name][:, :]

    def _lane_gate(lane):
        eng = LANE_ENG[lane]
        mw = p.max_waiter.get(lane, 0)
        if mw > p.gated_to.get(lane, 0):
            p.on(eng, lambda e, S, n=lane, v=mw: e.wait_ge(S[n], v))
            p.gated_to[lane] = mw

    def dma(lane, dst_fn, src_fn):
        eng = LANE_ENG[lane]
        _lane_gate(lane)
        v = p.inc(lane, 16)
        p.on(eng, lambda e, S, d=dst_fn, s=src_fn, sem=lane:
             e.dma_start(out=d(), in_=s()).then_inc(S[sem], 16))
        return v

    def dmaT(lane, dst_fn, src_fn):
        eng = LANE_ENG[lane]
        _lane_gate(lane)
        v = p.inc(lane, 16)
        p.on(eng, lambda e, S, d=dst_fn, s=src_fn, sem=lane:
             e.dma_start_transpose(out=d(), in_=s()).then_inc(S[sem], 16))
        return v

    def op(eng, mk, inc=False, sem=None):
        if inc:
            sem = sem or CSEM[eng]
            v = p.inc(sem)
            p.on(eng, lambda e, S, mk=mk, sem=sem:
                 mk(e, S).then_inc(S[sem], 1))
            return v
        p.on(eng, lambda e, S, mk=mk: mk(e, S))
        return None

    def ag1_tile(off_m, rowlen, d0, col0, ncol, shard_rows=256):
        r = d0 // shard_rows
        i = d0 % shard_rows
        base = r * R1 + off_m + i * rowlen
        flat = ag1_out[0, :]
        return flat[base:base + 128 * rowlen].rearrange(
            "(p f) -> p f", f=rowlen)[:, col0:col0 + ncol]

    def kvk_ap(rank, r0, r1, hp):
        base = rank * R2
        flat = kv_out[0, :]
        return flat[base:base + T * D].rearrange(
            "(t d) -> t d", d=D)[r0:r1, 128 * hp:128 * (hp + 1)]

    def kvv_ap(rank, r0, r1, hp):
        base = rank * R2 + OFF_KV_V
        flat = kv_out[0, :]
        return flat[base:base + T * D].rearrange(
            "(t d) -> t d", d=D)[r0:r1, 128 * hp:128 * (hp + 1)]

    def quant_chain(src_fn, dst_fn, nfree, vbuf="q_v", rbuf="q_r"):
        nblk = nfree // 32
        assert nblk <= 64

        def v3(ap):
            return ap.rearrange("p (nb b) -> p nb b", b=32)
        op("vector", lambda e, S: e.tensor_reduce(
            out=Asm("wq_amax")[:, :nblk], in_=v3(src_fn()), axis=AX.X,
            op=ALU.max, apply_absolute_value=True))
        p.drain("vector")
        op("vector", lambda e, S: e.tensor_scalar(
            out=Asm("wq_amax2")[:, :nblk], in0=Asm("wq_amax")[:, :nblk],
            scalar1=1e-30, scalar2=None, op0=ALU.max))
        p.drain("vector")
        op("vector", lambda e, S: e.tensor_copy(
            out=Asm("wq_bits")[:, :nblk],
            in_=Asm("wq_amax2")[:, :nblk].bitcast(I32)))
        p.drain("vector")
        a = op("vector", lambda e, S: e.tensor_scalar(
            out=Asm("wq_m")[:, :nblk], in0=Asm("wq_bits")[:, :nblk],
            scalar1=23, scalar2=None, op0=ALU.logical_shift_right), inc=True)
        p.wait("scalar", "vv", a)
        b = op("scalar", lambda e, S: e.copy(
            out=Asm("wq_mf")[:, :nblk], in_=Asm("wq_m")[:, :nblk]), inc=True)
        p.wait("vector", "ss", b)
        c = op("vector", lambda e, S: e.tensor_scalar(
            out=Asm("wq_mf2")[:, :nblk], in0=Asm("wq_mf")[:, :nblk],
            scalar1=float(1 << 23), scalar2=-float(6 << 23),
            op0=ALU.mult, op1=ALU.add), inc=True)
        p.wait("scalar", "vv", c)
        d = op("scalar", lambda e, S: e.copy(
            out=Asm("wq_sb")[:, :nblk], in_=Asm("wq_mf2")[:, :nblk]), inc=True)
        p.wait("vector", "ss", d)
        op("vector", lambda e, S: e.tensor_copy(
            out=Asm("wq_scale")[:, :nblk],
            in_=Asm("wq_sb")[:, :nblk].bitcast(F32)))
        p.drain("vector")
        op("vector", lambda e, S: e.reciprocal(
            out=Asm("wq_inv")[:, :nblk], in_=Asm("wq_scale")[:, :nblk]))
        p.drain("vector")
        op("vector", lambda e, S: e.tensor_tensor(
            out=v3(A(vbuf)[:, :nfree]), in0=v3(src_fn()),
            in1=Asm("wq_inv")[:, :nblk].unsqueeze(2).broadcast_to(
                [128, nblk, 32]), op=ALU.mult))
        p.drain("vector")
        op("vector", lambda e, S: e.tensor_scalar(
            out=A(rbuf)[:, :nfree], in0=A(vbuf)[:, :nfree],
            scalar1=MAGIC, scalar2=MAGIC, op0=ALU.add, op1=ALU.subtract))
        p.drain("vector")
        op("vector", lambda e, S: e.tensor_scalar(
            out=A(vbuf)[:, :nfree], in0=A(rbuf)[:, :nfree],
            scalar1=127.0, scalar2=-128.0, op0=ALU.min, op1=ALU.max))
        p.drain("vector")
        fin = op("vector", lambda e, S: e.tensor_tensor(
            out=v3(dst_fn()), in0=v3(A(vbuf)[:, :nfree]),
            in1=Asm("wq_scale")[:, :nblk].unsqueeze(2).broadcast_to(
                [128, nblk, 32]), op=ALU.mult), inc=True)
        return fin

    def ln_stats(src_fn):
        op("scalar", lambda e, S: e.activation(
            out=A("q2_v"), in_=src_fn()[:, 0:1024], func=AF.Square,
            accum_out=Asm("ln_s2")))
        h1 = op("scalar", lambda e, S: e.activation(
            out=A("q2_r"), in_=src_fn()[:, 1024:2048], func=AF.Square,
            accum_out=Asm("ln_s2b")), inc=True)
        p.wait("scalar", "ss", h1)
        op("scalar", lambda e, S: e.activation(
            out=A("q2_v"), in_=src_fn()[:, 0:1024], func=AF.Identity,
            accum_out=Asm("ln_s1")))
        a = op("scalar", lambda e, S: e.activation(
            out=A("q2_r"), in_=src_fn()[:, 1024:2048], func=AF.Identity,
            accum_out=Asm("ln_s1b")), inc=True)
        p.wait("vector", "ss", a)
        op("vector", lambda e, S: e.tensor_tensor(
            out=Asm("ln_s1"), in0=Asm("ln_s1"), in1=Asm("ln_s1b"),
            op=ALU.add))
        op("vector", lambda e, S: e.tensor_tensor(
            out=Asm("ln_s2"), in0=Asm("ln_s2"), in1=Asm("ln_s2b"),
            op=ALU.add))
        p.drain("vector")
        op("vector", lambda e, S: e.tensor_scalar(
            out=Asm("ln_mu"), in0=Asm("ln_s1"), scalar1=1.0 / D,
            scalar2=None, op0=ALU.mult))
        p.drain("vector")
        op("vector", lambda e, S: e.tensor_tensor(
            out=Asm("ln_var"), in0=Asm("ln_mu"), in1=Asm("ln_mu"),
            op=ALU.mult))
        p.drain("vector")
        op("vector", lambda e, S: e.scalar_tensor_tensor(
            out=Asm("ln_var"), in0=Asm("ln_s2"), scalar=1.0 / D,
            in1=Asm("ln_var"), op0=ALU.mult, op1=ALU.subtract))
        p.drain("vector")
        b = op("vector", lambda e, S: e.tensor_scalar(
            out=Asm("ln_sd"), in0=Asm("ln_var"), scalar1=1e-5,
            scalar2=None, op0=ALU.add), inc=True)
        p.wait("scalar", "vv", b)
        c = op("scalar", lambda e, S: e.sqrt(
            out=Asm("ln_sd"), in_=Asm("ln_sd")), inc=True)
        p.wait("vector", "ss", c)
        op("vector", lambda e, S: e.reciprocal(
            out=Asm("ln_r"), in_=Asm("ln_sd")))
        p.drain("vector")
        d = op("vector", lambda e, S: e.scalar_tensor_tensor(
            out=Asm("ln_nmr"), in0=Asm("ln_mu"), scalar=-1.0,
            in1=Asm("ln_r"), op0=ALU.mult, op1=ALU.mult), inc=True)
        p.wait("scalar", "vv", d)
        return d


    prev_apply = None
    prev_lnstore = None
    for tc in range(TC):
        if prev_apply is not None:
            p.wait("sync", "ss", prev_apply)
        t = dma("Lx", lambda: A("x_t"),
                lambda tc=tc: x_in[128 * tc:128 * (tc + 1), :])
        p.wait("scalar", "Lx", t)
        ln_stats(lambda: A("x_t"))
        if prev_lnstore is not None:
            p.wait("scalar", "Ls", prev_lnstore)
        ap_ = op("scalar", lambda e, S: e.activation(
            out=Abf("lnx_bf"), in_=A("x_t"), func=AF.Identity,
            scale=Asm("ln_r"), bias=Asm("ln_nmr")), inc=True)
        prev_apply = ap_
        p.wait("sync", "ss", ap_)
        prev_lnstore = dma(
            "Ls", lambda tc=tc: s_ln1x[128 * tc:128 * (tc + 1), :],
            lambda: Abf("lnx_bf"))
    ln1_last_store = p.val("Ls")
    p.wait("scalar", "Ls", ln1_last_store)
    for dc in range(DC):
        dmaT("Tl",
             lambda dc=dc: Abf("lfm").rearrange(
                 "p (c f) -> p c f", f=1024)[:, dc, :],
             lambda dc=dc: s_ln1x[:, 128 * dc:128 * (dc + 1)])
    lfm_done = p.val("Tl")

    prep_state = {"fin": None, "fin_quant": True, "store": None}

    def prep_rows(src, dst_dram, nrow_chunk, nchunks, ncol, quant):
        prev_fin = prep_state["fin"]
        prev_quant = prep_state["fin_quant"]
        prev_store = prep_state["store"]
        for ch in range(nchunks):
            r0 = nrow_chunk * ch
            nsub = nrow_chunk // 128
            nfree = nsub * ncol
            if prev_fin is not None:
                p.wait("sync", "vv" if prev_quant else "ss", prev_fin)
            prev_quant = quant
            t0 = dma("Lp",
                     lambda nsub=nsub, ncol=ncol: A("prep_f").rearrange(
                         "p (c f) -> p c f", f=ncol)[:, 0:nsub, :],
                     lambda src=src, r0=r0, nrow_chunk=nrow_chunk:
                     src[r0:r0 + nrow_chunk, :].rearrange(
                         "(c p) f -> p c f", p=128))
            eng2 = "vector" if quant else "scalar"
            p.wait(eng2, "Lp", t0)
            if prev_store is not None:
                p.wait(eng2, "Ls", prev_store)
            if quant:
                fin = quant_chain(
                    lambda nfree=nfree: A("prep_f")[:, :nfree],
                    lambda nfree=nfree: Abf("prep_q")[:, :nfree], nfree)
            else:
                fin = op("scalar", lambda e, S, nfree=nfree: e.copy(
                    out=Abf("prep_q")[:, :nfree],
                    in_=A("prep_f")[:, :nfree]), inc=True)
            p.wait("sync", "vv" if quant else "ss", fin)
            prev_store = dma(
                "Ls",
                lambda dst_dram=dst_dram, r0=r0, nrow_chunk=nrow_chunk:
                dst_dram[r0:r0 + nrow_chunk, :].rearrange(
                    "(c p) f -> p c f", p=128),
                lambda nsub=nsub, ncol=ncol: Abf("prep_q").rearrange(
                    "p (c f) -> p c f", f=ncol)[:, 0:nsub, :])
            prev_fin = fin
        prep_state["fin"] = prev_fin
        prep_state["fin_quant"] = quant
        prep_state["store"] = prev_store
        return prev_store

    prep_rows(w1_in, s_w1, 1024, 8, 256, True)
    prep_rows(w2_in, s_w2, 256, 8, 1024, True)
    for m in "qkvo":
        prep_rows(w_in[m], s_w[m], 1024, 2, 256, False)
    wprep_stored = p.val("Ls")
    p.wait("scalar", "Ls", wprep_stored)

    def prep_transpose(src_dram, ncols_src, off_dst, nrows_src):
        for i in range(ncols_src // 128):
            tW = dmaT("Tw",
                      lambda nrows_src=nrows_src: Abf("wtr")[:, :nrows_src],
                      lambda src_dram=src_dram, i=i:
                      src_dram[:, 128 * i:128 * (i + 1)])
            p.wait("scalar", "Tw", tW)
            stW = dma("Tw",
                      lambda off_dst=off_dst, i=i, nrows_src=nrows_src:
                      ag1_in[0, off_dst + 128 * i * nrows_src:
                             off_dst + 128 * (i + 1) * nrows_src].rearrange(
                          "(p f) -> p f", p=128),
                      lambda nrows_src=nrows_src: Abf("wtr")[:, :nrows_src])
            p.wait("scalar", "Tw", stW)

    prep_transpose(s_w1, 256, OFF_W1, DFF)
    prep_transpose(s_w2, 1024, OFF_W2, D)
    for m, off in (("q", OFF_Q), ("k", OFF_K), ("v", OFF_V), ("o", OFF_O)):
        prep_transpose(s_w[m], 256, off, D)
    wprep_done = p.val("Tw")

    p.wait("gpsimd", "Tw", wprep_done)
    ag1_done = p.inc("ag")
    p.on("gpsimd", lambda e, S: e.collective_compute(
        "AllGather", ALU.bypass, replica_groups=[list(range(NP))],
        ins=[ag1_in[:, :]], outs=[ag1_out[:, :]]).then_inc(S["ag"], 1))

    if DEBUG_PHASE >= 1:
        p.wait("gpsimd", "ag", ag1_done)
        dma("Gd", lambda: dbg["ag1"][:, :], lambda: ag1_in[:, :])
        dma("Gd", lambda: dbg["ln1x"][:, :], lambda: s_ln1x[:, :])

    c1, c2, c3 = _cody_waite_consts()
    do_rope = "rope" not in SKIP
    iv = op("gpsimd", lambda e, S: e.iota(
        out=Asm("iv_i")[:, :], pattern=[[0, 2], [1, 32]], base=0,
        channel_multiplier=0), inc=True)
    mi = None
    mv = 0
    for j in ([] if "masks" in SKIP else range(4)):
        mi = op("gpsimd", lambda e, S, j=j: e.iota(
            out=Asm("msk_ii")[:, :], pattern=[[1, 512]], base=-128 * j,
            channel_multiplier=-1), inc=True)
        p.wait("scalar", "gc", mi)
        mc = op("scalar", lambda e, S: e.copy(
            out=Asm("msk_s"), in_=Asm("msk_ii")), inc=True)
        p.wait("vector", "ss", mc)
        mv = op("vector", lambda e, S, j=j: e.tensor_scalar(
            out=Abf("mskA")[:, 512 * j:512 * (j + 1)], in0=Asm("msk_s"),
            scalar1=0.0, scalar2=-1e30, op0=ALU.is_lt, op1=ALU.mult),
            inc=True)
        if j < 3:
            p.wait("gpsimd", "vv", mv)
    masks_done = mv
    p.wait("scalar", "gc", iv)
    s1v = op("scalar", lambda e, S: e.activation(
        out=A("invfB"), in_=Asm("iv_i")[:, :], func=AF.Exp,
        scale=-math.log(10000.0) / 32.0), inc=True)
    p.wait("vector", "ss", s1v)
    op("vector", lambda e, S: e.memset(Asm("bm10"), -10.0))
    op("vector", lambda e, S: e.memset(Asm("m512"), 512.0))
    op("vector", lambda e, S: e.memset(Asm("onescol"), 1.0))
    consts_v = op("vector", lambda e, S: e.memset(Asm("onesrow"), 1.0),
                  inc=True)
    for tc in (range(TC) if do_rope else []):
        ldp = dma("Lr", lambda: Asm("rp_pos"),
                  lambda tc=tc: pos_in[0, 128 * tc:128 * (tc + 1)].rearrange(
                      "(p one) -> p one", one=1))
        p.wait("vector", "Lr", ldp)
        thv = op("vector", lambda e, S: e.tensor_scalar(
            out=A("rp_th"), in0=A("invfB"), scalar1=Asm("rp_pos"),
            scalar2=None, op0=ALU.mult), inc=True)
        p.wait("sync", "vv", thv)
        p.drain("vector")
        op("vector", lambda e, S: e.tensor_scalar(
            out=A("rp_u"), in0=A("rp_th"), scalar1=1.0 / TWO_PI,
            scalar2=None, op0=ALU.mult))
        p.drain("vector")
        op("vector", lambda e, S: e.tensor_scalar(
            out=A("rp_k"), in0=A("rp_u"), scalar1=MAGIC, scalar2=MAGIC,
            op0=ALU.add, op1=ALU.subtract))
        p.drain("vector")
        op("vector", lambda e, S: e.scalar_tensor_tensor(
            out=A("rp_r"), in0=A("rp_k"), scalar=-c1, in1=A("rp_th"),
            op0=ALU.mult, op1=ALU.add))
        p.drain("vector")
        op("vector", lambda e, S: e.scalar_tensor_tensor(
            out=A("rp_u"), in0=A("rp_k"), scalar=-c2, in1=A("rp_r"),
            op0=ALU.mult, op1=ALU.add))
        p.drain("vector")
        op("vector", lambda e, S: e.scalar_tensor_tensor(
            out=A("rp_r"), in0=A("rp_k"), scalar=-c3, in1=A("rp_u"),
            op0=ALU.mult, op1=ALU.add))
        p.drain("vector")
        op("vector", lambda e, S: e.tensor_scalar(
            out=A("rp_si"), in0=A("rp_r"), scalar1=math.pi, scalar2=None,
            op0=ALU.is_gt))
        p.drain("vector")
        op("vector", lambda e, S: e.tensor_scalar(
            out=A("rp_u"), in0=A("rp_r"), scalar1=-math.pi, scalar2=None,
            op0=ALU.is_lt))
        p.drain("vector")
        op("vector", lambda e, S: e.tensor_tensor(
            out=A("rp_u"), in0=A("rp_u"), in1=A("rp_si"), op=ALU.subtract))
        p.drain("vector")
        op("vector", lambda e, S: e.scalar_tensor_tensor(
            out=A("rp_si"), in0=A("rp_u"), scalar=TWO_PI, in1=A("rp_r"),
            op0=ALU.mult, op1=ALU.add))
        p.drain("vector")
        op("vector", lambda e, S: e.tensor_scalar(
            out=A("rp_ci"), in0=A("rp_r"), scalar1=math.pi / 2, scalar2=None,
            op0=ALU.add))
        p.drain("vector")
        op("vector", lambda e, S: e.tensor_scalar(
            out=A("rp_u"), in0=A("rp_ci"), scalar1=math.pi, scalar2=None,
            op0=ALU.is_gt))
        p.drain("vector")
        rw = op("vector", lambda e, S: e.scalar_tensor_tensor(
            out=A("rp_ci"), in0=A("rp_u"), scalar=-TWO_PI, in1=A("rp_ci"),
            op0=ALU.mult, op1=ALU.add), inc=True)
        p.wait("scalar", "vv", rw)
        op("scalar", lambda e, S: e.activation(
            out=A("rp_sin"), in_=A("rp_si"), func=AF.Sin))
        sc = op("scalar", lambda e, S, tc=tc: e.activation(
            out=A("cosT")[:, 64 * tc:64 * (tc + 1)], in_=A("rp_ci"),
            func=AF.Sin), inc=True)
        p.wait("vector", "ss", sc)
        op("vector", lambda e, S, tc=tc: e.tensor_scalar(
            out=A("sinS")[:, 64 * tc:64 * tc + 32], in0=A("rp_sin")[:, 0:32],
            scalar1=-1.0, scalar2=None, op0=ALU.mult))
        op("vector", lambda e, S, tc=tc: e.tensor_copy(
            out=A("sinS")[:, 64 * tc + 32:64 * (tc + 1)],
            in_=A("rp_sin")[:, 32:64]), inc=True)
    rope_done = p.val("vv")

    WBUFS = ["wrhs0", "wrhs1", "wrhs2", "wrhs3"]
    WLANE = {"wrhs0": "W0", "wrhs1": "W1", "wrhs2": "W2", "wrhs3": "W3",
             "wrhsB0": "W0", "wrhsB1": "W1", "wrhsB2": "W2", "wrhsB3": "W3",
             "wrhsD0": "W0", "wrhsD1": "W1"}
    wb_state = {"i": 0, "guard": {}}

    def stream_w(tile_fn, bufs=None):
        bufs = bufs if bufs is not None else WBUFS
        buf = bufs[wb_state["i"] % len(bufs)]
        wb_state["i"] += 1
        lane = WLANE[buf]
        g = wb_state["guard"].get(buf)
        if g is not None:
            p.wait("sync", g[0], g[1])
        t = dma(lane, lambda buf=buf: Abf(buf)[:, :512], tile_fn)
        return buf, t, lane

    def mm_group8(lhs_fn, tile_fn, nK, evac_fn, psum_guards=None,
                  bufs=None, pre_kc=None):
        for kc in range(nK):
            buf, t, lane = stream_w(lambda kc=kc: tile_fn(kc), bufs=bufs)
            if pre_kc is not None:
                pre_kc(kc)
            p.wait("tensor", lane, t)
            for tc in range(TC):
                if kc == 0 and psum_guards and psum_guards[tc] is not None:
                    p.wait("tensor", psum_guards[tc][0], psum_guards[tc][1])
                last = (kc == nK - 1)
                if last:
                    p.inc("mm")
                    p.on("tensor", lambda e, S, kc=kc, tc=tc, buf=buf:
                         e.matmul(ps[f"ps{tc}"][:, :], lhs_fn(kc, tc),
                                  Abf(buf)[:, :512], start=(kc == 0),
                                  stop=True).then_inc(S["mm"], 1))
                else:
                    p.inc("mq")
                    p.on("tensor", lambda e, S, kc=kc, tc=tc, buf=buf:
                         e.matmul(ps[f"ps{tc}"][:, :], lhs_fn(kc, tc),
                                  Abf(buf)[:, :512], start=(kc == 0),
                                  stop=False).then_inc(S["mq"], 1))
            wb_state["guard"][buf] = \
                ("mm", p.val("mm")) if kc == nK - 1 else ("mq", p.val("mq"))
        base = p.val("mm") - TC
        frees = []
        for tc in range(TC):
            frees.append(evac_fn(tc, base + tc + 1))
        return frees

    def lfm_lhs(kc, tc):
        return Abf("lfm").rearrange("p (c f) -> p c f", f=1024)[
            :, kc, 128 * tc:128 * (tc + 1)]

    EVB = ["ev0", "ev1", "ev2", "ev3"]
    EVLANE = {"ev0": "E0", "ev1": "E1", "ev2": "E2", "ev3": "E3"}
    ev_state = {"i": 0, "guard": {}}

    def qkv_evac(kind, pc):
        def f(tc, mm_target):
            buf = EVB[ev_state["i"] % 4]
            ev_state["i"] += 1
            weng = "scalar" if kind == "v" else "vector"
            g = ev_state["guard"].get(buf)
            if g is not None:
                p.wait(weng, EVLANE[buf], g)
            if kind == "v":
                p.wait("scalar", "mm", mm_target)
                evv = op("scalar", lambda e, S, buf=buf, tc=tc: e.activation(
                    out=Abf(buf)[:, :512], in_=ps[f"ps{tc}"][:, :],
                    func=AF.Identity), inc=True)
                p.wait("sync", "ss", evv)
                free = ("ss", evv)
            else:
                p.wait("vector", "mm", mm_target)

                def psv(tc2):
                    return ps[f"ps{tc2}"][:, :].rearrange(
                        "p (h d) -> p h d", d=64)

                def cosb(tc2):
                    return A("cosT")[:, 64 * tc2:64 * (tc2 + 1)].unsqueeze(1)\
                        .broadcast_to([128, 8, 64])

                def sinb(tc2, half):
                    s0 = 64 * tc2 + 32 * half
                    return A("sinS")[:, s0:s0 + 32].unsqueeze(1)\
                        .broadcast_to([128, 8, 32])
                op("vector", lambda e, S, tc=tc: e.tensor_tensor(
                    out=A("rope_t1").rearrange("p (h d) -> p h d", d=64),
                    in0=psv(tc), in1=cosb(tc), op=ALU.mult))
                op("vector", lambda e, S, tc=tc: e.tensor_tensor(
                    out=A("rope_t2").rearrange(
                        "p (h d) -> p h d", d=64)[:, :, 0:32],
                    in0=psv(tc)[:, :, 32:64], in1=sinb(tc, 0), op=ALU.mult))
                pf = op("vector", lambda e, S, tc=tc: e.tensor_tensor(
                    out=A("rope_t2").rearrange(
                        "p (h d) -> p h d", d=64)[:, :, 32:64],
                    in0=psv(tc)[:, :, 0:32], in1=sinb(tc, 1), op=ALU.mult),
                    inc=True)
                p.drain("vector")
                evv = op("vector", lambda e, S, buf=buf: e.tensor_tensor(
                    out=Abf(buf)[:, :512], in0=A("rope_t1"),
                    in1=A("rope_t2"), op=ALU.add), inc=True)
                p.drain("vector")
                p.wait("sync", "vv", evv)
                free = ("vv", pf)
            if kind == "q":
                dst = lambda tc=tc, pc=pc: s_qhat[
                    128 * tc:128 * (tc + 1), 512 * pc:512 * (pc + 1)]
            elif kind == "k":
                dst = lambda tc=tc, pc=pc: kv_in[0, 0:OFF_KV_V].rearrange(
                    "(t d) -> t d", d=D)[128 * tc:128 * (tc + 1),
                                         512 * pc:512 * (pc + 1)]
            else:
                dst = lambda tc=tc, pc=pc: kv_in[0, OFF_KV_V:].rearrange(
                    "(t d) -> t d", d=D)[128 * tc:128 * (tc + 1),
                                         512 * pc:512 * (pc + 1)]
            g = dma(EVLANE[buf], dst, lambda buf=buf: Abf(buf)[:, :512])
            ev_state["guard"][buf] = g
            return free
        return f

    p.wait("tensor", "ag", ag1_done)
    p.wait("tensor", "Tl", lfm_done)
    p.wait("sync", "ag", ag1_done)
    p.wait("vector", "vv", rope_done)
    pg = None
    for kind, off_m in ([] if "qkv" in SKIP else
                        (("q", OFF_Q), ("k", OFF_K), ("v", OFF_V))):
        for pc in range(4):
            pg = mm_group8(lfm_lhs,
                           lambda kc, off_m=off_m, pc=pc:
                           ag1_tile(off_m, D, 128 * kc, 512 * pc, 512),
                           DC, qkv_evac(kind, pc), psum_guards=pg)
    for _ln in EVL:
        p.wait("gpsimd", _ln, p.val(_ln))
    ag2_done = p.inc("ag")
    p.on("gpsimd", lambda e, S: e.collective_compute(
        "AllGather", ALU.bypass, replica_groups=[list(range(NP))],
        ins=[kv_in[:, :]], outs=[kv_out[:, :]]).then_inc(S["ag"], 1))

    if DEBUG_PHASE >= 2:
        p.wait("gpsimd", "ag", ag2_done)
        dma("Gd", lambda: dbg["qhat"][:, :], lambda: s_qhat[:, :])
        dma("Gd", lambda: dbg["kv"][:, :], lambda: kv_in[:, :])

    p.wait("scalar", "mm", p.val("mm"))
    for hp in range(NHP):
        dmaT("Tl",
             lambda hp=hp: Abf("qfm").rearrange(
                 "p (c f) -> p c f", f=1024)[:, hp, :],
             lambda hp=hp: s_qhat[:, 128 * hp:128 * (hp + 1)])
    qfm_done = p.val("Tl")

    def emit_scores_arm(s):
        b, c = divmod(s, 4)
        B0 = 1024 * c
        st = {"vp_i": 0, "vp_guard": {}, "A_guard": {}, "S_guard": {}}

        def khp_load(hp, lo, hi):
            if st.get("last_smm") is not None:
                p.wait("scalar", "tt", st["last_smm"])
            off = 0
            v = None
            t0 = lo
            while t0 < hi:
                r = 4 * b + t0 // 1024
                r0 = t0 % 1024
                r1 = min(1024, r0 + (hi - t0))
                n = r1 - r0
                v = dmaT("Tk",
                         lambda off=off, n=n: Abf("khp")[:, off:off + n],
                         lambda r=r, r0=r0, r1=r1, hp=hp: kvk_ap(r, r0, r1, hp))
                off += n
                t0 += n
            return v

        def do_pair(hp, qp, kcs, first, last_flags):
            Q0 = B0 + 512 * qp
            for i, (ktok, kind) in enumerate(kcs):
                par = (ktok // 128) % 2
                sA, sB = f"ps{2 * par}", f"ps{2 * par + 1}"
                vb = ["vp0", "vp1"][st["vp_i"] % 2]
                st["vp_i"] += 1
                gv = st["vp_guard"].get(vb)
                if gv is not None:
                    p.wait("sync", gv[0], gv[1])
                r = 4 * b + ktok // 1024
                r0 = ktok % 1024
                vld = dma("V0" if vb == "vp0" else "V1",
                          lambda vb=vb: Abf(vb)[:, :],
                          lambda r=r, r0=r0, hp=hp: kvv_ap(r, r0, r0 + 128, hp))
                gS = st["S_guard"].get(par)
                if gS is not None:
                    p.wait("tensor", gS[0], gS[1])
                koff = None
                smm = None
                kofs = st["koff"][ktok]
                op("tensor", lambda e, S, sA=sA, kofs=kofs, hp=hp, qp=qp:
                   e.matmul(ps[sA][:, :],
                            Abf("khp")[0:64, kofs:kofs + 128],
                            Abf("qfm").rearrange(
                                "p (c f) -> p c f", f=1024)[
                                0:64, hp, 512 * qp:512 * (qp + 1)],
                            start=True, stop=True, tile_position=(0, 0)),
                   inc=True, sem="tt")
                smm = op("tensor", lambda e, S, sB=sB, kofs=kofs, hp=hp, qp=qp:
                         e.matmul(ps[sB][:, :],
                                  Abf("khp")[64:128, kofs:kofs + 128],
                                  Abf("qfm").rearrange(
                                      "p (c f) -> p c f", f=1024)[
                                      64:128, hp, 512 * qp:512 * (qp + 1)],
                                  start=True, stop=True,
                                  tile_position=(64, 0)), inc=True, sem="tt")
                aA, aB = f"A{2 * par}", f"A{2 * par + 1}"
                for head, (sX, aX) in enumerate(((sA, aA), (sB, aB))):
                    gA = st["A_guard"].get(aX)
                    if gA is not None:
                        p.wait("scalar", gA[0], gA[1])
                    p.wait("scalar", "tt", smm)
                    if kind == "diag":
                        j = (ktok - Q0) // 128
                        assert 0 <= j < 4
                        p.wait("vector", "tt", smm)
                        if st.get("mskf_free") is not None:
                            p.wait("vector", "ss", st["mskf_free"])
                        mk = op("vector", lambda e, S, sX=sX, j=j:
                                e.tensor_tensor(
                                    out=A("msk_f"), in0=ps[sX][:, :],
                                    in1=Abf("mskA")[:, 512 * j:512 * (j + 1)],
                                    op=ALU.add), inc=True)
                        p.wait("scalar", "vv", mk)
                        ex = op("scalar", lambda e, S, aX=aX: e.activation(
                            out=Abf(aX)[:, :], in_=A("msk_f"), func=AF.Exp,
                            scale=0.125, bias=Asm("bm10")), inc=True)
                        st["mskf_free"] = ex
                    else:
                        ex = op("scalar", lambda e, S, sX=sX, aX=aX:
                                e.activation(
                                    out=Abf(aX)[:, :], in_=ps[sX][:, :],
                                    func=AF.Exp, scale=0.125,
                                    bias=Asm("bm10")), inc=True)
                    st[f"exp_{head}"] = ex
                st["S_guard"][par] = ("ss", st["exp_1"])
                st["last_smm"] = smm
                exA, exB = st["exp_0"], st["exp_1"]
                p.wait("tensor", "ss", exB)
                p.wait("tensor", "V0" if vb == "vp0" else "V1", vld)
                stt = (first and i == 0)
                stp = (last_flags and i == len(kcs) - 1)
                op("tensor", lambda e, S, vb=vb, aA=aA, stt=stt, stp=stp:
                   e.matmul(ps["ps4"][0:64, :], Abf(vb)[:, 0:64],
                            Abf(aA)[:, :], start=stt, stop=stp,
                            tile_position=(0, 0)), inc=True, sem="tt")
                op("tensor", lambda e, S, vb=vb, aB=aB, stt=stt, stp=stp:
                   e.matmul(ps["ps5"][64:128, :], Abf(vb)[:, 64:128],
                            Abf(aB)[:, :], start=stt, stop=stp,
                            tile_position=(0, 64)), inc=True, sem="tt")
                op("tensor", lambda e, S, aA=aA, stt=stt, stp=stp:
                   e.matmul(ps["ps6"][0:1, :], Asm("onescol"),
                            Abf(aA)[:, :], start=stt, stop=stp,
                            tile_position=(0, 0)), inc=True, sem="tt")
                avv = op("tensor", lambda e, S, aB=aB, stt=stt, stp=stp:
                         e.matmul(ps["ps7"][64:65, :], Asm("onescol"),
                                  Abf(aB)[:, :], start=stt, stop=stp,
                                  tile_position=(0, 64)), inc=True, sem="tt")
                st["vp_guard"][vb] = ("tt", avv)
                st["A_guard"][aA] = ("tt", avv)
                st["A_guard"][aB] = ("tt", avv)
            return avv

        def finish_pair(hp, qp, av_done):
            p.wait("scalar", "tt", av_done)
            se1 = op("scalar", lambda e, S: e.copy(
                out=Asm("serow")[0:1, 0:512], in_=ps["ps6"][0:1, :]),
                inc=True)
            se2 = op("scalar", lambda e, S: e.copy(
                out=Asm("serow")[64:65, 0:512], in_=ps["ps7"][64:65, :]),
                inc=True)
            p.wait("vector", "ss", se2)
            op("vector", lambda e, S: e.reciprocal(
                out=Asm("serow")[0:1, 512:1024], in_=Asm("serow")[0:1, 0:512]))
            op("vector", lambda e, S: e.reciprocal(
                out=Asm("serow")[64:65, 512:1024],
                in_=Asm("serow")[64:65, 0:512]))
            p.drain("vector")
            op("vector", lambda e, S: e.tensor_copy(
                out=Abf("selo")[0:1, 0:512], in_=Asm("serow")[0:1, 512:1024]))
            op("vector", lambda e, S: e.tensor_copy(
                out=Abf("selo")[64:65, 0:512],
                in_=Asm("serow")[64:65, 512:1024]))
            p.drain("vector")
            op("vector", lambda e, S: e.tensor_tensor(
                out=Asm("serow")[0:1, 0:512],
                in0=Asm("serow")[0:1, 512:1024], in1=Abf("selo")[0:1, 0:512],
                op=ALU.subtract))
            op("vector", lambda e, S: e.tensor_tensor(
                out=Asm("serow")[64:65, 0:512],
                in0=Asm("serow")[64:65, 512:1024],
                in1=Abf("selo")[64:65, 0:512], op=ALU.subtract))
            p.drain("vector")
            op("vector", lambda e, S: e.tensor_copy(
                out=Abf("selo")[0:1, 512:1024], in_=Asm("serow")[0:1, 0:512]))
            lov = op("vector", lambda e, S: e.tensor_copy(
                out=Abf("selo")[64:65, 512:1024],
                in_=Asm("serow")[64:65, 0:512]), inc=True)
            p.wait("tensor", "vv", lov)
            op("tensor", lambda e, S: e.matmul(
                ps["ps6"][0:64, :], Asm("onesrow")[0:1, :],
                Abf("selo")[0:1, 0:512], start=True, stop=False,
                tile_position=(0, 0)), inc=True, sem="tt")
            op("tensor", lambda e, S: e.matmul(
                ps["ps6"][0:64, :], Asm("onesrow")[0:1, :],
                Abf("selo")[0:1, 512:1024], start=False, stop=True,
                tile_position=(0, 0)), inc=True, sem="tt")
            op("tensor", lambda e, S: e.matmul(
                ps["ps7"][64:128, :], Asm("onesrow")[64:65, :],
                Abf("selo")[64:65, 0:512], start=True, stop=False,
                tile_position=(64, 64)), inc=True, sem="tt")
            bcv2 = op("tensor", lambda e, S: e.matmul(
                ps["ps7"][64:128, :], Asm("onesrow")[64:65, :],
                Abf("selo")[64:65, 512:1024], start=False, stop=True,
                tile_position=(64, 64)), inc=True, sem="tt")
            p.wait("scalar", "tt", bcv2)
            r1v = op("scalar", lambda e, S: e.copy(
                out=A("smrcp")[0:64, :], in_=ps["ps6"][0:64, :]), inc=True)
            r2v = op("scalar", lambda e, S: e.copy(
                out=A("smrcp")[64:128, :], in_=ps["ps7"][64:128, :]),
                inc=True)
            p.wait("vector", "ss", r2v)
            op("vector", lambda e, S, hp=hp, qp=qp: e.tensor_tensor(
                out=Abf("afm").rearrange("p (c f) -> p c f", f=1024)[
                    0:64, hp, 512 * qp:512 * (qp + 1)],
                in0=ps["ps4"][0:64, :], in1=A("smrcp")[0:64, :],
                op=ALU.mult))
            fin = op("vector", lambda e, S, hp=hp, qp=qp: e.tensor_tensor(
                out=Abf("afm").rearrange("p (c f) -> p c f", f=1024)[
                    64:128, hp, 512 * qp:512 * (qp + 1)],
                in0=ps["ps5"][64:128, :], in1=A("smrcp")[64:128, :],
                op=ALU.mult), inc=True)
            p.wait("tensor", "vv", fin)
            return fin

        for hp in range(15):
            S0 = max(0, B0 - 512)
            span_end = B0 + 1024
            st["koff"] = {tok: tok - S0 for tok in range(S0, span_end, 128)}
            kl = khp_load(hp, S0, span_end)
            p.wait("tensor", "Tk", kl)
            for qp in range(2):
                Q0 = B0 + 512 * qp
                kcs = []
                for ktok in range(S0, span_end, 128):
                    if ktok >= Q0 + 512:
                        continue
                    kcs.append((ktok, "diag" if ktok + 127 > Q0 else "full"))
                av = do_pair(hp, qp, kcs, True, True)
                finish_pair(hp, qp, av)
        hp = 15
        for qp in range(2):
            Q0 = B0 + 512 * qp
            hi_all = Q0 + 512
            scs = [(lo, min(lo + 2048, hi_all))
                   for lo in range(0, hi_all, 2048)]
            first = True
            for si, (lo, hi) in enumerate(scs):
                st["koff"] = {tok: tok - lo for tok in range(lo, hi, 128)}
                kl = khp_load(hp, lo, hi)
                p.wait("tensor", "Tk", kl)
                kcs = []
                for ktok in range(lo, hi, 128):
                    kcs.append((ktok,
                                "diag" if ktok + 127 > Q0 else "full"))
                av = do_pair(hp, qp, kcs, first, si == len(scs) - 1)
                first = False
            finish_pair(hp, qp, av)

    if UPTO >= 3:
        p.wait("tensor", "ag", p.val("ag"))
        p.wait("tensor", "Tl", qfm_done)
        p.wait("tensor", "vv", consts_v)
        p.wait("scalar", "Tl", qfm_done)
        p.wait("scalar", "ag", p.val("ag"))
        p.wait("vector", "ag", p.val("ag"))
        for _ln in EVL:
            p.wait("vector", _ln, p.val(_ln))
            p.wait("scalar", _ln, p.val(_ln))
        p.wait("sync", "ag", p.val("ag"))

        sw_engines = ("tensor", "vector", "scalar", "sync")
        snap = dict(p.cnt)
        snap_mw = dict(p.max_waiter)
        snap_gt = dict(p.gated_to)
        collected = {a: {eng: [] for eng in sw_engines} for a in range(8)}
        finals = []
        fin_mw, fin_gt = [], []
        for arm in range(8):
            p.cnt = dict(snap)
            p.max_waiter = dict(snap_mw)
            p.gated_to = dict(snap_gt)
            saved = {eng: p.ops[eng] for eng in sw_engines}
            for eng in sw_engines:
                p.ops[eng] = collected[arm][eng]
            emit_scores_arm(arm)
            for eng in sw_engines:
                p.ops[eng] = saved[eng]
            finals.append(dict(p.cnt))
            fin_mw.append(dict(p.max_waiter))
            fin_gt.append(dict(p.gated_to))
        canon = {}
        for f in finals + [snap]:
            for k, v in f.items():
                canon[k] = max(canon.get(k, 0), v)
        for arm in range(8):
            fin = finals[arm]
            for eng in sw_engines:
                tail = []
                for semn in ESEM[eng]:
                    fv = fin.get(semn, 0)
                    cv = canon.get(semn, 0)
                    tail.append((semn, fv, cv - fv))
                ops_list = collected[arm][eng]
                for semn, fv, pad in tail:
                    if fv > 0:
                        ops_list.append(
                            lambda e, S, n=semn, v=fv: e.wait_ge(S[n], v))
                    if pad > 0:
                        ops_list.append(
                            lambda e, S, n=semn, d=pad: e.sem_inc(S[n], d))
        p.cnt = canon
        p.max_waiter = {k: max(canon.get(k, 0),
                               max((m.get(k, 0) for m in fin_mw), default=0))
                        for k in set(canon) | set(snap_mw)}
        p.gated_to = {}
        for eng in sw_engines:
            def mk_switch(eng):
                arms_ops = [collected[a][eng] for a in range(8)]

                def fn(e, S):
                    pid = e.partition_id()
                    for arm in e.Switch(pid, 8):
                        for f in arms_ops[arm]:
                            f(e, S)
                return fn
            p.on(eng, mk_switch(eng))

        if DEBUG_PHASE >= 3:
            pass

    if UPTO >= 4:
        def afm_lhs(kc, tc):
            return Abf("afm").rearrange("p (c f) -> p c f", f=1024)[
                :, kc, 128 * tc:128 * (tc + 1)]

        ZEV = [("Z0", 0), ("Z1", 512), ("Z2", 1024), ("Z3", 1536)]
        zev_state = {"i": 0, "guard": {}}

        def wo_evac(pc):
            def f(tc, mm_target):
                zlane, off = ZEV[zev_state["i"] % 4]
                zev_state["i"] += 1
                g = zev_state["guard"].get(off)
                if g is not None:
                    p.wait("scalar", zlane, g)
                p.wait("scalar", "mm", mm_target)
                evv = op("scalar", lambda e, S, off=off, tc=tc: e.activation(
                    out=A("z_t")[:, off:off + 512], in_=ps[f"ps{tc}"][:, :],
                    func=AF.Identity), inc=True)
                p.wait("scalar", "ss", evv)
                st2 = dma(zlane,
                          lambda tc=tc, pc=pc: s_z[128 * tc:128 * (tc + 1),
                                                   512 * pc:512 * (pc + 1)],
                          lambda off=off: A("z_t")[:, off:off + 512])
                zev_state["guard"][off] = st2
                return ("ss", evv)
            return f

        pg2 = pg
        pg2 = [("vv", p.val("vv"))] * TC
        for pc in range(4):
            pg2 = mm_group8(afm_lhs,
                            lambda kc, pc=pc:
                            ag1_tile(OFF_O, D, 128 * kc, 512 * pc, 512),
                            DC, wo_evac(pc), psum_guards=pg2,
                            bufs=["wrhsB0", "wrhsB1", "wrhsB2", "wrhsB3"])
        wo_done = {zl: p.val(zl) for zl in ("Z0", "Z1", "Z2", "Z3")}

        if DEBUG_PHASE >= 3:
            for zl, zv in wo_done.items():
                p.wait("gpsimd", zl, zv)
            dma("Gd", lambda: dbg["z"][:, :], lambda: s_z[:, :])

        for nm, src in (("bo_b", bo_in), ("lnw_b", lnw_in), ("lnb_b", lnb_in)):
            t = dma("Gd", lambda nm=nm: A(nm),
                    lambda src=src: src[0:1, :].broadcast_to([128, D]))
        bco = p.val("Gd")
        p.wait("vector", "Gd", bco)
        p.wait("scalar", "Gd", bco)

        for zl, zv in wo_done.items():
            p.wait("sync", zl, zv)
        for tc in range(TC):
            tz = dma("Lt", lambda: A("t_t"),
                     lambda tc=tc: s_z[128 * tc:128 * (tc + 1), :])
            tx = dma("Lx", lambda: A("x_t"),
                     lambda tc=tc: x_in[128 * tc:128 * (tc + 1), :])
            p.wait("vector", "Lt", tz)
            p.wait("vector", "Lx", tx)
            zb = op("vector", lambda e, S: e.tensor_tensor(
                out=A("z_t"), in0=A("t_t"), in1=A("bo_b"), op=ALU.add),
                inc=True)
            p.wait("scalar", "vv", zb)
            ln_stats(lambda: A("z_t"))
            ap2 = op("scalar", lambda e, S: e.activation(
                out=A("t_t"), in_=A("z_t"), func=AF.Identity,
                scale=Asm("ln_r"), bias=Asm("ln_nmr")), inc=True)
            p.wait("vector", "ss", ap2)
            x1v = op("vector", lambda e, S: e.tensor_tensor(
                out=A("z_t"), in0=A("t_t"), in1=A("x_t"), op=ALU.add),
                inc=True)
            p.wait("sync", "vv", x1v)
            sx1 = dma("Ls", lambda tc=tc: s_x1[128 * tc:128 * (tc + 1), :],
                      lambda: A("z_t"))
            p.wait("scalar", "vv", x1v)
            ln_stats(lambda: A("z_t"))
            h0v = op("scalar", lambda e, S: e.activation(
                out=Abf("h0st"), in_=A("z_t"), func=AF.Identity,
                scale=Asm("ln_r"), bias=Asm("ln_nmr")), inc=True)
            p.wait("sync", "ss", h0v)
            dma("Ls", lambda tc=tc: s_h0[128 * tc:128 * (tc + 1), :],
                lambda: Abf("h0st"))
            p.wait("scalar", "ss", h0v)
            ln_stats(lambda: Abf("h0st"))
            t2v = op("scalar", lambda e, S: e.activation(
                out=A("t_t"), in_=Abf("h0st"), func=AF.Identity,
                scale=Asm("ln_r"), bias=Asm("ln_nmr")), inc=True)
            p.wait("vector", "ss", t2v)
            p.wait("vector", "Ls")
            op("vector", lambda e, S: e.tensor_tensor(
                out=A("z_t"), in0=A("t_t"), in1=A("lnw_b"), op=ALU.mult))
            p.drain("vector")
            hv = op("vector", lambda e, S: e.tensor_tensor(
                out=Abf("hst"), in0=A("z_t"), in1=A("lnb_b"), op=ALU.add),
                inc=True)
            p.wait("vector", "vv", hv)
            for half in range(2):
                qf = quant_chain(
                    lambda half=half: Abf("hst")[:, 1024 * half:1024 * (half + 1)],
                    lambda half=half: Abf("hqst")[:, 1024 * half:1024 * (half + 1)],
                    1024, vbuf="q2_v", rbuf="q2_r")
            p.wait("sync", "vv", qf)
            dma("Ls", lambda tc=tc: s_hq[128 * tc:128 * (tc + 1), :],
                lambda: Abf("hqst"))
            p.wait("sync", "vv", qf)
        hq_stored = p.val("Ls")

        if DEBUG_PHASE >= 4:
            p.wait("gpsimd", "Ls", hq_stored)
            dma("Gd", lambda: dbg["x1"][:, :], lambda: s_x1[:, :])
            dma("Gd", lambda: dbg["hq"][:, :], lambda: s_hq[:, :])

    if UPTO >= 5:
        p.wait("scalar", "Ls", hq_stored)
        for dc in range(DC):
            dmaT("Tl",
                 lambda dc=dc: Abf("hqfm").rearrange(
                     "p (c f) -> p c f", f=1024)[:, dc, :],
                 lambda dc=dc: s_hq[:, 128 * dc:128 * (dc + 1)])
        hqfm_done = p.val("Tl")

        def hq_lhs(kc, tc):
            return Abf("hqfm").rearrange("p (c f) -> p c f", f=1024)[
                :, kc, 128 * tc:128 * (tc + 1)]

        p.wait("gpsimd", "vv", p.val("vv"))
        p.wait("gpsimd", "ss", p.val("ss"))
        dma("Gd", lambda: A("b1full"),
            lambda: b1_in[0:1, :].broadcast_to([128, DFF]))
        tb2 = dma("Gd", lambda: A("b2full"),
                  lambda: b2_in[0:1, :].broadcast_to([128, D]))

        w1st = {"i": 0, "g_y1p": [None] * 6, "g_y1g": [None] * 6,
                "g_y1h": [None] * 6, "g_gqs": [None] * 6,
                "grp_st": [], "stash": []}

        def w1_flush():
            yvs = []
            for (tc, sl, evv, jc) in w1st["stash"]:
                p.wait("vector", "ss", evv)
                if w1st["g_y1g"][sl] is not None:
                    p.wait("vector", "ss", w1st["g_y1g"][sl])
                yv = op("vector", lambda e, S, sl=sl, jc=jc:
                        e.tensor_tensor(
                            out=Abf(f"y1g{sl}")[:, :], in0=A(f"y1p{sl}"),
                            in1=A("b1full")[:, 512 * jc:512 * (jc + 1)],
                            op=ALU.add), inc=True)
                w1st["g_y1p"][sl] = yv
                yvs.append(yv)
            for (tc, sl, evv, jc), yv in zip(w1st["stash"], yvs):
                el = EVL[w1st["i"] & 3]
                w1st["i"] += 1
                p.wait("scalar", "vv", yv)
                if w1st["g_y1h"][sl] is not None:
                    p.wait("scalar", "vv", w1st["g_y1h"][sl])
                gv = op("scalar", lambda e, S, sl=sl: e.activation(
                    out=Abf(f"y1h{sl}")[:, :], in_=Abf(f"y1g{sl}")[:, :],
                    func=AF.Gelu_apprx_tanh), inc=True)
                w1st["g_y1g"][sl] = gv
                p.wait("vector", "ss", gv)
                if w1st["g_gqs"][sl] is not None:
                    gl, gvv = w1st["g_gqs"][sl]
                    p.wait("vector", gl, p.val(gl))
                qv = quant_chain(lambda sl=sl: Abf(f"y1h{sl}")[:, :],
                                 lambda sl=sl: Abf(f"gqs{sl}")[:, :],
                                 512, vbuf="q2_v", rbuf="q2_r")
                w1st["g_y1h"][sl] = qv
                p.wait("sync", "vv", qv)
                stv = dma(el,
                          lambda tc=tc, jc=jc: s_gq[128 * tc:128 * (tc + 1),
                                                    512 * jc:512 * (jc + 1)],
                          lambda sl=sl: Abf(f"gqs{sl}")[:, :])
                w1st["g_gqs"][sl] = (el, stv)
                w1st["grp_st"].append((el, stv))
            w1st["stash"] = []

        def w1_evac(jc):
            def f(tc, mm_target):
                sl = (w1st["i"] + len(w1st["stash"])) % 6
                p.wait("scalar", "mm", mm_target)
                if w1st["g_y1p"][sl] is not None:
                    p.wait("scalar", "vv", w1st["g_y1p"][sl])
                evv = op("scalar", lambda e, S, sl=sl, tc=tc: e.activation(
                    out=A(f"y1p{sl}"), in_=ps[f"ps{tc}"][:, :],
                    func=AF.Identity), inc=True)
                w1st["stash"].append((tc, sl, evv, jc))
                if len(w1st["stash"]) == 6 or tc == TC - 1:
                    w1_flush()
                return ("ss", evv)
            return f

        pg3 = [("ss", p.val("ss"))] * TC
        p.wait("tensor", "Tl", hqfm_done)
        p.wait("vector", "Gd", tb2)
        grp_done = []
        for jc in range(16):
            w1st["grp_st"] = []
            pg3 = mm_group8(hq_lhs,
                            lambda kc, jc=jc:
                            ag1_tile(OFF_W1, DFF, 128 * kc, 512 * jc, 512),
                            DC, w1_evac(jc), psum_guards=pg3)
            lanemax = {}
            for el, sv in w1st["grp_st"]:
                lanemax[el] = max(lanemax.get(el, 0), sv)
            grp_done.append(lanemax)
        w1_mm_done = p.val("mm")

        if DEBUG_PHASE >= 5:
            for el, sv in grp_done[-1].items():
                p.wait("gpsimd", el, sv)
            for lm in grp_done:
                for el, sv in lm.items():
                    p.wait("gpsimd", el, sv)
            dma("Gd", lambda: dbg["gq"][:, :], lambda: s_gq[:, :])

        w2st = {"guard": [None, None]}

        def w2_pre(kc):
            sl = kc & 1
            g = w2st["guard"][sl]
            if g is not None:
                p.wait("scalar", g[0], g[1])
            for el, sv in grp_done[kc // 4].items():
                p.wait("scalar", el, sv)
            t = dmaT("Tl",
                     lambda sl=sl: Abf(f"glhs{sl}")[:, :],
                     lambda kc=kc: s_gq[:, 128 * kc:128 * (kc + 1)])
            if kc == 63:
                w2st["guard"][sl] = ("mm", p.val("mm") + TC)
            else:
                w2st["guard"][sl] = ("mq", p.val("mq") + TC)
            p.wait("tensor", "Tl", t)

        def gq_lhs(kc, tc):
            return Abf(f"glhs{kc & 1}")[:, 128 * tc:128 * (tc + 1)]

        h0_state = {"g": [None, None], "fg": [None, None], "pend": {}}

        def w2_load_h0(oc, tc):
            sl = tc & 1
            g = h0_state["g"][sl]
            if g is not None:
                p.wait("sync", "vv", g)
            th = dma("Lh", lambda sl=sl: Abf(f"h0t{sl}")[:, :],
                     lambda tc=tc, oc=oc: s_h0[128 * tc:128 * (tc + 1),
                                               512 * oc:512 * (oc + 1)])
            h0_state["pend"][(oc, tc)] = th

        def w2_evac(oc):
            def f(tc, mm_target):
                sl = tc & 1
                fsl = w1st["i"] & 1
                w1st["i"] += 1
                if (oc, tc) not in h0_state["pend"]:
                    w2_load_h0(oc, tc)
                th = h0_state["pend"].pop((oc, tc))
                nxt = (oc, tc + 1) if tc < TC - 1 else                     ((oc + 1, 0) if oc < 3 else None)
                if nxt is not None:
                    w2_load_h0(*nxt)
                p.wait("vector", "mm", mm_target)
                p.wait("vector", "Lh", p.val("Lh"))
                if h0_state.get("pfv") is not None:
                    p.wait("vector", "vv", h0_state["pfv"])
                yv = op("vector", lambda e, S, tc=tc, oc=oc: e.tensor_tensor(
                    out=A("y2_f"), in0=ps[f"ps{tc}"][:, :],
                    in1=A("b2full")[:, 512 * oc:512 * (oc + 1)],
                    op=ALU.add), inc=True)
                p.drain("vector")
                if h0_state["fg"][fsl] is not None:
                    p.wait("vector", "Lf", p.val("Lf"))
                fv = op("vector", lambda e, S, sl=sl, fsl=fsl:
                        e.tensor_tensor(
                            out=Abf(f"fot{fsl}")[:, :], in0=A("y2_f"),
                            in1=Abf(f"h0t{sl}")[:, :], op=ALU.add), inc=True)
                h0_state["g"][sl] = fv
                h0_state["pfv"] = fv
                p.wait("sync", "vv", fv)
                sv = dma("Lf",
                         lambda tc=tc, oc=oc: s_fo[128 * tc:128 * (tc + 1),
                                                   512 * oc:512 * (oc + 1)],
                         lambda fsl=fsl: Abf(f"fot{fsl}")[:, :])
                h0_state["fg"][fsl] = sv
                return ("vv", yv)
            return f

        pg4 = pg3
        for oc in range(4):
            pg4 = mm_group8(gq_lhs,
                            lambda kc, oc=oc:
                            ag1_tile(OFF_W2, D, 128 * kc, 512 * oc, 512, 1024),
                            64, w2_evac(oc), psum_guards=pg4,
                            bufs=["wrhsD0", "wrhsD1"], pre_kc=w2_pre)
        fo_stored = p.val("Lf")

        p.wait("sync", "Lf", fo_stored)
        p.wait("scalar", "mm", p.val("mm"))
        fprev = None
        for tc in range(TC):
            if fprev is not None:
                p.wait("sync", "vv", fprev)
            tf = dma("Lg", lambda: Abf("fo_ld"),
                     lambda tc=tc: s_fo[128 * tc:128 * (tc + 1), :])
            tx1 = dma("Lt", lambda: A("t_t"),
                      lambda tc=tc: s_x1[128 * tc:128 * (tc + 1), :])
            p.wait("scalar", "Lg", tf)
            p.wait("scalar", "Lt", tx1)
            p.wait("vector", "Lt", tx1)
            ln_stats(lambda: Abf("fo_ld"))
            apf = op("scalar", lambda e, S: e.activation(
                out=A("z_t"), in_=Abf("fo_ld"), func=AF.Identity,
                scale=Asm("ln_r"), bias=Asm("ln_nmr")), inc=True)
            p.wait("vector", "ss", apf)
            if fprev is not None:
                p.wait("vector", "Ls", syp)
            yv = op("vector", lambda e, S: e.tensor_tensor(
                out=A("x_t"), in0=A("z_t"), in1=A("t_t"), op=ALU.add),
                inc=True)
            p.wait("sync", "vv", yv)
            syp = dma("Ls", lambda tc=tc: y_out[128 * tc:128 * (tc + 1), :],
                      lambda: A("x_t"))
            fprev = yv
    else:
        pv = None
        for tc in range(TC):
            if pv is not None:
                p.wait("sync", "Ls", pv)
            tl0 = dma("Lx", lambda: A("x_t"),
                      lambda tc=tc: x_in[128 * tc:128 * (tc + 1), :])
            p.wait("sync", "Lx", tl0)
            pv = dma("Ls", lambda tc=tc: y_out[128 * tc:128 * (tc + 1), :],
                     lambda: A("x_t"))

    for _ln in SYNC_LANES + SCAL_LANES + GP_LANES:
        if p.val(_ln) > 0:
            p.wait("gpsimd", _ln, p.val(_ln))

    import contextlib
    with contextlib.ExitStack() as stack:
        sb["AF32"] = stack.enter_context(
            nc.sbuf_tensor("AF32", [128, NF32], F32))
        sb["ABF"] = stack.enter_context(
            nc.sbuf_tensor("ABF", [128, NBF], BF16))
        for name, (shape, dt) in sm.items():
            sb[name] = stack.enter_context(nc.sbuf_tensor(name, shape, dt))
        for i in range(8):
            ps[f"ps{i}"] = stack.enter_context(
                nc.psum_tensor(f"ps{i}", [128, 512], F32))
        S = {n: stack.enter_context(nc.semaphore(n)) for n in SEM_NAMES}
        p.emit(nc, S)
    return nc


def kernel(L0_attn_hidden_states, L0_attn_Wq, L0_attn_Wk, L0_attn_Wv,
           L0_attn_Wo, L0_attn_bo, L0_ffn_ln_w, L0_ffn_ln_b, L0_ffn_W1,
           L0_ffn_b1, L0_ffn_W2, L0_ffn_b2):
    x = np.ascontiguousarray(np.asarray(L0_attn_hidden_states, np.float32))
    Wq = np.asarray(L0_attn_Wq, np.float32)
    Wk = np.asarray(L0_attn_Wk, np.float32)
    Wv = np.asarray(L0_attn_Wv, np.float32)
    Wo = np.asarray(L0_attn_Wo, np.float32)
    W1 = np.asarray(L0_ffn_W1, np.float32)
    W2 = np.asarray(L0_ffn_W2, np.float32)
    in_maps = []
    for s in range(NP):
        b, c = divmod(s, 4)
        in_maps.append({
            "x": np.ascontiguousarray(x[b, 1024 * c:1024 * (c + 1), :]),
            "pos": np.arange(1024 * c, 1024 * (c + 1),
                             dtype=np.float32)[None, :],
            "wq": np.ascontiguousarray(Wq[:, 256 * s:256 * (s + 1)]),
            "wk": np.ascontiguousarray(Wk[:, 256 * s:256 * (s + 1)]),
            "wv": np.ascontiguousarray(Wv[:, 256 * s:256 * (s + 1)]),
            "wo": np.ascontiguousarray(Wo[:, 256 * s:256 * (s + 1)]),
            "w1": np.ascontiguousarray(W1[:, 256 * s:256 * (s + 1)]),
            "w2": np.ascontiguousarray(W2[:, 1024 * s:1024 * (s + 1)]),
            "bo": np.asarray(L0_attn_bo, np.float32)[None, :],
            "lnw": np.asarray(L0_ffn_ln_w, np.float32)[None, :],
            "lnb": np.asarray(L0_ffn_ln_b, np.float32)[None, :],
            "b1": np.asarray(L0_ffn_b1, np.float32)[None, :],
            "b2": np.asarray(L0_ffn_b2, np.float32)[None, :],
        })
    nc = build_nc()
    res = run_bass_kernel_spmd(nc, in_maps, core_ids=list(range(NP)))
    out = np.empty((B, L, D), np.float32)
    for s in range(NP):
        b, c = divmod(s, 4)
        out[b, 1024 * c:1024 * (c + 1), :] = res.results[s]["y"]
    return out


# version 05
# speedup vs baseline: 1.0524x; 1.0136x over previous
import math
import numpy as np
import ml_dtypes
import concourse.bass as bass
import concourse.mybir as mybir
from concourse.bass_utils import run_bass_kernel_spmd

F32 = mybir.dt.float32
BF16 = mybir.dt.bfloat16
I32 = mybir.dt.int32
AF = mybir.ActivationFunctionType
ALU = mybir.AluOpType
AX = mybir.AxisListType

NP = 8
B, L, D = 2, 4096, 2048
T = 1024
DFF = 8192
TC = T // 128
DC = D // 128
NHP = 16
MAGIC = float(np.float32(1.5 * 2 ** 23))
TWO_PI = 2.0 * math.pi

OFF_Q = 0
OFF_K = OFF_Q + 256 * D
OFF_V = OFF_K + 256 * D
OFF_O = OFF_V + 256 * D
OFF_W1 = OFF_O + 256 * D
OFF_W2 = OFF_W1 + 256 * DFF
R1 = OFF_W2 + 1024 * D

OFF_KV_V = T * D
R2 = 2 * T * D

UPTO = 99
DEBUG_PHASE = 0
SKIP = set()

CSEM = {"vector": "vv", "scalar": "ss", "gpsimd": "gc", "tensor": "tt"}
SYNC_LANES = ["Lx", "Lt", "Lp", "Ls", "Lr", "W0", "W1", "W2", "W3",
              "E0", "E1", "E2", "E3", "V0", "V1", "Lb", "Lh", "Lf", "Lg"]
SCAL_LANES = ["Tl", "Tw", "Tk", "Z0", "Z1", "Z2", "Z3"]
GP_LANES = ["Gd"]
LANE_ENG = {**{n: "sync" for n in SYNC_LANES},
            **{n: "scalar" for n in SCAL_LANES},
            **{n: "gpsimd" for n in GP_LANES}}
EVL = ["E0", "E1", "E2", "E3"]
WL = ["W0", "W1", "W2", "W3"]
ESEM = {"sync": SYNC_LANES, "scalar": SCAL_LANES + ["ss"], "vector": ["vv"],
        "tensor": ["tt", "mm", "mq"], "gpsimd": GP_LANES + ["gc"]}
SEM_NAMES = (SYNC_LANES + SCAL_LANES + GP_LANES +
             ["ag", "vv", "ss", "tt", "mm", "mq", "gc"])


def _cody_waite_consts():
    def chop(x):
        v = np.frombuffer(np.float32(x).tobytes(), dtype=np.uint32)[0]
        v = np.uint32(v & np.uint32(0xFFFFF000))
        return float(np.frombuffer(v.tobytes(), dtype=np.float32)[0])
    c1 = chop(TWO_PI)
    c2 = chop(TWO_PI - c1)
    c3 = float(np.float32(TWO_PI - c1 - c2))
    return c1, c2, c3


class Prog:
    ENGS = ("sync", "scalar", "vector", "tensor", "gpsimd")

    def __init__(self):
        self.ops = {e: [] for e in self.ENGS}
        self.cnt = {}
        self.max_waiter = {}
        self.gated_to = {}

    def on(self, eng, fn):
        self.ops[eng].append(fn)

    def inc(self, name, by=1):
        self.cnt[name] = self.cnt.get(name, 0) + by
        return self.cnt[name]

    def val(self, name):
        return self.cnt.get(name, 0)

    def wait(self, eng, name, val=None):
        v = self.cnt.get(name, 0) if val is None else val
        if name in LANE_ENG:
            self.max_waiter[name] = max(self.max_waiter.get(name, 0), v)
        self.on(eng, lambda e, S, n=name, v=v: e.wait_ge(S[n], v))

    def drain(self, eng):
        self.on(eng, lambda e, S: e.drain())

    def emit(self, nc, sems):
        with nc.Block() as block:
            for eng_name in self.ENGS:
                deco = getattr(block, eng_name)
                ops = self.ops[eng_name]

                def make(ops):
                    def body(e):
                        for fn in ops:
                            fn(e, sems)
                    return body
                deco(make(ops))


def build_nc():
    nc = bass.Bass("TRN2", target_bir_lowering=False, debug=False)

    x_in = nc.dram_tensor("x", [T, D], F32, kind="ExternalInput").ap()
    pos_in = nc.dram_tensor("pos", [1, T], F32, kind="ExternalInput").ap()
    w_in = {m: nc.dram_tensor(f"w{m}", [D, 256], F32, kind="ExternalInput").ap()
            for m in "qkvo"}
    w1_in = nc.dram_tensor("w1", [DFF, 256], F32, kind="ExternalInput").ap()
    w2_in = nc.dram_tensor("w2", [D, 1024], F32, kind="ExternalInput").ap()
    bo_in = nc.dram_tensor("bo", [1, D], F32, kind="ExternalInput").ap()
    lnw_in = nc.dram_tensor("lnw", [1, D], F32, kind="ExternalInput").ap()
    lnb_in = nc.dram_tensor("lnb", [1, D], F32, kind="ExternalInput").ap()
    b1_in = nc.dram_tensor("b1", [1, DFF], F32, kind="ExternalInput").ap()
    b2_in = nc.dram_tensor("b2", [1, D], F32, kind="ExternalInput").ap()
    y_out = nc.dram_tensor("y", [T, D], F32, kind="ExternalOutput").ap()

    ag1_in = nc.dram_tensor("ag1_in", [1, R1], BF16, kind="Internal").ap()
    ag1_out = nc.dram_tensor("ag1_out", [1, NP * R1], BF16, kind="Internal",
                             addr_space="Shared").ap()
    kv_in = nc.dram_tensor("kv_in", [1, R2], BF16, kind="Internal").ap()
    kv_out = nc.dram_tensor("kv_out", [1, NP * R2], BF16, kind="Internal",
                            addr_space="Shared").ap()
    s_w = {m: nc.dram_tensor(f"s_w{m}", [D, 256], BF16, kind="Internal").ap()
           for m in "qkvo"}
    s_w1 = nc.dram_tensor("s_w1", [DFF, 256], BF16, kind="Internal").ap()
    s_w2 = nc.dram_tensor("s_w2", [D, 1024], BF16, kind="Internal").ap()
    s_ln1x = nc.dram_tensor("s_ln1x", [T, D], BF16, kind="Internal").ap()
    s_qhat = nc.dram_tensor("s_qhat", [T, D], BF16, kind="Internal").ap()
    s_z = nc.dram_tensor("s_z", [T, D], F32, kind="Internal").ap()
    s_x1 = nc.dram_tensor("s_x1", [T, D], F32, kind="Internal").ap()
    s_h0 = nc.dram_tensor("s_h0", [T, D], BF16, kind="Internal").ap()
    s_hq = nc.dram_tensor("s_hq", [T, D], BF16, kind="Internal").ap()
    s_g = nc.dram_tensor("s_g", [T, DFF], BF16, kind="Internal").ap()
    s_gq = nc.dram_tensor("s_gq", [T, DFF], BF16, kind="Internal").ap()
    s_fo = nc.dram_tensor("s_fo", [T, D], BF16, kind="Internal").ap()

    dbg = {}
    if DEBUG_PHASE >= 1:
        dbg["ag1"] = nc.dram_tensor("d_ag1", [1, R1], BF16,
                                    kind="ExternalOutput").ap()
        dbg["ln1x"] = nc.dram_tensor("d_ln1x", [T, D], BF16,
                                     kind="ExternalOutput").ap()
    if DEBUG_PHASE >= 2:
        dbg["qhat"] = nc.dram_tensor("d_qhat", [T, D], BF16,
                                     kind="ExternalOutput").ap()
        dbg["kv"] = nc.dram_tensor("d_kv", [1, R2], BF16,
                                   kind="ExternalOutput").ap()
    if DEBUG_PHASE >= 3:
        dbg["z"] = nc.dram_tensor("d_z", [T, D], F32,
                                  kind="ExternalOutput").ap()
    if DEBUG_PHASE >= 4:
        dbg["x1"] = nc.dram_tensor("d_x1", [T, D], F32,
                                   kind="ExternalOutput").ap()
        dbg["hq"] = nc.dram_tensor("d_hq", [T, D], BF16,
                                   kind="ExternalOutput").ap()
    if DEBUG_PHASE >= 5:
        dbg["gq"] = nc.dram_tensor("d_gq", [T, DFF], BF16,
                                   kind="ExternalOutput").ap()

    p = Prog()

    NF32 = 15360
    NBF = 68096

    fz = {}

    def F(name, start, n):
        assert start + n <= NF32, (name, start + n)
        fz[name] = (start, n)
    F("prep_f", 0, 2048)
    F("q_v", 2048, 2048)
    F("q_r", 4096, 2048)
    F("rope_t1", 8192, 512)
    F("rope_t2", 8704, 512)
    F("cosT", 9216, 512)
    F("sinS", 9728, 512)
    F("rp_th", 10240, 64)
    F("rp_u", 10304, 64)
    F("rp_k", 10368, 64)
    F("rp_r", 10432, 64)
    F("rp_si", 10496, 64)
    F("rp_ci", 10560, 64)
    F("rp_sin", 10624, 64)
    F("invfB", 10688, 64)
    F("msk_f", 10752, 512)
    F("x_t", 11264, 2048)
    F("q2_v", 13312, 1024)
    F("q2_r", 14336, 1024)
    F("smrcp", 0, 512)
    F("z_t", 512, 2048)
    F("t_t", 2560, 2048)
    F("bo_b", 4608, 2048)
    F("lnw_b", 6656, 2048)
    F("lnb_b", 8704, 2048)
    F("bias_b", 10752, 512)
    F("y2_f", 0, 512)
    F("b1full", 512, 8192)
    F("b2full", 8704, 2048)
    F("y1p0", 10752, 512)
    F("y1p1", 11264, 512)
    F("y1p2", 11776, 512)
    F("y1p3", 12288, 512)
    F("y1p4", 12800, 512)
    F("y1p5", 0, 512)

    bz = {}

    def Bf(name, start, n):
        assert start + n <= NBF, (name, start + n)
        bz[name] = (start, n)
    Bf("lfm", 0, 16384)
    Bf("wtr", 16384, 8192)
    Bf("prep_q", 24576, 2048)
    Bf("lnx_bf", 24576, 2048)
    Bf("ev0", 26624, 512)
    Bf("ev1", 27136, 512)
    Bf("ev2", 27648, 512)
    Bf("ev3", 28160, 512)
    Bf("wrhs0", 28672, 512)
    Bf("wrhs1", 29184, 512)
    Bf("wrhs2", 29696, 512)
    Bf("wrhs3", 30208, 512)
    Bf("qfm", 0, 16384)
    Bf("afm", 16384, 16384)
    Bf("khp", 32768, 2048)
    Bf("A0", 34816, 512)
    Bf("A1", 35328, 512)
    Bf("A2", 35840, 512)
    Bf("A3", 36352, 512)
    Bf("vp0", 36864, 128)
    Bf("vp1", 36992, 128)
    Bf("selo", 37120, 1024)
    Bf("mskA", 38144, 2048)
    Bf("wrhsB0", 40960, 512)
    Bf("wrhsB1", 41472, 512)
    Bf("wrhsB2", 41984, 512)
    Bf("wrhsB3", 42496, 512)
    Bf("h0st", 16384, 2048)
    Bf("hst", 18432, 2048)
    Bf("hqst", 20480, 2048)
    Bf("hqfm", 0, 16384)
    for _i in range(6):
        Bf(f"y1g{_i}", 35840 + 512 * _i, 512)
        Bf(f"y1h{_i}", 38912 + 512 * _i, 512)
        Bf(f"gqs{_i}", 41984 + 512 * _i, 512)
    Bf("glhs0", 18432, 1024)
    Bf("glhs1", 19456, 1024)
    Bf("wrhsD0", 65536, 512)
    Bf("wrhsD1", 66048, 512)
    Bf("h0t0", 33792, 512)
    Bf("h0t1", 34304, 512)
    Bf("fot0", 66560, 512)
    Bf("fot1", 67072, 512)
    Bf("fo_ld", 65536, 2048)

    sm = {}

    def SMT(name, shape, dt):
        sm[name] = (shape, dt)
    SMT("ln_s1", [128, 1], F32)
    SMT("ln_s1b", [128, 1], F32)
    SMT("ln_s2", [128, 1], F32)
    SMT("ln_s2b", [128, 1], F32)
    SMT("ln_mu", [128, 1], F32)
    SMT("ln_var", [128, 1], F32)
    SMT("ln_sd", [128, 1], F32)
    SMT("ln_r", [128, 1], F32)
    SMT("ln_nmr", [128, 1], F32)
    SMT("iv_i", [128, 64], I32)
    SMT("rp_pos", [128, 1], F32)
    SMT("bm10", [128, 1], F32)
    SMT("m512", [128, 1], F32)
    SMT("onescol", [128, 1], BF16)
    SMT("onesrow", [128, 64], BF16)
    SMT("serow", [128, 1024], F32)
    SMT("wq_amax", [128, 64], F32)
    SMT("wq_amax2", [128, 64], F32)
    SMT("wq_bits", [128, 64], I32)
    SMT("wq_m", [128, 64], I32)
    SMT("wq_mf", [128, 64], F32)
    SMT("wq_mf2", [128, 64], F32)
    SMT("wq_sb", [128, 64], I32)
    SMT("wq_scale", [128, 64], F32)
    SMT("wq_inv", [128, 64], F32)
    SMT("msk_ii", [128, 512], I32)
    SMT("msk_s", [128, 512], F32)

    sb = {}
    ps = {}

    def A(name):
        s0, n = fz[name]
        return sb["AF32"][:, s0:s0 + n]

    def Abf(name):
        s0, n = bz[name]
        return sb["ABF"][:, s0:s0 + n]

    def Asm(name):
        return sb[name][:, :]

    def _lane_gate(lane):
        eng = LANE_ENG[lane]
        mw = p.max_waiter.get(lane, 0)
        if mw > p.gated_to.get(lane, 0):
            p.on(eng, lambda e, S, n=lane, v=mw: e.wait_ge(S[n], v))
            p.gated_to[lane] = mw

    def dma(lane, dst_fn, src_fn):
        eng = LANE_ENG[lane]
        _lane_gate(lane)
        v = p.inc(lane, 16)
        p.on(eng, lambda e, S, d=dst_fn, s=src_fn, sem=lane:
             e.dma_start(out=d(), in_=s()).then_inc(S[sem], 16))
        return v

    def dmaT(lane, dst_fn, src_fn):
        eng = LANE_ENG[lane]
        _lane_gate(lane)
        v = p.inc(lane, 16)
        p.on(eng, lambda e, S, d=dst_fn, s=src_fn, sem=lane:
             e.dma_start_transpose(out=d(), in_=s()).then_inc(S[sem], 16))
        return v

    def op(eng, mk, inc=False, sem=None):
        if inc:
            sem = sem or CSEM[eng]
            v = p.inc(sem)
            p.on(eng, lambda e, S, mk=mk, sem=sem:
                 mk(e, S).then_inc(S[sem], 1))
            return v
        p.on(eng, lambda e, S, mk=mk: mk(e, S))
        return None

    def ag1_tile(off_m, rowlen, d0, col0, ncol, shard_rows=256):
        r = d0 // shard_rows
        i = d0 % shard_rows
        base = r * R1 + off_m + i * rowlen
        flat = ag1_out[0, :]
        return flat[base:base + 128 * rowlen].rearrange(
            "(p f) -> p f", f=rowlen)[:, col0:col0 + ncol]

    def kvk_ap(rank, r0, r1, hp):
        base = rank * R2
        flat = kv_out[0, :]
        return flat[base:base + T * D].rearrange(
            "(t d) -> t d", d=D)[r0:r1, 128 * hp:128 * (hp + 1)]

    def kvv_ap(rank, r0, r1, hp):
        base = rank * R2 + OFF_KV_V
        flat = kv_out[0, :]
        return flat[base:base + T * D].rearrange(
            "(t d) -> t d", d=D)[r0:r1, 128 * hp:128 * (hp + 1)]

    def quant_chain(src_fn, dst_fn, nfree, vbuf="q_v", rbuf="q_r"):
        nblk = nfree // 32
        assert nblk <= 64

        def v3(ap):
            return ap.rearrange("p (nb b) -> p nb b", b=32)
        op("vector", lambda e, S: e.tensor_reduce(
            out=Asm("wq_amax")[:, :nblk], in_=v3(src_fn()), axis=AX.X,
            op=ALU.max, apply_absolute_value=True))
        p.drain("vector")
        op("vector", lambda e, S: e.tensor_scalar(
            out=Asm("wq_amax2")[:, :nblk], in0=Asm("wq_amax")[:, :nblk],
            scalar1=1e-30, scalar2=None, op0=ALU.max))
        p.drain("vector")
        op("vector", lambda e, S: e.tensor_copy(
            out=Asm("wq_bits")[:, :nblk],
            in_=Asm("wq_amax2")[:, :nblk].bitcast(I32)))
        p.drain("vector")
        op("vector", lambda e, S: e.tensor_scalar(
            out=Asm("wq_m")[:, :nblk], in0=Asm("wq_bits")[:, :nblk],
            scalar1=23, scalar2=None, op0=ALU.logical_shift_right))
        p.drain("vector")
        op("vector", lambda e, S: e.tensor_copy(
            out=Asm("wq_mf")[:, :nblk], in_=Asm("wq_m")[:, :nblk]))
        p.drain("vector")
        op("vector", lambda e, S: e.tensor_scalar(
            out=Asm("wq_mf2")[:, :nblk], in0=Asm("wq_mf")[:, :nblk],
            scalar1=float(1 << 23), scalar2=-float(6 << 23),
            op0=ALU.mult, op1=ALU.add))
        p.drain("vector")
        op("vector", lambda e, S: e.tensor_copy(
            out=Asm("wq_sb")[:, :nblk], in_=Asm("wq_mf2")[:, :nblk]))
        p.drain("vector")
        op("vector", lambda e, S: e.tensor_copy(
            out=Asm("wq_scale")[:, :nblk],
            in_=Asm("wq_sb")[:, :nblk].bitcast(F32)))
        p.drain("vector")
        op("vector", lambda e, S: e.reciprocal(
            out=Asm("wq_inv")[:, :nblk], in_=Asm("wq_scale")[:, :nblk]))
        p.drain("vector")
        op("vector", lambda e, S: e.tensor_tensor(
            out=v3(A(vbuf)[:, :nfree]), in0=v3(src_fn()),
            in1=Asm("wq_inv")[:, :nblk].unsqueeze(2).broadcast_to(
                [128, nblk, 32]), op=ALU.mult))
        p.drain("vector")
        op("vector", lambda e, S: e.tensor_scalar(
            out=A(rbuf)[:, :nfree], in0=A(vbuf)[:, :nfree],
            scalar1=MAGIC, scalar2=MAGIC, op0=ALU.add, op1=ALU.subtract))
        p.drain("vector")
        op("vector", lambda e, S: e.tensor_scalar(
            out=A(vbuf)[:, :nfree], in0=A(rbuf)[:, :nfree],
            scalar1=127.0, scalar2=-128.0, op0=ALU.min, op1=ALU.max))
        p.drain("vector")
        fin = op("vector", lambda e, S: e.tensor_tensor(
            out=v3(dst_fn()), in0=v3(A(vbuf)[:, :nfree]),
            in1=Asm("wq_scale")[:, :nblk].unsqueeze(2).broadcast_to(
                [128, nblk, 32]), op=ALU.mult), inc=True)
        return fin

    def ln_stats(src_fn):
        op("scalar", lambda e, S: e.activation(
            out=A("q2_v"), in_=src_fn()[:, 0:1024], func=AF.Square,
            accum_out=Asm("ln_s2")))
        h1 = op("scalar", lambda e, S: e.activation(
            out=A("q2_r"), in_=src_fn()[:, 1024:2048], func=AF.Square,
            accum_out=Asm("ln_s2b")), inc=True)
        p.wait("scalar", "ss", h1)
        op("scalar", lambda e, S: e.activation(
            out=A("q2_v"), in_=src_fn()[:, 0:1024], func=AF.Identity,
            accum_out=Asm("ln_s1")))
        a = op("scalar", lambda e, S: e.activation(
            out=A("q2_r"), in_=src_fn()[:, 1024:2048], func=AF.Identity,
            accum_out=Asm("ln_s1b")), inc=True)
        p.wait("vector", "ss", a)
        op("vector", lambda e, S: e.tensor_tensor(
            out=Asm("ln_s1"), in0=Asm("ln_s1"), in1=Asm("ln_s1b"),
            op=ALU.add))
        op("vector", lambda e, S: e.tensor_tensor(
            out=Asm("ln_s2"), in0=Asm("ln_s2"), in1=Asm("ln_s2b"),
            op=ALU.add))
        p.drain("vector")
        op("vector", lambda e, S: e.tensor_scalar(
            out=Asm("ln_mu"), in0=Asm("ln_s1"), scalar1=1.0 / D,
            scalar2=None, op0=ALU.mult))
        p.drain("vector")
        op("vector", lambda e, S: e.tensor_tensor(
            out=Asm("ln_var"), in0=Asm("ln_mu"), in1=Asm("ln_mu"),
            op=ALU.mult))
        p.drain("vector")
        op("vector", lambda e, S: e.scalar_tensor_tensor(
            out=Asm("ln_var"), in0=Asm("ln_s2"), scalar=1.0 / D,
            in1=Asm("ln_var"), op0=ALU.mult, op1=ALU.subtract))
        p.drain("vector")
        b = op("vector", lambda e, S: e.tensor_scalar(
            out=Asm("ln_sd"), in0=Asm("ln_var"), scalar1=1e-5,
            scalar2=None, op0=ALU.add), inc=True)
        p.wait("scalar", "vv", b)
        c = op("scalar", lambda e, S: e.sqrt(
            out=Asm("ln_sd"), in_=Asm("ln_sd")), inc=True)
        p.wait("vector", "ss", c)
        op("vector", lambda e, S: e.reciprocal(
            out=Asm("ln_r"), in_=Asm("ln_sd")))
        p.drain("vector")
        d = op("vector", lambda e, S: e.scalar_tensor_tensor(
            out=Asm("ln_nmr"), in0=Asm("ln_mu"), scalar=-1.0,
            in1=Asm("ln_r"), op0=ALU.mult, op1=ALU.mult), inc=True)
        p.wait("scalar", "vv", d)
        return d


    prev_apply = None
    prev_lnstore = None
    for tc in range(TC):
        if prev_apply is not None:
            p.wait("sync", "ss", prev_apply)
        t = dma("Lx", lambda: A("x_t"),
                lambda tc=tc: x_in[128 * tc:128 * (tc + 1), :])
        p.wait("scalar", "Lx", t)
        ln_stats(lambda: A("x_t"))
        if prev_lnstore is not None:
            p.wait("scalar", "Ls", prev_lnstore)
        ap_ = op("scalar", lambda e, S: e.activation(
            out=Abf("lnx_bf"), in_=A("x_t"), func=AF.Identity,
            scale=Asm("ln_r"), bias=Asm("ln_nmr")), inc=True)
        prev_apply = ap_
        p.wait("sync", "ss", ap_)
        prev_lnstore = dma(
            "Ls", lambda tc=tc: s_ln1x[128 * tc:128 * (tc + 1), :],
            lambda: Abf("lnx_bf"))
    ln1_last_store = p.val("Ls")
    p.wait("scalar", "Ls", ln1_last_store)
    for dc in range(DC):
        dmaT("Tl",
             lambda dc=dc: Abf("lfm").rearrange(
                 "p (c f) -> p c f", f=1024)[:, dc, :],
             lambda dc=dc: s_ln1x[:, 128 * dc:128 * (dc + 1)])
    lfm_done = p.val("Tl")

    prep_state = {"fin": None, "fin_quant": True, "store": None}

    def prep_rows(src, dst_dram, nrow_chunk, nchunks, ncol, quant):
        prev_fin = prep_state["fin"]
        prev_quant = prep_state["fin_quant"]
        prev_store = prep_state["store"]
        for ch in range(nchunks):
            r0 = nrow_chunk * ch
            nsub = nrow_chunk // 128
            nfree = nsub * ncol
            if prev_fin is not None:
                p.wait("sync", "vv" if prev_quant else "ss", prev_fin)
            prev_quant = quant
            t0 = dma("Lp",
                     lambda nsub=nsub, ncol=ncol: A("prep_f").rearrange(
                         "p (c f) -> p c f", f=ncol)[:, 0:nsub, :],
                     lambda src=src, r0=r0, nrow_chunk=nrow_chunk:
                     src[r0:r0 + nrow_chunk, :].rearrange(
                         "(c p) f -> p c f", p=128))
            eng2 = "vector" if quant else "scalar"
            p.wait(eng2, "Lp", t0)
            if prev_store is not None:
                p.wait(eng2, "Ls", prev_store)
            else:
                p.wait(eng2, "Ls", p.val("Ls"))
            if quant:
                fin = quant_chain(
                    lambda nfree=nfree: A("prep_f")[:, :nfree],
                    lambda nfree=nfree: Abf("prep_q")[:, :nfree], nfree)
            else:
                fin = op("scalar", lambda e, S, nfree=nfree: e.copy(
                    out=Abf("prep_q")[:, :nfree],
                    in_=A("prep_f")[:, :nfree]), inc=True)
            p.wait("sync", "vv" if quant else "ss", fin)
            prev_store = dma(
                "Ls",
                lambda dst_dram=dst_dram, r0=r0, nrow_chunk=nrow_chunk:
                dst_dram[r0:r0 + nrow_chunk, :].rearrange(
                    "(c p) f -> p c f", p=128),
                lambda nsub=nsub, ncol=ncol: Abf("prep_q").rearrange(
                    "p (c f) -> p c f", f=ncol)[:, 0:nsub, :])
            prev_fin = fin
        prep_state["fin"] = prev_fin
        prep_state["fin_quant"] = quant
        prep_state["store"] = prev_store
        return prev_store

    prep_rows(w1_in, s_w1, 1024, 8, 256, True)
    prep_rows(w2_in, s_w2, 256, 8, 1024, True)
    for m in "qkvo":
        prep_rows(w_in[m], s_w[m], 1024, 2, 256, False)
    wprep_stored = p.val("Ls")
    p.wait("scalar", "Ls", wprep_stored)

    def prep_transpose(src_dram, ncols_src, off_dst, nrows_src):
        for i in range(ncols_src // 128):
            tW = dmaT("Tw",
                      lambda nrows_src=nrows_src: Abf("wtr")[:, :nrows_src],
                      lambda src_dram=src_dram, i=i:
                      src_dram[:, 128 * i:128 * (i + 1)])
            p.wait("scalar", "Tw", tW)
            stW = dma("Tw",
                      lambda off_dst=off_dst, i=i, nrows_src=nrows_src:
                      ag1_in[0, off_dst + 128 * i * nrows_src:
                             off_dst + 128 * (i + 1) * nrows_src].rearrange(
                          "(p f) -> p f", p=128),
                      lambda nrows_src=nrows_src: Abf("wtr")[:, :nrows_src])
            p.wait("scalar", "Tw", stW)

    prep_transpose(s_w1, 256, OFF_W1, DFF)
    prep_transpose(s_w2, 1024, OFF_W2, D)
    for m, off in (("q", OFF_Q), ("k", OFF_K), ("v", OFF_V), ("o", OFF_O)):
        prep_transpose(s_w[m], 256, off, D)
    wprep_done = p.val("Tw")

    p.wait("gpsimd", "Tw", wprep_done)
    ag1_done = p.inc("ag")
    p.on("gpsimd", lambda e, S: e.collective_compute(
        "AllGather", ALU.bypass, replica_groups=[list(range(NP))],
        ins=[ag1_in[:, :]], outs=[ag1_out[:, :]]).then_inc(S["ag"], 1))

    if DEBUG_PHASE >= 1:
        p.wait("gpsimd", "ag", ag1_done)
        dma("Gd", lambda: dbg["ag1"][:, :], lambda: ag1_in[:, :])
        dma("Gd", lambda: dbg["ln1x"][:, :], lambda: s_ln1x[:, :])

    c1, c2, c3 = _cody_waite_consts()
    do_rope = "rope" not in SKIP
    iv = op("gpsimd", lambda e, S: e.iota(
        out=Asm("iv_i")[:, :], pattern=[[0, 2], [1, 32]], base=0,
        channel_multiplier=0), inc=True)
    mi = None
    mv = 0
    for j in ([] if "masks" in SKIP else range(4)):
        mi = op("gpsimd", lambda e, S, j=j: e.iota(
            out=Asm("msk_ii")[:, :], pattern=[[1, 512]], base=-128 * j,
            channel_multiplier=-1), inc=True)
        p.wait("scalar", "gc", mi)
        mc = op("scalar", lambda e, S: e.copy(
            out=Asm("msk_s"), in_=Asm("msk_ii")), inc=True)
        p.wait("vector", "ss", mc)
        mv = op("vector", lambda e, S, j=j: e.tensor_scalar(
            out=Abf("mskA")[:, 512 * j:512 * (j + 1)], in0=Asm("msk_s"),
            scalar1=0.0, scalar2=-1e30, op0=ALU.is_lt, op1=ALU.mult),
            inc=True)
        if j < 3:
            p.wait("gpsimd", "vv", mv)
    masks_done = mv
    p.wait("scalar", "gc", iv)
    s1v = op("scalar", lambda e, S: e.activation(
        out=A("invfB"), in_=Asm("iv_i")[:, :], func=AF.Exp,
        scale=-math.log(10000.0) / 32.0), inc=True)
    p.wait("vector", "ss", s1v)
    op("vector", lambda e, S: e.memset(Asm("bm10"), -10.0))
    op("vector", lambda e, S: e.memset(Asm("m512"), 512.0))
    op("vector", lambda e, S: e.memset(Asm("onescol"), 1.0))
    consts_v = op("vector", lambda e, S: e.memset(Asm("onesrow"), 1.0),
                  inc=True)
    for tc in (range(TC) if do_rope else []):
        ldp = dma("Lr", lambda: Asm("rp_pos"),
                  lambda tc=tc: pos_in[0, 128 * tc:128 * (tc + 1)].rearrange(
                      "(p one) -> p one", one=1))
        p.wait("vector", "Lr", ldp)
        thv = op("vector", lambda e, S: e.tensor_scalar(
            out=A("rp_th"), in0=A("invfB"), scalar1=Asm("rp_pos"),
            scalar2=None, op0=ALU.mult), inc=True)
        p.wait("sync", "vv", thv)
        p.drain("vector")
        op("vector", lambda e, S: e.tensor_scalar(
            out=A("rp_u"), in0=A("rp_th"), scalar1=1.0 / TWO_PI,
            scalar2=None, op0=ALU.mult))
        p.drain("vector")
        op("vector", lambda e, S: e.tensor_scalar(
            out=A("rp_k"), in0=A("rp_u"), scalar1=MAGIC, scalar2=MAGIC,
            op0=ALU.add, op1=ALU.subtract))
        p.drain("vector")
        op("vector", lambda e, S: e.scalar_tensor_tensor(
            out=A("rp_r"), in0=A("rp_k"), scalar=-c1, in1=A("rp_th"),
            op0=ALU.mult, op1=ALU.add))
        p.drain("vector")
        op("vector", lambda e, S: e.scalar_tensor_tensor(
            out=A("rp_u"), in0=A("rp_k"), scalar=-c2, in1=A("rp_r"),
            op0=ALU.mult, op1=ALU.add))
        p.drain("vector")
        op("vector", lambda e, S: e.scalar_tensor_tensor(
            out=A("rp_r"), in0=A("rp_k"), scalar=-c3, in1=A("rp_u"),
            op0=ALU.mult, op1=ALU.add))
        p.drain("vector")
        op("vector", lambda e, S: e.tensor_scalar(
            out=A("rp_si"), in0=A("rp_r"), scalar1=math.pi, scalar2=None,
            op0=ALU.is_gt))
        p.drain("vector")
        op("vector", lambda e, S: e.tensor_scalar(
            out=A("rp_u"), in0=A("rp_r"), scalar1=-math.pi, scalar2=None,
            op0=ALU.is_lt))
        p.drain("vector")
        op("vector", lambda e, S: e.tensor_tensor(
            out=A("rp_u"), in0=A("rp_u"), in1=A("rp_si"), op=ALU.subtract))
        p.drain("vector")
        op("vector", lambda e, S: e.scalar_tensor_tensor(
            out=A("rp_si"), in0=A("rp_u"), scalar=TWO_PI, in1=A("rp_r"),
            op0=ALU.mult, op1=ALU.add))
        p.drain("vector")
        op("vector", lambda e, S: e.tensor_scalar(
            out=A("rp_ci"), in0=A("rp_r"), scalar1=math.pi / 2, scalar2=None,
            op0=ALU.add))
        p.drain("vector")
        op("vector", lambda e, S: e.tensor_scalar(
            out=A("rp_u"), in0=A("rp_ci"), scalar1=math.pi, scalar2=None,
            op0=ALU.is_gt))
        p.drain("vector")
        rw = op("vector", lambda e, S: e.scalar_tensor_tensor(
            out=A("rp_ci"), in0=A("rp_u"), scalar=-TWO_PI, in1=A("rp_ci"),
            op0=ALU.mult, op1=ALU.add), inc=True)
        p.wait("scalar", "vv", rw)
        op("scalar", lambda e, S: e.activation(
            out=A("rp_sin"), in_=A("rp_si"), func=AF.Sin))
        sc = op("scalar", lambda e, S, tc=tc: e.activation(
            out=A("cosT")[:, 64 * tc:64 * (tc + 1)], in_=A("rp_ci"),
            func=AF.Sin), inc=True)
        p.wait("vector", "ss", sc)
        op("vector", lambda e, S, tc=tc: e.tensor_scalar(
            out=A("sinS")[:, 64 * tc:64 * tc + 32], in0=A("rp_sin")[:, 0:32],
            scalar1=-1.0, scalar2=None, op0=ALU.mult))
        op("vector", lambda e, S, tc=tc: e.tensor_copy(
            out=A("sinS")[:, 64 * tc + 32:64 * (tc + 1)],
            in_=A("rp_sin")[:, 32:64]), inc=True)
    rope_done = p.val("vv")

    WBUFS = ["wrhs0", "wrhs1", "wrhs2", "wrhs3"]
    WLANE = {"wrhs0": "W0", "wrhs1": "W1", "wrhs2": "W2", "wrhs3": "W3",
             "wrhsB0": "W0", "wrhsB1": "W1", "wrhsB2": "W2", "wrhsB3": "W3",
             "wrhsD0": "W0", "wrhsD1": "W1"}
    wb_state = {"i": 0, "guard": {}}

    def stream_w(tile_fn, bufs=None):
        bufs = bufs if bufs is not None else WBUFS
        buf = bufs[wb_state["i"] % len(bufs)]
        wb_state["i"] += 1
        lane = WLANE[buf]
        g = wb_state["guard"].get(buf)
        if g is not None:
            p.wait("sync", g[0], g[1])
        t = dma(lane, lambda buf=buf: Abf(buf)[:, :512], tile_fn)
        return buf, t, lane

    def mm_group8(lhs_fn, tile_fn, nK, evac_fn, psum_guards=None,
                  bufs=None, pre_kc=None):
        for kc in range(nK):
            buf, t, lane = stream_w(lambda kc=kc: tile_fn(kc), bufs=bufs)
            if pre_kc is not None:
                pre_kc(kc)
            p.wait("tensor", lane, t)
            for tc in range(TC):
                if kc == 0 and psum_guards and psum_guards[tc] is not None:
                    p.wait("tensor", psum_guards[tc][0], psum_guards[tc][1])
                last = (kc == nK - 1)
                if last:
                    p.inc("mm")
                    p.on("tensor", lambda e, S, kc=kc, tc=tc, buf=buf:
                         e.matmul(ps[f"ps{tc}"][:, :], lhs_fn(kc, tc),
                                  Abf(buf)[:, :512], start=(kc == 0),
                                  stop=True).then_inc(S["mm"], 1))
                else:
                    p.inc("mq")
                    p.on("tensor", lambda e, S, kc=kc, tc=tc, buf=buf:
                         e.matmul(ps[f"ps{tc}"][:, :], lhs_fn(kc, tc),
                                  Abf(buf)[:, :512], start=(kc == 0),
                                  stop=False).then_inc(S["mq"], 1))
            wb_state["guard"][buf] = \
                ("mm", p.val("mm")) if kc == nK - 1 else ("mq", p.val("mq"))
        base = p.val("mm") - TC
        frees = []
        for tc in range(TC):
            frees.append(evac_fn(tc, base + tc + 1))
        return frees

    def lfm_lhs(kc, tc):
        return Abf("lfm").rearrange("p (c f) -> p c f", f=1024)[
            :, kc, 128 * tc:128 * (tc + 1)]

    EVB = ["ev0", "ev1", "ev2", "ev3"]
    EVLANE = {"ev0": "E0", "ev1": "E1", "ev2": "E2", "ev3": "E3"}
    ev_state = {"i": 0, "guard": {}}

    def qkv_evac(kind, pc):
        def f(tc, mm_target):
            buf = EVB[ev_state["i"] % 4]
            ev_state["i"] += 1
            weng = "scalar" if kind == "v" else "vector"
            g = ev_state["guard"].get(buf)
            if g is not None:
                p.wait(weng, EVLANE[buf], g)
            if kind == "v":
                p.wait("scalar", "mm", mm_target)
                evv = op("scalar", lambda e, S, buf=buf, tc=tc: e.activation(
                    out=Abf(buf)[:, :512], in_=ps[f"ps{tc}"][:, :],
                    func=AF.Identity), inc=True)
                p.wait("sync", "ss", evv)
                free = ("ss", evv)
            else:
                p.wait("vector", "mm", mm_target)

                def psv(tc2):
                    return ps[f"ps{tc2}"][:, :].rearrange(
                        "p (h d) -> p h d", d=64)

                def cosb(tc2):
                    return A("cosT")[:, 64 * tc2:64 * (tc2 + 1)].unsqueeze(1)\
                        .broadcast_to([128, 8, 64])

                def sinb(tc2, half):
                    s0 = 64 * tc2 + 32 * half
                    return A("sinS")[:, s0:s0 + 32].unsqueeze(1)\
                        .broadcast_to([128, 8, 32])
                op("vector", lambda e, S, tc=tc: e.tensor_tensor(
                    out=A("rope_t1").rearrange("p (h d) -> p h d", d=64),
                    in0=psv(tc), in1=cosb(tc), op=ALU.mult))
                op("vector", lambda e, S, tc=tc: e.tensor_tensor(
                    out=A("rope_t2").rearrange(
                        "p (h d) -> p h d", d=64)[:, :, 0:32],
                    in0=psv(tc)[:, :, 32:64], in1=sinb(tc, 0), op=ALU.mult))
                pf = op("vector", lambda e, S, tc=tc: e.tensor_tensor(
                    out=A("rope_t2").rearrange(
                        "p (h d) -> p h d", d=64)[:, :, 32:64],
                    in0=psv(tc)[:, :, 0:32], in1=sinb(tc, 1), op=ALU.mult),
                    inc=True)
                p.drain("vector")
                evv = op("vector", lambda e, S, buf=buf: e.tensor_tensor(
                    out=Abf(buf)[:, :512], in0=A("rope_t1"),
                    in1=A("rope_t2"), op=ALU.add), inc=True)
                p.drain("vector")
                p.wait("sync", "vv", evv)
                free = ("vv", pf)
            if kind == "q":
                dst = lambda tc=tc, pc=pc: s_qhat[
                    128 * tc:128 * (tc + 1), 512 * pc:512 * (pc + 1)]
            elif kind == "k":
                dst = lambda tc=tc, pc=pc: kv_in[0, 0:OFF_KV_V].rearrange(
                    "(t d) -> t d", d=D)[128 * tc:128 * (tc + 1),
                                         512 * pc:512 * (pc + 1)]
            else:
                dst = lambda tc=tc, pc=pc: kv_in[0, OFF_KV_V:].rearrange(
                    "(t d) -> t d", d=D)[128 * tc:128 * (tc + 1),
                                         512 * pc:512 * (pc + 1)]
            g = dma(EVLANE[buf], dst, lambda buf=buf: Abf(buf)[:, :512])
            ev_state["guard"][buf] = g
            return free
        return f

    p.wait("tensor", "ag", ag1_done)
    p.wait("tensor", "Tl", lfm_done)
    p.wait("sync", "ag", ag1_done)
    p.wait("vector", "vv", rope_done)
    pg = None
    for kind, off_m in ([] if "qkv" in SKIP else
                        (("q", OFF_Q), ("k", OFF_K), ("v", OFF_V))):
        for pc in range(4):
            pg = mm_group8(lfm_lhs,
                           lambda kc, off_m=off_m, pc=pc:
                           ag1_tile(off_m, D, 128 * kc, 512 * pc, 512),
                           DC, qkv_evac(kind, pc), psum_guards=pg)
    for _ln in EVL:
        p.wait("gpsimd", _ln, p.val(_ln))
    ag2_done = p.inc("ag")
    p.on("gpsimd", lambda e, S: e.collective_compute(
        "AllGather", ALU.bypass, replica_groups=[list(range(NP))],
        ins=[kv_in[:, :]], outs=[kv_out[:, :]]).then_inc(S["ag"], 1))

    if DEBUG_PHASE >= 2:
        p.wait("gpsimd", "ag", ag2_done)
        dma("Gd", lambda: dbg["qhat"][:, :], lambda: s_qhat[:, :])
        dma("Gd", lambda: dbg["kv"][:, :], lambda: kv_in[:, :])

    p.wait("scalar", "mm", p.val("mm"))
    for hp in range(NHP):
        dmaT("Tl",
             lambda hp=hp: Abf("qfm").rearrange(
                 "p (c f) -> p c f", f=1024)[:, hp, :],
             lambda hp=hp: s_qhat[:, 128 * hp:128 * (hp + 1)])
    qfm_done = p.val("Tl")

    def emit_scores_arm(s):
        b, c = divmod(s, 4)
        B0 = 1024 * c
        st = {"vp_i": 0, "vp_guard": {}, "A_guard": {}, "S_guard": {}}

        def khp_load(hp, lo, hi):
            if st.get("last_smm") is not None:
                p.wait("scalar", "tt", st["last_smm"])
            off = 0
            v = None
            t0 = lo
            while t0 < hi:
                r = 4 * b + t0 // 1024
                r0 = t0 % 1024
                r1 = min(1024, r0 + (hi - t0))
                n = r1 - r0
                v = dmaT("Tk",
                         lambda off=off, n=n: Abf("khp")[:, off:off + n],
                         lambda r=r, r0=r0, r1=r1, hp=hp: kvk_ap(r, r0, r1, hp))
                off += n
                t0 += n
            return v

        def do_pair(hp, qp, kcs, first, last_flags):
            Q0 = B0 + 512 * qp
            for i, (ktok, kind) in enumerate(kcs):
                par = (ktok // 128) % 2
                sA, sB = f"ps{2 * par}", f"ps{2 * par + 1}"
                vb = ["vp0", "vp1"][st["vp_i"] % 2]
                st["vp_i"] += 1
                gv = st["vp_guard"].get(vb)
                if gv is not None:
                    p.wait("sync", gv[0], gv[1])
                r = 4 * b + ktok // 1024
                r0 = ktok % 1024
                vld = dma("V0" if vb == "vp0" else "V1",
                          lambda vb=vb: Abf(vb)[:, :],
                          lambda r=r, r0=r0, hp=hp: kvv_ap(r, r0, r0 + 128, hp))
                gS = st["S_guard"].get(par)
                if gS is not None:
                    p.wait("tensor", gS[0], gS[1])
                koff = None
                smm = None
                kofs = st["koff"][ktok]
                op("tensor", lambda e, S, sA=sA, kofs=kofs, hp=hp, qp=qp:
                   e.matmul(ps[sA][:, :],
                            Abf("khp")[0:64, kofs:kofs + 128],
                            Abf("qfm").rearrange(
                                "p (c f) -> p c f", f=1024)[
                                0:64, hp, 512 * qp:512 * (qp + 1)],
                            start=True, stop=True, tile_position=(0, 0)),
                   inc=True, sem="tt")
                smm = op("tensor", lambda e, S, sB=sB, kofs=kofs, hp=hp, qp=qp:
                         e.matmul(ps[sB][:, :],
                                  Abf("khp")[64:128, kofs:kofs + 128],
                                  Abf("qfm").rearrange(
                                      "p (c f) -> p c f", f=1024)[
                                      64:128, hp, 512 * qp:512 * (qp + 1)],
                                  start=True, stop=True,
                                  tile_position=(64, 0)), inc=True, sem="tt")
                aA, aB = f"A{2 * par}", f"A{2 * par + 1}"
                for head, (sX, aX) in enumerate(((sA, aA), (sB, aB))):
                    gA = st["A_guard"].get(aX)
                    if gA is not None:
                        p.wait("scalar", gA[0], gA[1])
                    p.wait("scalar", "tt", smm)
                    if kind == "diag":
                        j = (ktok - Q0) // 128
                        assert 0 <= j < 4
                        p.wait("vector", "tt", smm)
                        if st.get("mskf_free") is not None:
                            p.wait("vector", "ss", st["mskf_free"])
                        mk = op("vector", lambda e, S, sX=sX, j=j:
                                e.tensor_tensor(
                                    out=A("msk_f"), in0=ps[sX][:, :],
                                    in1=Abf("mskA")[:, 512 * j:512 * (j + 1)],
                                    op=ALU.add), inc=True)
                        p.wait("scalar", "vv", mk)
                        ex = op("scalar", lambda e, S, aX=aX: e.activation(
                            out=Abf(aX)[:, :], in_=A("msk_f"), func=AF.Exp,
                            scale=0.125, bias=Asm("bm10")), inc=True)
                        st["mskf_free"] = ex
                    else:
                        ex = op("scalar", lambda e, S, sX=sX, aX=aX:
                                e.activation(
                                    out=Abf(aX)[:, :], in_=ps[sX][:, :],
                                    func=AF.Exp, scale=0.125,
                                    bias=Asm("bm10")), inc=True)
                    st[f"exp_{head}"] = ex
                st["S_guard"][par] = ("ss", st["exp_1"])
                st["last_smm"] = smm
                exA, exB = st["exp_0"], st["exp_1"]
                p.wait("tensor", "ss", exB)
                p.wait("tensor", "V0" if vb == "vp0" else "V1", vld)
                stt = (first and i == 0)
                stp = (last_flags and i == len(kcs) - 1)
                op("tensor", lambda e, S, vb=vb, aA=aA, stt=stt, stp=stp:
                   e.matmul(ps["ps4"][0:64, :], Abf(vb)[:, 0:64],
                            Abf(aA)[:, :], start=stt, stop=stp,
                            tile_position=(0, 0)), inc=True, sem="tt")
                op("tensor", lambda e, S, vb=vb, aB=aB, stt=stt, stp=stp:
                   e.matmul(ps["ps5"][64:128, :], Abf(vb)[:, 64:128],
                            Abf(aB)[:, :], start=stt, stop=stp,
                            tile_position=(0, 64)), inc=True, sem="tt")
                op("tensor", lambda e, S, aA=aA, stt=stt, stp=stp:
                   e.matmul(ps["ps6"][0:1, :], Asm("onescol"),
                            Abf(aA)[:, :], start=stt, stop=stp,
                            tile_position=(0, 0)), inc=True, sem="tt")
                avv = op("tensor", lambda e, S, aB=aB, stt=stt, stp=stp:
                         e.matmul(ps["ps7"][64:65, :], Asm("onescol"),
                                  Abf(aB)[:, :], start=stt, stop=stp,
                                  tile_position=(0, 64)), inc=True, sem="tt")
                st["vp_guard"][vb] = ("tt", avv)
                st["A_guard"][aA] = ("tt", avv)
                st["A_guard"][aB] = ("tt", avv)
            return avv

        def finish_pair(hp, qp, av_done):
            p.wait("scalar", "tt", av_done)
            se1 = op("scalar", lambda e, S: e.copy(
                out=Asm("serow")[0:1, 0:512], in_=ps["ps6"][0:1, :]),
                inc=True)
            se2 = op("scalar", lambda e, S: e.copy(
                out=Asm("serow")[64:65, 0:512], in_=ps["ps7"][64:65, :]),
                inc=True)
            p.wait("vector", "ss", se2)
            op("vector", lambda e, S: e.reciprocal(
                out=Asm("serow")[0:1, 512:1024], in_=Asm("serow")[0:1, 0:512]))
            op("vector", lambda e, S: e.reciprocal(
                out=Asm("serow")[64:65, 512:1024],
                in_=Asm("serow")[64:65, 0:512]))
            p.drain("vector")
            op("vector", lambda e, S: e.tensor_copy(
                out=Abf("selo")[0:1, 0:512], in_=Asm("serow")[0:1, 512:1024]))
            op("vector", lambda e, S: e.tensor_copy(
                out=Abf("selo")[64:65, 0:512],
                in_=Asm("serow")[64:65, 512:1024]))
            p.drain("vector")
            op("vector", lambda e, S: e.tensor_tensor(
                out=Asm("serow")[0:1, 0:512],
                in0=Asm("serow")[0:1, 512:1024], in1=Abf("selo")[0:1, 0:512],
                op=ALU.subtract))
            op("vector", lambda e, S: e.tensor_tensor(
                out=Asm("serow")[64:65, 0:512],
                in0=Asm("serow")[64:65, 512:1024],
                in1=Abf("selo")[64:65, 0:512], op=ALU.subtract))
            p.drain("vector")
            op("vector", lambda e, S: e.tensor_copy(
                out=Abf("selo")[0:1, 512:1024], in_=Asm("serow")[0:1, 0:512]))
            lov = op("vector", lambda e, S: e.tensor_copy(
                out=Abf("selo")[64:65, 512:1024],
                in_=Asm("serow")[64:65, 0:512]), inc=True)
            p.wait("tensor", "vv", lov)
            op("tensor", lambda e, S: e.matmul(
                ps["ps6"][0:64, :], Asm("onesrow")[0:1, :],
                Abf("selo")[0:1, 0:512], start=True, stop=False,
                tile_position=(0, 0)), inc=True, sem="tt")
            op("tensor", lambda e, S: e.matmul(
                ps["ps6"][0:64, :], Asm("onesrow")[0:1, :],
                Abf("selo")[0:1, 512:1024], start=False, stop=True,
                tile_position=(0, 0)), inc=True, sem="tt")
            op("tensor", lambda e, S: e.matmul(
                ps["ps7"][64:128, :], Asm("onesrow")[64:65, :],
                Abf("selo")[64:65, 0:512], start=True, stop=False,
                tile_position=(64, 64)), inc=True, sem="tt")
            bcv2 = op("tensor", lambda e, S: e.matmul(
                ps["ps7"][64:128, :], Asm("onesrow")[64:65, :],
                Abf("selo")[64:65, 512:1024], start=False, stop=True,
                tile_position=(64, 64)), inc=True, sem="tt")
            p.wait("scalar", "tt", bcv2)
            r1v = op("scalar", lambda e, S: e.copy(
                out=A("smrcp")[0:64, :], in_=ps["ps6"][0:64, :]), inc=True)
            r2v = op("scalar", lambda e, S: e.copy(
                out=A("smrcp")[64:128, :], in_=ps["ps7"][64:128, :]),
                inc=True)
            p.wait("vector", "ss", r2v)
            op("vector", lambda e, S, hp=hp, qp=qp: e.tensor_tensor(
                out=Abf("afm").rearrange("p (c f) -> p c f", f=1024)[
                    0:64, hp, 512 * qp:512 * (qp + 1)],
                in0=ps["ps4"][0:64, :], in1=A("smrcp")[0:64, :],
                op=ALU.mult))
            fin = op("vector", lambda e, S, hp=hp, qp=qp: e.tensor_tensor(
                out=Abf("afm").rearrange("p (c f) -> p c f", f=1024)[
                    64:128, hp, 512 * qp:512 * (qp + 1)],
                in0=ps["ps5"][64:128, :], in1=A("smrcp")[64:128, :],
                op=ALU.mult), inc=True)
            p.wait("tensor", "vv", fin)
            return fin

        for hp in range(15):
            S0 = max(0, B0 - 512)
            span_end = B0 + 1024
            st["koff"] = {tok: tok - S0 for tok in range(S0, span_end, 128)}
            kl = khp_load(hp, S0, span_end)
            p.wait("tensor", "Tk", kl)
            for qp in range(2):
                Q0 = B0 + 512 * qp
                kcs = []
                for ktok in range(S0, span_end, 128):
                    if ktok >= Q0 + 512:
                        continue
                    kcs.append((ktok, "diag" if ktok + 127 > Q0 else "full"))
                av = do_pair(hp, qp, kcs, True, True)
                finish_pair(hp, qp, av)
        hp = 15
        for qp in range(2):
            Q0 = B0 + 512 * qp
            hi_all = Q0 + 512
            scs = [(lo, min(lo + 2048, hi_all))
                   for lo in range(0, hi_all, 2048)]
            first = True
            for si, (lo, hi) in enumerate(scs):
                st["koff"] = {tok: tok - lo for tok in range(lo, hi, 128)}
                kl = khp_load(hp, lo, hi)
                p.wait("tensor", "Tk", kl)
                kcs = []
                for ktok in range(lo, hi, 128):
                    kcs.append((ktok,
                                "diag" if ktok + 127 > Q0 else "full"))
                av = do_pair(hp, qp, kcs, first, si == len(scs) - 1)
                first = False
            finish_pair(hp, qp, av)

    if UPTO >= 3:
        p.wait("tensor", "ag", p.val("ag"))
        p.wait("tensor", "Tl", qfm_done)
        p.wait("tensor", "vv", consts_v)
        p.wait("scalar", "Tl", qfm_done)
        p.wait("scalar", "ag", p.val("ag"))
        p.wait("vector", "ag", p.val("ag"))
        for _ln in EVL:
            p.wait("vector", _ln, p.val(_ln))
            p.wait("scalar", _ln, p.val(_ln))
        p.wait("sync", "ag", p.val("ag"))

        sw_engines = ("tensor", "vector", "scalar", "sync")
        snap = dict(p.cnt)
        snap_mw = dict(p.max_waiter)
        snap_gt = dict(p.gated_to)
        collected = {a: {eng: [] for eng in sw_engines} for a in range(8)}
        finals = []
        fin_mw, fin_gt = [], []
        for arm in range(8):
            p.cnt = dict(snap)
            p.max_waiter = dict(snap_mw)
            p.gated_to = dict(snap_gt)
            saved = {eng: p.ops[eng] for eng in sw_engines}
            for eng in sw_engines:
                p.ops[eng] = collected[arm][eng]
            emit_scores_arm(arm)
            for eng in sw_engines:
                p.ops[eng] = saved[eng]
            finals.append(dict(p.cnt))
            fin_mw.append(dict(p.max_waiter))
            fin_gt.append(dict(p.gated_to))
        canon = {}
        for f in finals + [snap]:
            for k, v in f.items():
                canon[k] = max(canon.get(k, 0), v)
        for arm in range(8):
            fin = finals[arm]
            for eng in sw_engines:
                tail = []
                for semn in ESEM[eng]:
                    fv = fin.get(semn, 0)
                    cv = canon.get(semn, 0)
                    tail.append((semn, fv, cv - fv))
                ops_list = collected[arm][eng]
                for semn, fv, pad in tail:
                    if fv > 0:
                        ops_list.append(
                            lambda e, S, n=semn, v=fv: e.wait_ge(S[n], v))
                    if pad > 0:
                        ops_list.append(
                            lambda e, S, n=semn, d=pad: e.sem_inc(S[n], d))
        p.cnt = canon
        p.max_waiter = {k: max(canon.get(k, 0),
                               max((m.get(k, 0) for m in fin_mw), default=0))
                        for k in set(canon) | set(snap_mw)}
        p.gated_to = {}
        for eng in sw_engines:
            def mk_switch(eng):
                arms_ops = [collected[a][eng] for a in range(8)]

                def fn(e, S):
                    pid = e.partition_id()
                    for arm in e.Switch(pid, 8):
                        for f in arms_ops[arm]:
                            f(e, S)
                return fn
            p.on(eng, mk_switch(eng))

        if DEBUG_PHASE >= 3:
            pass

    if UPTO >= 4:
        def afm_lhs(kc, tc):
            return Abf("afm").rearrange("p (c f) -> p c f", f=1024)[
                :, kc, 128 * tc:128 * (tc + 1)]

        ZEV = [("Z0", 0), ("Z1", 512), ("Z2", 1024), ("Z3", 1536)]
        zev_state = {"i": 0, "guard": {}}

        def wo_evac(pc):
            def f(tc, mm_target):
                zlane, off = ZEV[zev_state["i"] % 4]
                zev_state["i"] += 1
                g = zev_state["guard"].get(off)
                if g is not None:
                    p.wait("scalar", zlane, g)
                p.wait("scalar", "mm", mm_target)
                evv = op("scalar", lambda e, S, off=off, tc=tc: e.activation(
                    out=A("z_t")[:, off:off + 512], in_=ps[f"ps{tc}"][:, :],
                    func=AF.Identity), inc=True)
                p.wait("scalar", "ss", evv)
                st2 = dma(zlane,
                          lambda tc=tc, pc=pc: s_z[128 * tc:128 * (tc + 1),
                                                   512 * pc:512 * (pc + 1)],
                          lambda off=off: A("z_t")[:, off:off + 512])
                zev_state["guard"][off] = st2
                return ("ss", evv)
            return f

        pg2 = pg
        pg2 = [("vv", p.val("vv"))] * TC
        for pc in range(4):
            pg2 = mm_group8(afm_lhs,
                            lambda kc, pc=pc:
                            ag1_tile(OFF_O, D, 128 * kc, 512 * pc, 512),
                            DC, wo_evac(pc), psum_guards=pg2,
                            bufs=["wrhsB0", "wrhsB1", "wrhsB2", "wrhsB3"])
        wo_done = {zl: p.val(zl) for zl in ("Z0", "Z1", "Z2", "Z3")}

        if DEBUG_PHASE >= 3:
            for zl, zv in wo_done.items():
                p.wait("gpsimd", zl, zv)
            dma("Gd", lambda: dbg["z"][:, :], lambda: s_z[:, :])

        for nm, src in (("bo_b", bo_in), ("lnw_b", lnw_in), ("lnb_b", lnb_in)):
            t = dma("Gd", lambda nm=nm: A(nm),
                    lambda src=src: src[0:1, :].broadcast_to([128, D]))
        bco = p.val("Gd")
        p.wait("vector", "Gd", bco)
        p.wait("scalar", "Gd", bco)

        for zl, zv in wo_done.items():
            p.wait("sync", zl, zv)
        for tc in range(TC):
            tz = dma("Lt", lambda: A("t_t"),
                     lambda tc=tc: s_z[128 * tc:128 * (tc + 1), :])
            tx = dma("Lx", lambda: A("x_t"),
                     lambda tc=tc: x_in[128 * tc:128 * (tc + 1), :])
            p.wait("vector", "Lt", tz)
            p.wait("vector", "Lx", tx)
            zb = op("vector", lambda e, S: e.tensor_tensor(
                out=A("z_t"), in0=A("t_t"), in1=A("bo_b"), op=ALU.add),
                inc=True)
            p.wait("scalar", "vv", zb)
            ln_stats(lambda: A("z_t"))
            ap2 = op("scalar", lambda e, S: e.activation(
                out=A("t_t"), in_=A("z_t"), func=AF.Identity,
                scale=Asm("ln_r"), bias=Asm("ln_nmr")), inc=True)
            p.wait("vector", "ss", ap2)
            x1v = op("vector", lambda e, S: e.tensor_tensor(
                out=A("z_t"), in0=A("t_t"), in1=A("x_t"), op=ALU.add),
                inc=True)
            p.wait("sync", "vv", x1v)
            sx1 = dma("Ls", lambda tc=tc: s_x1[128 * tc:128 * (tc + 1), :],
                      lambda: A("z_t"))
            p.wait("scalar", "vv", x1v)
            ln_stats(lambda: A("z_t"))
            h0v = op("scalar", lambda e, S: e.activation(
                out=Abf("h0st"), in_=A("z_t"), func=AF.Identity,
                scale=Asm("ln_r"), bias=Asm("ln_nmr")), inc=True)
            p.wait("sync", "ss", h0v)
            dma("Ls", lambda tc=tc: s_h0[128 * tc:128 * (tc + 1), :],
                lambda: Abf("h0st"))
            p.wait("scalar", "ss", h0v)
            ln_stats(lambda: Abf("h0st"))
            t2v = op("scalar", lambda e, S: e.activation(
                out=A("t_t"), in_=Abf("h0st"), func=AF.Identity,
                scale=Asm("ln_r"), bias=Asm("ln_nmr")), inc=True)
            p.wait("vector", "ss", t2v)
            p.wait("vector", "Ls")
            op("vector", lambda e, S: e.tensor_tensor(
                out=A("z_t"), in0=A("t_t"), in1=A("lnw_b"), op=ALU.mult))
            p.drain("vector")
            hv = op("vector", lambda e, S: e.tensor_tensor(
                out=Abf("hst"), in0=A("z_t"), in1=A("lnb_b"), op=ALU.add),
                inc=True)
            p.wait("vector", "vv", hv)
            for half in range(2):
                qf = quant_chain(
                    lambda half=half: Abf("hst")[:, 1024 * half:1024 * (half + 1)],
                    lambda half=half: Abf("hqst")[:, 1024 * half:1024 * (half + 1)],
                    1024, vbuf="q2_v", rbuf="q2_r")
            p.wait("sync", "vv", qf)
            dma("Ls", lambda tc=tc: s_hq[128 * tc:128 * (tc + 1), :],
                lambda: Abf("hqst"))
            p.wait("sync", "vv", qf)
        hq_stored = p.val("Ls")

        if DEBUG_PHASE >= 4:
            p.wait("gpsimd", "Ls", hq_stored)
            dma("Gd", lambda: dbg["x1"][:, :], lambda: s_x1[:, :])
            dma("Gd", lambda: dbg["hq"][:, :], lambda: s_hq[:, :])

    if UPTO >= 5:
        p.wait("scalar", "Ls", hq_stored)
        for dc in range(DC):
            dmaT("Tl",
                 lambda dc=dc: Abf("hqfm").rearrange(
                     "p (c f) -> p c f", f=1024)[:, dc, :],
                 lambda dc=dc: s_hq[:, 128 * dc:128 * (dc + 1)])
        hqfm_done = p.val("Tl")

        def hq_lhs(kc, tc):
            return Abf("hqfm").rearrange("p (c f) -> p c f", f=1024)[
                :, kc, 128 * tc:128 * (tc + 1)]

        p.wait("gpsimd", "vv", p.val("vv"))
        p.wait("gpsimd", "ss", p.val("ss"))
        dma("Gd", lambda: A("b1full"),
            lambda: b1_in[0:1, :].broadcast_to([128, DFF]))
        tb2 = dma("Gd", lambda: A("b2full"),
                  lambda: b2_in[0:1, :].broadcast_to([128, D]))

        w1st = {"i": 0, "g_y1p": [None] * 6, "g_y1g": [None] * 6,
                "g_y1h": [None] * 6, "g_gqs": [None] * 6,
                "grp_st": [], "stash": []}

        def w1_flush():
            yvs = []
            for (tc, sl, evv, jc) in w1st["stash"]:
                p.wait("vector", "ss", evv)
                if w1st["g_y1g"][sl] is not None:
                    p.wait("vector", "ss", w1st["g_y1g"][sl])
                yv = op("vector", lambda e, S, sl=sl, jc=jc:
                        e.tensor_tensor(
                            out=Abf(f"y1g{sl}")[:, :], in0=A(f"y1p{sl}"),
                            in1=A("b1full")[:, 512 * jc:512 * (jc + 1)],
                            op=ALU.add), inc=True)
                w1st["g_y1p"][sl] = yv
                yvs.append(yv)
            for (tc, sl, evv, jc), yv in zip(w1st["stash"], yvs):
                el = EVL[w1st["i"] & 3]
                w1st["i"] += 1
                p.wait("scalar", "vv", yv)
                if w1st["g_y1h"][sl] is not None:
                    p.wait("scalar", "vv", w1st["g_y1h"][sl])
                gv = op("scalar", lambda e, S, sl=sl: e.activation(
                    out=Abf(f"y1h{sl}")[:, :], in_=Abf(f"y1g{sl}")[:, :],
                    func=AF.Gelu_apprx_tanh), inc=True)
                w1st["g_y1g"][sl] = gv
                p.wait("vector", "ss", gv)
                if w1st["g_gqs"][sl] is not None:
                    gl, gvv = w1st["g_gqs"][sl]
                    p.wait("vector", gl, p.val(gl))
                qv = quant_chain(lambda sl=sl: Abf(f"y1h{sl}")[:, :],
                                 lambda sl=sl: Abf(f"gqs{sl}")[:, :],
                                 512, vbuf="q2_v", rbuf="q2_r")
                w1st["g_y1h"][sl] = qv
                p.wait("sync", "vv", qv)
                stv = dma(el,
                          lambda tc=tc, jc=jc: s_gq[128 * tc:128 * (tc + 1),
                                                    512 * jc:512 * (jc + 1)],
                          lambda sl=sl: Abf(f"gqs{sl}")[:, :])
                w1st["g_gqs"][sl] = (el, stv)
                w1st["grp_st"].append((el, stv))
            w1st["stash"] = []

        def w1_evac(jc):
            def f(tc, mm_target):
                sl = (w1st["i"] + len(w1st["stash"])) % 6
                p.wait("scalar", "mm", mm_target)
                if w1st["g_y1p"][sl] is not None:
                    p.wait("scalar", "vv", w1st["g_y1p"][sl])
                evv = op("scalar", lambda e, S, sl=sl, tc=tc: e.activation(
                    out=A(f"y1p{sl}"), in_=ps[f"ps{tc}"][:, :],
                    func=AF.Identity), inc=True)
                w1st["stash"].append((tc, sl, evv, jc))
                if len(w1st["stash"]) == 6 or tc == TC - 1:
                    w1_flush()
                return ("ss", evv)
            return f

        pg3 = [("ss", p.val("ss"))] * TC
        p.wait("tensor", "Tl", hqfm_done)
        p.wait("vector", "Gd", tb2)
        grp_done = []
        for jc in range(16):
            w1st["grp_st"] = []
            pg3 = mm_group8(hq_lhs,
                            lambda kc, jc=jc:
                            ag1_tile(OFF_W1, DFF, 128 * kc, 512 * jc, 512),
                            DC, w1_evac(jc), psum_guards=pg3)
            lanemax = {}
            for el, sv in w1st["grp_st"]:
                lanemax[el] = max(lanemax.get(el, 0), sv)
            grp_done.append(lanemax)
        w1_mm_done = p.val("mm")

        if DEBUG_PHASE >= 5:
            for el, sv in grp_done[-1].items():
                p.wait("gpsimd", el, sv)
            for lm in grp_done:
                for el, sv in lm.items():
                    p.wait("gpsimd", el, sv)
            dma("Gd", lambda: dbg["gq"][:, :], lambda: s_gq[:, :])

        w2st = {"guard": [None, None]}

        def w2_pre(kc):
            sl = kc & 1
            g = w2st["guard"][sl]
            if g is not None:
                p.wait("scalar", g[0], g[1])
            for el, sv in grp_done[kc // 4].items():
                p.wait("scalar", el, sv)
            t = dmaT("Tl",
                     lambda sl=sl: Abf(f"glhs{sl}")[:, :],
                     lambda kc=kc: s_gq[:, 128 * kc:128 * (kc + 1)])
            if kc == 63:
                w2st["guard"][sl] = ("mm", p.val("mm") + TC)
            else:
                w2st["guard"][sl] = ("mq", p.val("mq") + TC)
            p.wait("tensor", "Tl", t)

        def gq_lhs(kc, tc):
            return Abf(f"glhs{kc & 1}")[:, 128 * tc:128 * (tc + 1)]

        h0_state = {"g": [None, None], "fg": [None, None], "pend": {}}

        def w2_load_h0(oc, tc):
            sl = tc & 1
            g = h0_state["g"][sl]
            if g is not None:
                p.wait("sync", "vv", g)
            th = dma("Lh", lambda sl=sl: Abf(f"h0t{sl}")[:, :],
                     lambda tc=tc, oc=oc: s_h0[128 * tc:128 * (tc + 1),
                                               512 * oc:512 * (oc + 1)])
            h0_state["pend"][(oc, tc)] = th

        def w2_evac(oc):
            def f(tc, mm_target):
                sl = tc & 1
                fsl = w1st["i"] & 1
                w1st["i"] += 1
                if (oc, tc) not in h0_state["pend"]:
                    w2_load_h0(oc, tc)
                th = h0_state["pend"].pop((oc, tc))
                nxt = (oc, tc + 1) if tc < TC - 1 else                     ((oc + 1, 0) if oc < 3 else None)
                if nxt is not None:
                    w2_load_h0(*nxt)
                p.wait("vector", "mm", mm_target)
                p.wait("vector", "Lh", p.val("Lh"))
                if h0_state.get("pfv") is not None:
                    p.wait("vector", "vv", h0_state["pfv"])
                yv = op("vector", lambda e, S, tc=tc, oc=oc: e.tensor_tensor(
                    out=A("y2_f"), in0=ps[f"ps{tc}"][:, :],
                    in1=A("b2full")[:, 512 * oc:512 * (oc + 1)],
                    op=ALU.add), inc=True)
                p.drain("vector")
                if h0_state["fg"][fsl] is not None:
                    p.wait("vector", "Lf", p.val("Lf"))
                fv = op("vector", lambda e, S, sl=sl, fsl=fsl:
                        e.tensor_tensor(
                            out=Abf(f"fot{fsl}")[:, :], in0=A("y2_f"),
                            in1=Abf(f"h0t{sl}")[:, :], op=ALU.add), inc=True)
                h0_state["g"][sl] = fv
                h0_state["pfv"] = fv
                p.wait("sync", "vv", fv)
                sv = dma("Lf",
                         lambda tc=tc, oc=oc: s_fo[128 * tc:128 * (tc + 1),
                                                   512 * oc:512 * (oc + 1)],
                         lambda fsl=fsl: Abf(f"fot{fsl}")[:, :])
                h0_state["fg"][fsl] = sv
                return ("vv", yv)
            return f

        pg4 = pg3
        for oc in range(4):
            pg4 = mm_group8(gq_lhs,
                            lambda kc, oc=oc:
                            ag1_tile(OFF_W2, D, 128 * kc, 512 * oc, 512, 1024),
                            64, w2_evac(oc), psum_guards=pg4,
                            bufs=["wrhsD0", "wrhsD1"], pre_kc=w2_pre)
        fo_stored = p.val("Lf")

        p.wait("sync", "Lf", fo_stored)
        p.wait("scalar", "mm", p.val("mm"))
        fprev = None
        for tc in range(TC):
            if fprev is not None:
                p.wait("sync", "vv", fprev)
            tf = dma("Lg", lambda: Abf("fo_ld"),
                     lambda tc=tc: s_fo[128 * tc:128 * (tc + 1), :])
            tx1 = dma("Lt", lambda: A("t_t"),
                      lambda tc=tc: s_x1[128 * tc:128 * (tc + 1), :])
            p.wait("scalar", "Lg", tf)
            p.wait("scalar", "Lt", tx1)
            p.wait("vector", "Lt", tx1)
            ln_stats(lambda: Abf("fo_ld"))
            apf = op("scalar", lambda e, S: e.activation(
                out=A("z_t"), in_=Abf("fo_ld"), func=AF.Identity,
                scale=Asm("ln_r"), bias=Asm("ln_nmr")), inc=True)
            p.wait("vector", "ss", apf)
            if fprev is not None:
                p.wait("vector", "Ls", syp)
            yv = op("vector", lambda e, S: e.tensor_tensor(
                out=A("x_t"), in0=A("z_t"), in1=A("t_t"), op=ALU.add),
                inc=True)
            p.wait("sync", "vv", yv)
            syp = dma("Ls", lambda tc=tc: y_out[128 * tc:128 * (tc + 1), :],
                      lambda: A("x_t"))
            fprev = yv
    else:
        pv = None
        for tc in range(TC):
            if pv is not None:
                p.wait("sync", "Ls", pv)
            tl0 = dma("Lx", lambda: A("x_t"),
                      lambda tc=tc: x_in[128 * tc:128 * (tc + 1), :])
            p.wait("sync", "Lx", tl0)
            pv = dma("Ls", lambda tc=tc: y_out[128 * tc:128 * (tc + 1), :],
                     lambda: A("x_t"))

    for _ln in SYNC_LANES + SCAL_LANES + GP_LANES:
        if p.val(_ln) > 0:
            p.wait("gpsimd", _ln, p.val(_ln))

    import contextlib
    with contextlib.ExitStack() as stack:
        sb["AF32"] = stack.enter_context(
            nc.sbuf_tensor("AF32", [128, NF32], F32))
        sb["ABF"] = stack.enter_context(
            nc.sbuf_tensor("ABF", [128, NBF], BF16))
        for name, (shape, dt) in sm.items():
            sb[name] = stack.enter_context(nc.sbuf_tensor(name, shape, dt))
        for i in range(8):
            ps[f"ps{i}"] = stack.enter_context(
                nc.psum_tensor(f"ps{i}", [128, 512], F32))
        S = {n: stack.enter_context(nc.semaphore(n)) for n in SEM_NAMES}
        p.emit(nc, S)
    return nc


def kernel(L0_attn_hidden_states, L0_attn_Wq, L0_attn_Wk, L0_attn_Wv,
           L0_attn_Wo, L0_attn_bo, L0_ffn_ln_w, L0_ffn_ln_b, L0_ffn_W1,
           L0_ffn_b1, L0_ffn_W2, L0_ffn_b2):
    x = np.ascontiguousarray(np.asarray(L0_attn_hidden_states, np.float32))
    Wq = np.asarray(L0_attn_Wq, np.float32)
    Wk = np.asarray(L0_attn_Wk, np.float32)
    Wv = np.asarray(L0_attn_Wv, np.float32)
    Wo = np.asarray(L0_attn_Wo, np.float32)
    W1 = np.asarray(L0_ffn_W1, np.float32)
    W2 = np.asarray(L0_ffn_W2, np.float32)
    in_maps = []
    for s in range(NP):
        b, c = divmod(s, 4)
        in_maps.append({
            "x": np.ascontiguousarray(x[b, 1024 * c:1024 * (c + 1), :]),
            "pos": np.arange(1024 * c, 1024 * (c + 1),
                             dtype=np.float32)[None, :],
            "wq": np.ascontiguousarray(Wq[:, 256 * s:256 * (s + 1)]),
            "wk": np.ascontiguousarray(Wk[:, 256 * s:256 * (s + 1)]),
            "wv": np.ascontiguousarray(Wv[:, 256 * s:256 * (s + 1)]),
            "wo": np.ascontiguousarray(Wo[:, 256 * s:256 * (s + 1)]),
            "w1": np.ascontiguousarray(W1[:, 256 * s:256 * (s + 1)]),
            "w2": np.ascontiguousarray(W2[:, 1024 * s:1024 * (s + 1)]),
            "bo": np.asarray(L0_attn_bo, np.float32)[None, :],
            "lnw": np.asarray(L0_ffn_ln_w, np.float32)[None, :],
            "lnb": np.asarray(L0_ffn_ln_b, np.float32)[None, :],
            "b1": np.asarray(L0_ffn_b1, np.float32)[None, :],
            "b2": np.asarray(L0_ffn_b2, np.float32)[None, :],
        })
    nc = build_nc()
    res = run_bass_kernel_spmd(nc, in_maps, core_ids=list(range(NP)))
    out = np.empty((B, L, D), np.float32)
    for s in range(NP):
        b, c = divmod(s, 4)
        out[b, 1024 * c:1024 * (c + 1), :] = res.results[s]["y"]
    return out


# version 06 (final)
# speedup vs baseline: 1.0814x; 1.0275x over previous
import math
import numpy as np
import ml_dtypes
import concourse.bass as bass
import concourse.mybir as mybir
from concourse.bass_utils import run_bass_kernel_spmd

F32 = mybir.dt.float32
BF16 = mybir.dt.bfloat16
I32 = mybir.dt.int32
AF = mybir.ActivationFunctionType
ALU = mybir.AluOpType
AX = mybir.AxisListType

NP = 8
B, L, D = 2, 4096, 2048
T = 1024
DFF = 8192
TC = T // 128
DC = D // 128
NHP = 16
MAGIC = float(np.float32(1.5 * 2 ** 23))
TWO_PI = 2.0 * math.pi

OFF_Q = 0
OFF_K = OFF_Q + 256 * D
OFF_V = OFF_K + 256 * D
OFF_O = OFF_V + 256 * D
OFF_W1 = OFF_O + 256 * D
OFF_W2 = OFF_W1 + 256 * DFF
R1 = OFF_W2 + 1024 * D

OFF_KV_V = T * D
R2 = 2 * T * D

UPTO = 99
DEBUG_PHASE = 0
SKIP = set()

CSEM = {"vector": "vv", "scalar": "ss", "gpsimd": "gc", "tensor": "tt"}
SYNC_LANES = ["Lx", "Lt", "Lp", "Ls", "Lr", "W0", "W1", "W2", "W3",
              "E0", "E1", "E2", "E3", "V0", "V1", "Lb", "Lh", "Lf", "Lg"]
SCAL_LANES = ["Tl", "Tw", "Tk", "Z0", "Z1", "Z2", "Z3"]
GP_LANES = ["Gd", "Vs"]
LANE_ENG = {**{n: "sync" for n in SYNC_LANES},
            **{n: "scalar" for n in SCAL_LANES},
            **{n: "gpsimd" for n in GP_LANES}}
EVL = ["E0", "E1", "E2", "E3"]
WL = ["W0", "W1", "W2", "W3"]
ESEM = {"sync": SYNC_LANES, "scalar": SCAL_LANES + ["ss"],
        "vector": ["vv"],
        "tensor": ["tt", "mm", "mq"], "gpsimd": GP_LANES + ["gc"]}
SEM_NAMES = (SYNC_LANES + SCAL_LANES + GP_LANES +
             ["ag", "vv", "ss", "tt", "mm", "mq", "gc"])


def _cody_waite_consts():
    def chop(x):
        v = np.frombuffer(np.float32(x).tobytes(), dtype=np.uint32)[0]
        v = np.uint32(v & np.uint32(0xFFFFF000))
        return float(np.frombuffer(v.tobytes(), dtype=np.float32)[0])
    c1 = chop(TWO_PI)
    c2 = chop(TWO_PI - c1)
    c3 = float(np.float32(TWO_PI - c1 - c2))
    return c1, c2, c3


class Prog:
    ENGS = ("sync", "scalar", "vector", "tensor", "gpsimd")

    def __init__(self):
        self.ops = {e: [] for e in self.ENGS}
        self.cnt = {}
        self.max_waiter = {}
        self.gated_to = {}

    def on(self, eng, fn):
        self.ops[eng].append(fn)

    def inc(self, name, by=1):
        self.cnt[name] = self.cnt.get(name, 0) + by
        return self.cnt[name]

    def val(self, name):
        return self.cnt.get(name, 0)

    def wait(self, eng, name, val=None):
        v = self.cnt.get(name, 0) if val is None else val
        if name in LANE_ENG:
            self.max_waiter[name] = max(self.max_waiter.get(name, 0), v)
        self.on(eng, lambda e, S, n=name, v=v: e.wait_ge(S[n], v))

    def drain(self, eng):
        self.on(eng, lambda e, S: e.drain())

    def emit(self, nc, sems):
        with nc.Block() as block:
            for eng_name in self.ENGS:
                deco = getattr(block, eng_name)
                ops = self.ops[eng_name]

                def make(ops):
                    def body(e):
                        for fn in ops:
                            fn(e, sems)
                    return body
                deco(make(ops))


def build_nc():
    nc = bass.Bass("TRN2", target_bir_lowering=False, debug=False)

    x_in = nc.dram_tensor("x", [T, D], F32, kind="ExternalInput").ap()
    pos_in = nc.dram_tensor("pos", [1, T], F32, kind="ExternalInput").ap()
    w_in = {m: nc.dram_tensor(f"w{m}", [D, 256], F32, kind="ExternalInput").ap()
            for m in "qkvo"}
    w1_in = nc.dram_tensor("w1", [DFF, 256], F32, kind="ExternalInput").ap()
    w2_in = nc.dram_tensor("w2", [D, 1024], F32, kind="ExternalInput").ap()
    bo_in = nc.dram_tensor("bo", [1, D], F32, kind="ExternalInput").ap()
    lnw_in = nc.dram_tensor("lnw", [1, D], F32, kind="ExternalInput").ap()
    lnb_in = nc.dram_tensor("lnb", [1, D], F32, kind="ExternalInput").ap()
    b1_in = nc.dram_tensor("b1", [1, DFF], F32, kind="ExternalInput").ap()
    b2_in = nc.dram_tensor("b2", [1, D], F32, kind="ExternalInput").ap()
    y_out = nc.dram_tensor("y", [T, D], F32, kind="ExternalOutput").ap()

    ag1_in = nc.dram_tensor("ag1_in", [1, R1], BF16, kind="Internal").ap()
    ag1_out = nc.dram_tensor("ag1_out", [1, NP * R1], BF16, kind="Internal",
                             addr_space="Shared").ap()
    kv_in = nc.dram_tensor("kv_in", [1, R2], BF16, kind="Internal").ap()
    kv_out = nc.dram_tensor("kv_out", [1, NP * R2], BF16, kind="Internal",
                            addr_space="Shared").ap()
    s_w = {m: nc.dram_tensor(f"s_w{m}", [D, 256], BF16, kind="Internal").ap()
           for m in "qkvo"}
    s_w1 = nc.dram_tensor("s_w1", [DFF, 256], BF16, kind="Internal").ap()
    s_w2 = nc.dram_tensor("s_w2", [D, 1024], BF16, kind="Internal").ap()
    s_ln1x = nc.dram_tensor("s_ln1x", [T, D], BF16, kind="Internal").ap()
    s_qhat = nc.dram_tensor("s_qhat", [T, D], BF16, kind="Internal").ap()
    s_z = nc.dram_tensor("s_z", [T, D], F32, kind="Internal").ap()
    s_x1 = nc.dram_tensor("s_x1", [T, D], F32, kind="Internal").ap()
    s_h0 = nc.dram_tensor("s_h0", [T, D], BF16, kind="Internal").ap()
    s_hq = nc.dram_tensor("s_hq", [T, D], BF16, kind="Internal").ap()
    s_g = nc.dram_tensor("s_g", [T, DFF], BF16, kind="Internal").ap()
    s_gq = nc.dram_tensor("s_gq", [T, DFF], BF16, kind="Internal").ap()
    s_fo = nc.dram_tensor("s_fo", [T, D], BF16, kind="Internal").ap()

    dbg = {}
    if DEBUG_PHASE >= 1:
        dbg["ag1"] = nc.dram_tensor("d_ag1", [1, R1], BF16,
                                    kind="ExternalOutput").ap()
        dbg["ln1x"] = nc.dram_tensor("d_ln1x", [T, D], BF16,
                                     kind="ExternalOutput").ap()
    if DEBUG_PHASE >= 2:
        dbg["qhat"] = nc.dram_tensor("d_qhat", [T, D], BF16,
                                     kind="ExternalOutput").ap()
        dbg["kv"] = nc.dram_tensor("d_kv", [1, R2], BF16,
                                   kind="ExternalOutput").ap()
    if DEBUG_PHASE >= 3:
        dbg["z"] = nc.dram_tensor("d_z", [T, D], F32,
                                  kind="ExternalOutput").ap()
    if DEBUG_PHASE >= 4:
        dbg["x1"] = nc.dram_tensor("d_x1", [T, D], F32,
                                   kind="ExternalOutput").ap()
        dbg["hq"] = nc.dram_tensor("d_hq", [T, D], BF16,
                                   kind="ExternalOutput").ap()
    if DEBUG_PHASE >= 5:
        dbg["gq"] = nc.dram_tensor("d_gq", [T, DFF], BF16,
                                   kind="ExternalOutput").ap()

    p = Prog()

    NF32 = 15360
    NBF = 68096

    fz = {}

    def F(name, start, n):
        assert start + n <= NF32, (name, start + n)
        fz[name] = (start, n)
    F("prep_f", 0, 2048)
    F("q_v", 2048, 2048)
    F("q_r", 4096, 2048)
    F("rope_t1", 8192, 512)
    F("rope_t2", 8704, 512)
    F("cosT", 9216, 512)
    F("sinS", 9728, 512)
    F("rp_th", 10240, 64)
    F("rp_u", 10304, 64)
    F("rp_k", 10368, 64)
    F("rp_r", 10432, 64)
    F("rp_si", 10496, 64)
    F("rp_ci", 10560, 64)
    F("rp_sin", 10624, 64)
    F("invfB", 10688, 64)
    F("msk_f", 10752, 512)
    F("x_t", 11264, 2048)
    F("q2_v", 13312, 1024)
    F("q2_r", 14336, 1024)
    F("smrcp", 0, 512)
    F("z_t", 512, 2048)
    F("t_t", 2560, 2048)
    F("bo_b", 4608, 2048)
    F("lnw_b", 6656, 2048)
    F("lnb_b", 8704, 2048)
    F("bias_b", 10752, 512)
    F("y2_f", 0, 512)
    F("b1full", 512, 8192)
    F("b2full", 8704, 2048)
    F("y1p0", 10752, 512)
    F("y1p1", 11264, 512)
    F("y1p2", 11776, 512)
    F("y1p3", 12288, 512)
    F("y1p4", 12800, 512)
    F("y1p5", 0, 512)

    bz = {}

    def Bf(name, start, n):
        assert start + n <= NBF, (name, start + n)
        bz[name] = (start, n)
    Bf("lfm", 0, 16384)
    Bf("wtr", 16384, 8192)
    Bf("prep_q", 24576, 2048)
    Bf("lnx_bf", 24576, 2048)
    Bf("ev0", 26624, 512)
    Bf("ev1", 27136, 512)
    Bf("ev2", 27648, 512)
    Bf("ev3", 28160, 512)
    Bf("wrhs0", 28672, 512)
    Bf("wrhs1", 29184, 512)
    Bf("wrhs2", 29696, 512)
    Bf("wrhs3", 30208, 512)
    Bf("qfm", 0, 16384)
    Bf("afm", 16384, 16384)
    Bf("khp", 32768, 2048)
    Bf("A0", 34816, 512)
    Bf("A1", 35328, 512)
    Bf("A2", 35840, 512)
    Bf("A3", 36352, 512)
    Bf("vp0", 36864, 128)
    Bf("vp1", 36992, 128)
    Bf("selo", 37120, 1024)
    Bf("mskA", 38144, 2048)
    Bf("wrhsB0", 40960, 512)
    Bf("wrhsB1", 41472, 512)
    Bf("wrhsB2", 41984, 512)
    Bf("wrhsB3", 42496, 512)
    Bf("h0st", 16384, 2048)
    Bf("hst", 18432, 2048)
    Bf("hqst", 20480, 2048)
    Bf("hqfm", 0, 16384)
    for _i in range(6):
        Bf(f"y1g{_i}", 35840 + 512 * _i, 512)
        Bf(f"y1h{_i}", 38912 + 512 * _i, 512)
        Bf(f"gqs{_i}", 41984 + 512 * _i, 512)
    Bf("glhs0", 18432, 1024)
    Bf("glhs1", 19456, 1024)
    Bf("wrhsD0", 65536, 512)
    Bf("wrhsD1", 66048, 512)
    Bf("h0t0", 33792, 512)
    Bf("h0t1", 34304, 512)
    Bf("fot0", 66560, 512)
    Bf("fot1", 67072, 512)
    Bf("fo_ld", 65536, 2048)

    sm = {}

    def SMT(name, shape, dt):
        sm[name] = (shape, dt)
    SMT("ln_s1", [128, 1], F32)
    SMT("ln_s1b", [128, 1], F32)
    SMT("ln_s2", [128, 1], F32)
    SMT("ln_s2b", [128, 1], F32)
    SMT("ln_mu", [128, 1], F32)
    SMT("ln_var", [128, 1], F32)
    SMT("ln_sd", [128, 1], F32)
    SMT("ln_r", [128, 1], F32)
    SMT("ln_nmr", [128, 1], F32)
    SMT("iv_i", [128, 64], I32)
    SMT("rp_pos", [128, 1], F32)
    SMT("bm10", [128, 1], F32)
    SMT("m512", [128, 1], F32)
    SMT("onescol", [128, 1], BF16)
    SMT("onesrow", [128, 64], BF16)
    SMT("serow", [128, 1024], F32)
    SMT("wq_amax", [128, 64], F32)
    SMT("wq_amax2", [128, 64], F32)
    SMT("wq_bits", [128, 64], I32)
    SMT("wq_m", [128, 64], I32)
    SMT("wq_mf", [128, 64], F32)
    SMT("wq_mf2", [128, 64], F32)
    SMT("wq_sb", [128, 64], I32)
    SMT("wq_scale", [128, 64], F32)
    SMT("wq_inv", [128, 64], F32)
    SMT("msk_ii", [128, 512], I32)
    SMT("msk_s", [128, 512], F32)

    sb = {}
    ps = {}

    def A(name):
        s0, n = fz[name]
        return sb["AF32"][:, s0:s0 + n]

    def Abf(name):
        s0, n = bz[name]
        return sb["ABF"][:, s0:s0 + n]

    def Asm(name):
        return sb[name][:, :]

    def _lane_gate(lane):
        eng = LANE_ENG[lane]
        mw = p.max_waiter.get(lane, 0)
        if mw > p.gated_to.get(lane, 0):
            p.on(eng, lambda e, S, n=lane, v=mw: e.wait_ge(S[n], v))
            p.gated_to[lane] = mw

    def dma(lane, dst_fn, src_fn):
        eng = LANE_ENG[lane]
        _lane_gate(lane)
        v = p.inc(lane, 16)
        p.on(eng, lambda e, S, d=dst_fn, s=src_fn, sem=lane:
             e.dma_start(out=d(), in_=s()).then_inc(S[sem], 16))
        return v

    def dmaT(lane, dst_fn, src_fn):
        eng = LANE_ENG[lane]
        _lane_gate(lane)
        v = p.inc(lane, 16)
        p.on(eng, lambda e, S, d=dst_fn, s=src_fn, sem=lane:
             e.dma_start_transpose(out=d(), in_=s()).then_inc(S[sem], 16))
        return v

    def op(eng, mk, inc=False, sem=None):
        if inc:
            sem = sem or CSEM[eng]
            v = p.inc(sem)
            p.on(eng, lambda e, S, mk=mk, sem=sem:
                 mk(e, S).then_inc(S[sem], 1))
            return v
        p.on(eng, lambda e, S, mk=mk: mk(e, S))
        return None

    def ag1_tile(off_m, rowlen, d0, col0, ncol, shard_rows=256):
        r = d0 // shard_rows
        i = d0 % shard_rows
        base = r * R1 + off_m + i * rowlen
        flat = ag1_out[0, :]
        return flat[base:base + 128 * rowlen].rearrange(
            "(p f) -> p f", f=rowlen)[:, col0:col0 + ncol]

    def kvk_ap(rank, r0, r1, hp):
        base = rank * R2
        flat = kv_out[0, :]
        return flat[base:base + T * D].rearrange(
            "(t d) -> t d", d=D)[r0:r1, 128 * hp:128 * (hp + 1)]

    def kvv_ap(rank, r0, r1, hp):
        base = rank * R2 + OFF_KV_V
        flat = kv_out[0, :]
        return flat[base:base + T * D].rearrange(
            "(t d) -> t d", d=D)[r0:r1, 128 * hp:128 * (hp + 1)]

    def quant_chain(src_fn, dst_fn, nfree, vbuf="q_v", rbuf="q_r"):
        nblk = nfree // 32
        assert nblk <= 64

        def v3(ap):
            return ap.rearrange("p (nb b) -> p nb b", b=32)
        op("vector", lambda e, S: e.tensor_reduce(
            out=Asm("wq_amax")[:, :nblk], in_=v3(src_fn()), axis=AX.X,
            op=ALU.max, apply_absolute_value=True))
        p.drain("vector")
        op("vector", lambda e, S: e.tensor_scalar(
            out=Asm("wq_amax2")[:, :nblk], in0=Asm("wq_amax")[:, :nblk],
            scalar1=1e-30, scalar2=None, op0=ALU.max))
        p.drain("vector")
        op("vector", lambda e, S: e.tensor_copy(
            out=Asm("wq_bits")[:, :nblk],
            in_=Asm("wq_amax2")[:, :nblk].bitcast(I32)))
        p.drain("vector")
        op("vector", lambda e, S: e.tensor_scalar(
            out=Asm("wq_m")[:, :nblk], in0=Asm("wq_bits")[:, :nblk],
            scalar1=23, scalar2=None, op0=ALU.logical_shift_right))
        p.drain("vector")
        op("vector", lambda e, S: e.tensor_copy(
            out=Asm("wq_mf")[:, :nblk], in_=Asm("wq_m")[:, :nblk]))
        p.drain("vector")
        op("vector", lambda e, S: e.tensor_scalar(
            out=Asm("wq_mf2")[:, :nblk], in0=Asm("wq_mf")[:, :nblk],
            scalar1=float(1 << 23), scalar2=-float(6 << 23),
            op0=ALU.mult, op1=ALU.add))
        p.drain("vector")
        op("vector", lambda e, S: e.tensor_copy(
            out=Asm("wq_sb")[:, :nblk], in_=Asm("wq_mf2")[:, :nblk]))
        p.drain("vector")
        op("vector", lambda e, S: e.tensor_copy(
            out=Asm("wq_scale")[:, :nblk],
            in_=Asm("wq_sb")[:, :nblk].bitcast(F32)))
        p.drain("vector")
        op("vector", lambda e, S: e.reciprocal(
            out=Asm("wq_inv")[:, :nblk], in_=Asm("wq_scale")[:, :nblk]))
        p.drain("vector")
        op("vector", lambda e, S: e.tensor_tensor(
            out=v3(A(vbuf)[:, :nfree]), in0=v3(src_fn()),
            in1=Asm("wq_inv")[:, :nblk].unsqueeze(2).broadcast_to(
                [128, nblk, 32]), op=ALU.mult))
        p.drain("vector")
        op("vector", lambda e, S: e.tensor_scalar(
            out=A(rbuf)[:, :nfree], in0=A(vbuf)[:, :nfree],
            scalar1=MAGIC, scalar2=MAGIC, op0=ALU.add, op1=ALU.subtract))
        p.drain("vector")
        op("vector", lambda e, S: e.tensor_scalar(
            out=A(vbuf)[:, :nfree], in0=A(rbuf)[:, :nfree],
            scalar1=127.0, scalar2=-128.0, op0=ALU.min, op1=ALU.max))
        p.drain("vector")
        fin = op("vector", lambda e, S: e.tensor_tensor(
            out=v3(dst_fn()), in0=v3(A(vbuf)[:, :nfree]),
            in1=Asm("wq_scale")[:, :nblk].unsqueeze(2).broadcast_to(
                [128, nblk, 32]), op=ALU.mult), inc=True)
        return fin

    def ln_stats(src_fn):
        op("scalar", lambda e, S: e.activation(
            out=A("q2_v"), in_=src_fn()[:, 0:1024], func=AF.Square,
            accum_out=Asm("ln_s2")))
        h1 = op("scalar", lambda e, S: e.activation(
            out=A("q2_r"), in_=src_fn()[:, 1024:2048], func=AF.Square,
            accum_out=Asm("ln_s2b")), inc=True)
        p.wait("scalar", "ss", h1)
        op("scalar", lambda e, S: e.activation(
            out=A("q2_v"), in_=src_fn()[:, 0:1024], func=AF.Identity,
            accum_out=Asm("ln_s1")))
        a = op("scalar", lambda e, S: e.activation(
            out=A("q2_r"), in_=src_fn()[:, 1024:2048], func=AF.Identity,
            accum_out=Asm("ln_s1b")), inc=True)
        p.wait("vector", "ss", a)
        op("vector", lambda e, S: e.tensor_tensor(
            out=Asm("ln_s1"), in0=Asm("ln_s1"), in1=Asm("ln_s1b"),
            op=ALU.add))
        op("vector", lambda e, S: e.tensor_tensor(
            out=Asm("ln_s2"), in0=Asm("ln_s2"), in1=Asm("ln_s2b"),
            op=ALU.add))
        p.drain("vector")
        op("vector", lambda e, S: e.tensor_scalar(
            out=Asm("ln_mu"), in0=Asm("ln_s1"), scalar1=1.0 / D,
            scalar2=None, op0=ALU.mult))
        p.drain("vector")
        op("vector", lambda e, S: e.tensor_tensor(
            out=Asm("ln_var"), in0=Asm("ln_mu"), in1=Asm("ln_mu"),
            op=ALU.mult))
        p.drain("vector")
        op("vector", lambda e, S: e.scalar_tensor_tensor(
            out=Asm("ln_var"), in0=Asm("ln_s2"), scalar=1.0 / D,
            in1=Asm("ln_var"), op0=ALU.mult, op1=ALU.subtract))
        p.drain("vector")
        b = op("vector", lambda e, S: e.tensor_scalar(
            out=Asm("ln_sd"), in0=Asm("ln_var"), scalar1=1e-5,
            scalar2=None, op0=ALU.add), inc=True)
        p.wait("scalar", "vv", b)
        c = op("scalar", lambda e, S: e.sqrt(
            out=Asm("ln_sd"), in_=Asm("ln_sd")), inc=True)
        p.wait("vector", "ss", c)
        op("vector", lambda e, S: e.reciprocal(
            out=Asm("ln_r"), in_=Asm("ln_sd")))
        p.drain("vector")
        d = op("vector", lambda e, S: e.scalar_tensor_tensor(
            out=Asm("ln_nmr"), in0=Asm("ln_mu"), scalar=-1.0,
            in1=Asm("ln_r"), op0=ALU.mult, op1=ALU.mult), inc=True)
        p.wait("scalar", "vv", d)
        return d


    prev_apply = None
    prev_lnstore = None
    for tc in range(TC):
        if prev_apply is not None:
            p.wait("sync", "ss", prev_apply)
        t = dma("Lx", lambda: A("x_t"),
                lambda tc=tc: x_in[128 * tc:128 * (tc + 1), :])
        p.wait("scalar", "Lx", t)
        ln_stats(lambda: A("x_t"))
        if prev_lnstore is not None:
            p.wait("scalar", "Ls", prev_lnstore)
        ap_ = op("scalar", lambda e, S: e.activation(
            out=Abf("lnx_bf"), in_=A("x_t"), func=AF.Identity,
            scale=Asm("ln_r"), bias=Asm("ln_nmr")), inc=True)
        prev_apply = ap_
        p.wait("sync", "ss", ap_)
        prev_lnstore = dma(
            "Ls", lambda tc=tc: s_ln1x[128 * tc:128 * (tc + 1), :],
            lambda: Abf("lnx_bf"))
    ln1_last_store = p.val("Ls")
    p.wait("scalar", "Ls", ln1_last_store)
    for dc in range(DC):
        dmaT("Tl",
             lambda dc=dc: Abf("lfm").rearrange(
                 "p (c f) -> p c f", f=1024)[:, dc, :],
             lambda dc=dc: s_ln1x[:, 128 * dc:128 * (dc + 1)])
    lfm_done = p.val("Tl")

    prep_state = {"fin": None, "fin_quant": True, "store": None}

    def prep_rows(src, dst_dram, nrow_chunk, nchunks, ncol, quant):
        prev_fin = prep_state["fin"]
        prev_quant = prep_state["fin_quant"]
        prev_store = prep_state["store"]
        for ch in range(nchunks):
            r0 = nrow_chunk * ch
            nsub = nrow_chunk // 128
            nfree = nsub * ncol
            if prev_fin is not None:
                p.wait("sync", "vv" if prev_quant else "ss", prev_fin)
            prev_quant = quant
            t0 = dma("Lp",
                     lambda nsub=nsub, ncol=ncol: A("prep_f").rearrange(
                         "p (c f) -> p c f", f=ncol)[:, 0:nsub, :],
                     lambda src=src, r0=r0, nrow_chunk=nrow_chunk:
                     src[r0:r0 + nrow_chunk, :].rearrange(
                         "(c p) f -> p c f", p=128))
            eng2 = "vector" if quant else "scalar"
            p.wait(eng2, "Lp", t0)
            if prev_store is not None:
                p.wait(eng2, "Ls", prev_store)
            else:
                p.wait(eng2, "Ls", p.val("Ls"))
            if quant:
                fin = quant_chain(
                    lambda nfree=nfree: A("prep_f")[:, :nfree],
                    lambda nfree=nfree: Abf("prep_q")[:, :nfree], nfree)
            else:
                fin = op("scalar", lambda e, S, nfree=nfree: e.copy(
                    out=Abf("prep_q")[:, :nfree],
                    in_=A("prep_f")[:, :nfree]), inc=True)
            p.wait("sync", "vv" if quant else "ss", fin)
            prev_store = dma(
                "Ls",
                lambda dst_dram=dst_dram, r0=r0, nrow_chunk=nrow_chunk:
                dst_dram[r0:r0 + nrow_chunk, :].rearrange(
                    "(c p) f -> p c f", p=128),
                lambda nsub=nsub, ncol=ncol: Abf("prep_q").rearrange(
                    "p (c f) -> p c f", f=ncol)[:, 0:nsub, :])
            prev_fin = fin
        prep_state["fin"] = prev_fin
        prep_state["fin_quant"] = quant
        prep_state["store"] = prev_store
        return prev_store

    prep_rows(w1_in, s_w1, 1024, 8, 256, True)
    prep_rows(w2_in, s_w2, 256, 8, 1024, True)
    for m in "qkvo":
        prep_rows(w_in[m], s_w[m], 1024, 2, 256, False)
    wprep_stored = p.val("Ls")
    p.wait("scalar", "Ls", wprep_stored)

    def prep_transpose(src_dram, ncols_src, off_dst, nrows_src):
        for i in range(ncols_src // 128):
            tW = dmaT("Tw",
                      lambda nrows_src=nrows_src: Abf("wtr")[:, :nrows_src],
                      lambda src_dram=src_dram, i=i:
                      src_dram[:, 128 * i:128 * (i + 1)])
            p.wait("scalar", "Tw", tW)
            stW = dma("Tw",
                      lambda off_dst=off_dst, i=i, nrows_src=nrows_src:
                      ag1_in[0, off_dst + 128 * i * nrows_src:
                             off_dst + 128 * (i + 1) * nrows_src].rearrange(
                          "(p f) -> p f", p=128),
                      lambda nrows_src=nrows_src: Abf("wtr")[:, :nrows_src])
            p.wait("scalar", "Tw", stW)

    prep_transpose(s_w1, 256, OFF_W1, DFF)
    prep_transpose(s_w2, 1024, OFF_W2, D)
    for m, off in (("q", OFF_Q), ("k", OFF_K), ("v", OFF_V), ("o", OFF_O)):
        prep_transpose(s_w[m], 256, off, D)
    wprep_done = p.val("Tw")

    p.wait("gpsimd", "Tw", wprep_done)
    ag1_done = p.inc("ag")
    p.on("gpsimd", lambda e, S: e.collective_compute(
        "AllGather", ALU.bypass, replica_groups=[list(range(NP))],
        ins=[ag1_in[:, :]], outs=[ag1_out[:, :]]).then_inc(S["ag"], 1))

    if DEBUG_PHASE >= 1:
        p.wait("gpsimd", "ag", ag1_done)
        dma("Gd", lambda: dbg["ag1"][:, :], lambda: ag1_in[:, :])
        dma("Gd", lambda: dbg["ln1x"][:, :], lambda: s_ln1x[:, :])

    c1, c2, c3 = _cody_waite_consts()
    do_rope = "rope" not in SKIP
    iv = op("gpsimd", lambda e, S: e.iota(
        out=Asm("iv_i")[:, :], pattern=[[0, 2], [1, 32]], base=0,
        channel_multiplier=0), inc=True)
    mi = None
    mv = 0
    for j in ([] if "masks" in SKIP else range(4)):
        mi = op("gpsimd", lambda e, S, j=j: e.iota(
            out=Asm("msk_ii")[:, :], pattern=[[1, 512]], base=-128 * j,
            channel_multiplier=-1), inc=True)
        p.wait("scalar", "gc", mi)
        mc = op("scalar", lambda e, S: e.copy(
            out=Asm("msk_s"), in_=Asm("msk_ii")), inc=True)
        p.wait("vector", "ss", mc)
        mv = op("vector", lambda e, S, j=j: e.tensor_scalar(
            out=Abf("mskA")[:, 512 * j:512 * (j + 1)], in0=Asm("msk_s"),
            scalar1=0.0, scalar2=-1e30, op0=ALU.is_lt, op1=ALU.mult),
            inc=True)
        if j < 3:
            p.wait("gpsimd", "vv", mv)
    masks_done = mv
    p.wait("scalar", "gc", iv)
    s1v = op("scalar", lambda e, S: e.activation(
        out=A("invfB"), in_=Asm("iv_i")[:, :], func=AF.Exp,
        scale=-math.log(10000.0) / 32.0), inc=True)
    p.wait("vector", "ss", s1v)
    op("vector", lambda e, S: e.memset(Asm("bm10"), -10.0))
    op("vector", lambda e, S: e.memset(Asm("m512"), 512.0))
    op("vector", lambda e, S: e.memset(Asm("onescol"), 1.0))
    consts_v = op("vector", lambda e, S: e.memset(Asm("onesrow"), 1.0),
                  inc=True)
    for tc in (range(TC) if do_rope else []):
        ldp = dma("Lr", lambda: Asm("rp_pos"),
                  lambda tc=tc: pos_in[0, 128 * tc:128 * (tc + 1)].rearrange(
                      "(p one) -> p one", one=1))
        p.wait("vector", "Lr", ldp)
        thv = op("vector", lambda e, S: e.tensor_scalar(
            out=A("rp_th"), in0=A("invfB"), scalar1=Asm("rp_pos"),
            scalar2=None, op0=ALU.mult), inc=True)
        p.wait("sync", "vv", thv)
        p.drain("vector")
        op("vector", lambda e, S: e.tensor_scalar(
            out=A("rp_u"), in0=A("rp_th"), scalar1=1.0 / TWO_PI,
            scalar2=None, op0=ALU.mult))
        p.drain("vector")
        op("vector", lambda e, S: e.tensor_scalar(
            out=A("rp_k"), in0=A("rp_u"), scalar1=MAGIC, scalar2=MAGIC,
            op0=ALU.add, op1=ALU.subtract))
        p.drain("vector")
        op("vector", lambda e, S: e.scalar_tensor_tensor(
            out=A("rp_r"), in0=A("rp_k"), scalar=-c1, in1=A("rp_th"),
            op0=ALU.mult, op1=ALU.add))
        p.drain("vector")
        op("vector", lambda e, S: e.scalar_tensor_tensor(
            out=A("rp_u"), in0=A("rp_k"), scalar=-c2, in1=A("rp_r"),
            op0=ALU.mult, op1=ALU.add))
        p.drain("vector")
        op("vector", lambda e, S: e.scalar_tensor_tensor(
            out=A("rp_r"), in0=A("rp_k"), scalar=-c3, in1=A("rp_u"),
            op0=ALU.mult, op1=ALU.add))
        p.drain("vector")
        op("vector", lambda e, S: e.tensor_scalar(
            out=A("rp_si"), in0=A("rp_r"), scalar1=math.pi, scalar2=None,
            op0=ALU.is_gt))
        p.drain("vector")
        op("vector", lambda e, S: e.tensor_scalar(
            out=A("rp_u"), in0=A("rp_r"), scalar1=-math.pi, scalar2=None,
            op0=ALU.is_lt))
        p.drain("vector")
        op("vector", lambda e, S: e.tensor_tensor(
            out=A("rp_u"), in0=A("rp_u"), in1=A("rp_si"), op=ALU.subtract))
        p.drain("vector")
        op("vector", lambda e, S: e.scalar_tensor_tensor(
            out=A("rp_si"), in0=A("rp_u"), scalar=TWO_PI, in1=A("rp_r"),
            op0=ALU.mult, op1=ALU.add))
        p.drain("vector")
        op("vector", lambda e, S: e.tensor_scalar(
            out=A("rp_ci"), in0=A("rp_r"), scalar1=math.pi / 2, scalar2=None,
            op0=ALU.add))
        p.drain("vector")
        op("vector", lambda e, S: e.tensor_scalar(
            out=A("rp_u"), in0=A("rp_ci"), scalar1=math.pi, scalar2=None,
            op0=ALU.is_gt))
        p.drain("vector")
        rw = op("vector", lambda e, S: e.scalar_tensor_tensor(
            out=A("rp_ci"), in0=A("rp_u"), scalar=-TWO_PI, in1=A("rp_ci"),
            op0=ALU.mult, op1=ALU.add), inc=True)
        p.wait("scalar", "vv", rw)
        op("scalar", lambda e, S: e.activation(
            out=A("rp_sin"), in_=A("rp_si"), func=AF.Sin))
        sc = op("scalar", lambda e, S, tc=tc: e.activation(
            out=A("cosT")[:, 64 * tc:64 * (tc + 1)], in_=A("rp_ci"),
            func=AF.Sin), inc=True)
        p.wait("vector", "ss", sc)
        op("vector", lambda e, S, tc=tc: e.tensor_scalar(
            out=A("sinS")[:, 64 * tc:64 * tc + 32], in0=A("rp_sin")[:, 0:32],
            scalar1=-1.0, scalar2=None, op0=ALU.mult))
        op("vector", lambda e, S, tc=tc: e.tensor_copy(
            out=A("sinS")[:, 64 * tc + 32:64 * (tc + 1)],
            in_=A("rp_sin")[:, 32:64]), inc=True)
    rope_done = p.val("vv")

    WBUFS = ["wrhs0", "wrhs1", "wrhs2", "wrhs3"]
    WLANE = {"wrhs0": "W0", "wrhs1": "W1", "wrhs2": "W2", "wrhs3": "W3",
             "wrhsB0": "W0", "wrhsB1": "W1", "wrhsB2": "W2", "wrhsB3": "W3",
             "wrhsD0": "W0", "wrhsD1": "W1"}
    wb_state = {"i": 0, "guard": {}}

    def stream_w(tile_fn, bufs=None):
        bufs = bufs if bufs is not None else WBUFS
        buf = bufs[wb_state["i"] % len(bufs)]
        wb_state["i"] += 1
        lane = WLANE[buf]
        g = wb_state["guard"].get(buf)
        if g is not None:
            p.wait("sync", g[0], g[1])
        t = dma(lane, lambda buf=buf: Abf(buf)[:, :512], tile_fn)
        return buf, t, lane

    def mm_group8(lhs_fn, tile_fn, nK, evac_fn, psum_guards=None,
                  bufs=None, pre_kc=None):
        for kc in range(nK):
            buf, t, lane = stream_w(lambda kc=kc: tile_fn(kc), bufs=bufs)
            if pre_kc is not None:
                pre_kc(kc)
            p.wait("tensor", lane, t)
            for tc in range(TC):
                if kc == 0 and psum_guards and psum_guards[tc] is not None:
                    p.wait("tensor", psum_guards[tc][0], psum_guards[tc][1])
                last = (kc == nK - 1)
                if last:
                    p.inc("mm")
                    p.on("tensor", lambda e, S, kc=kc, tc=tc, buf=buf:
                         e.matmul(ps[f"ps{tc}"][:, :], lhs_fn(kc, tc),
                                  Abf(buf)[:, :512], start=(kc == 0),
                                  stop=True).then_inc(S["mm"], 1))
                else:
                    p.inc("mq")
                    p.on("tensor", lambda e, S, kc=kc, tc=tc, buf=buf:
                         e.matmul(ps[f"ps{tc}"][:, :], lhs_fn(kc, tc),
                                  Abf(buf)[:, :512], start=(kc == 0),
                                  stop=False).then_inc(S["mq"], 1))
            wb_state["guard"][buf] = \
                ("mm", p.val("mm")) if kc == nK - 1 else ("mq", p.val("mq"))
        base = p.val("mm") - TC
        frees = []
        for tc in range(TC):
            frees.append(evac_fn(tc, base + tc + 1))
        return frees

    def lfm_lhs(kc, tc):
        return Abf("lfm").rearrange("p (c f) -> p c f", f=1024)[
            :, kc, 128 * tc:128 * (tc + 1)]

    EVB = ["ev0", "ev1", "ev2", "ev3"]
    EVLANE = {"ev0": "E0", "ev1": "E1", "ev2": "E2", "ev3": "E3"}
    ev_state = {"i": 0, "guard": {}}

    def qkv_evac(kind, pc):
        def f(tc, mm_target):
            buf = EVB[ev_state["i"] % 4]
            ev_state["i"] += 1
            weng = "scalar" if kind == "v" else "vector"
            g = ev_state["guard"].get(buf)
            if g is not None:
                p.wait(weng, EVLANE[buf], g)
            if kind == "v":
                p.wait("scalar", "mm", mm_target)
                evv = op("scalar", lambda e, S, buf=buf, tc=tc: e.activation(
                    out=Abf(buf)[:, :512], in_=ps[f"ps{tc}"][:, :],
                    func=AF.Identity), inc=True)
                p.wait("sync", "ss", evv)
                free = ("ss", evv)
            else:
                p.wait("vector", "mm", mm_target)

                def psv(tc2):
                    return ps[f"ps{tc2}"][:, :].rearrange(
                        "p (h d) -> p h d", d=64)

                def cosb(tc2):
                    return A("cosT")[:, 64 * tc2:64 * (tc2 + 1)].unsqueeze(1)\
                        .broadcast_to([128, 8, 64])

                def sinb(tc2, half):
                    s0 = 64 * tc2 + 32 * half
                    return A("sinS")[:, s0:s0 + 32].unsqueeze(1)\
                        .broadcast_to([128, 8, 32])
                op("vector", lambda e, S, tc=tc: e.tensor_tensor(
                    out=A("rope_t1").rearrange("p (h d) -> p h d", d=64),
                    in0=psv(tc), in1=cosb(tc), op=ALU.mult))
                op("vector", lambda e, S, tc=tc: e.tensor_tensor(
                    out=A("rope_t2").rearrange(
                        "p (h d) -> p h d", d=64)[:, :, 0:32],
                    in0=psv(tc)[:, :, 32:64], in1=sinb(tc, 0), op=ALU.mult))
                pf = op("vector", lambda e, S, tc=tc: e.tensor_tensor(
                    out=A("rope_t2").rearrange(
                        "p (h d) -> p h d", d=64)[:, :, 32:64],
                    in0=psv(tc)[:, :, 0:32], in1=sinb(tc, 1), op=ALU.mult),
                    inc=True)
                p.drain("vector")
                evv = op("vector", lambda e, S, buf=buf: e.tensor_tensor(
                    out=Abf(buf)[:, :512], in0=A("rope_t1"),
                    in1=A("rope_t2"), op=ALU.add), inc=True)
                p.drain("vector")
                p.wait("sync", "vv", evv)
                free = ("vv", pf)
            if kind == "q":
                dst = lambda tc=tc, pc=pc: s_qhat[
                    128 * tc:128 * (tc + 1), 512 * pc:512 * (pc + 1)]
            elif kind == "k":
                dst = lambda tc=tc, pc=pc: kv_in[0, 0:OFF_KV_V].rearrange(
                    "(t d) -> t d", d=D)[128 * tc:128 * (tc + 1),
                                         512 * pc:512 * (pc + 1)]
            else:
                dst = lambda tc=tc, pc=pc: kv_in[0, OFF_KV_V:].rearrange(
                    "(t d) -> t d", d=D)[128 * tc:128 * (tc + 1),
                                         512 * pc:512 * (pc + 1)]
            g = dma(EVLANE[buf], dst, lambda buf=buf: Abf(buf)[:, :512])
            ev_state["guard"][buf] = g
            return free
        return f

    p.wait("tensor", "ag", ag1_done)
    p.wait("tensor", "Tl", lfm_done)
    p.wait("sync", "ag", ag1_done)
    p.wait("vector", "vv", rope_done)
    pg = None
    for kind, off_m in ([] if "qkv" in SKIP else
                        (("q", OFF_Q), ("k", OFF_K), ("v", OFF_V))):
        for pc in range(4):
            pg = mm_group8(lfm_lhs,
                           lambda kc, off_m=off_m, pc=pc:
                           ag1_tile(off_m, D, 128 * kc, 512 * pc, 512),
                           DC, qkv_evac(kind, pc), psum_guards=pg)
    for _ln in EVL:
        p.wait("gpsimd", _ln, p.val(_ln))
    ag2_done = p.inc("ag")
    p.on("gpsimd", lambda e, S: e.collective_compute(
        "AllGather", ALU.bypass, replica_groups=[list(range(NP))],
        ins=[kv_in[:, :]], outs=[kv_out[:, :]]).then_inc(S["ag"], 1))

    if DEBUG_PHASE >= 2:
        p.wait("gpsimd", "ag", ag2_done)
        dma("Gd", lambda: dbg["qhat"][:, :], lambda: s_qhat[:, :])
        dma("Gd", lambda: dbg["kv"][:, :], lambda: kv_in[:, :])

    p.wait("scalar", "mm", p.val("mm"))
    for hp in range(NHP):
        dmaT("Tl",
             lambda hp=hp: Abf("qfm").rearrange(
                 "p (c f) -> p c f", f=1024)[:, hp, :],
             lambda hp=hp: s_qhat[:, 128 * hp:128 * (hp + 1)])
    qfm_done = p.val("Tl")

    def emit_scores_arm(s):
        b, c = divmod(s, 4)
        B0 = 1024 * c
        st = {"vp_i": 0, "vp_guard": {}, "A_guard": {}, "S_guard": {}}

        def khp_load(hp, lo, hi):
            if st.get("last_smm") is not None:
                p.wait("scalar", "tt", st["last_smm"])
            off = 0
            v = None
            t0 = lo
            while t0 < hi:
                r = 4 * b + t0 // 1024
                r0 = t0 % 1024
                r1 = min(1024, r0 + (hi - t0))
                n = r1 - r0
                v = dmaT("Tk",
                         lambda off=off, n=n: Abf("khp")[:, off:off + n],
                         lambda r=r, r0=r0, r1=r1, hp=hp: kvk_ap(r, r0, r1, hp))
                off += n
                t0 += n
            return v

        def do_pair(hp, qp, kcs, first, last_flags):
            Q0 = B0 + 512 * qp
            for i, (ktok, kind) in enumerate(kcs):
                par = (ktok // 128) % 2
                sA, sB = f"ps{2 * par}", f"ps{2 * par + 1}"
                vb = ["vp0", "vp1"][st["vp_i"] % 2]
                st["vp_i"] += 1
                gv = st["vp_guard"].get(vb)
                if gv is not None:
                    p.wait("sync", gv[0], gv[1])
                r = 4 * b + ktok // 1024
                r0 = ktok % 1024
                vld = dma("V0" if vb == "vp0" else "V1",
                          lambda vb=vb: Abf(vb)[:, :],
                          lambda r=r, r0=r0, hp=hp: kvv_ap(r, r0, r0 + 128, hp))
                gS = st["S_guard"].get(par)
                if gS is not None:
                    p.wait("tensor", gS[0], gS[1])
                koff = None
                smm = None
                kofs = st["koff"][ktok]
                op("tensor", lambda e, S, sA=sA, kofs=kofs, hp=hp, qp=qp:
                   e.matmul(ps[sA][:, :],
                            Abf("khp")[0:64, kofs:kofs + 128],
                            Abf("qfm").rearrange(
                                "p (c f) -> p c f", f=1024)[
                                0:64, hp, 512 * qp:512 * (qp + 1)],
                            start=True, stop=True, tile_position=(0, 0)),
                   inc=True, sem="tt")
                smm = op("tensor", lambda e, S, sB=sB, kofs=kofs, hp=hp, qp=qp:
                         e.matmul(ps[sB][:, :],
                                  Abf("khp")[64:128, kofs:kofs + 128],
                                  Abf("qfm").rearrange(
                                      "p (c f) -> p c f", f=1024)[
                                      64:128, hp, 512 * qp:512 * (qp + 1)],
                                  start=True, stop=True,
                                  tile_position=(64, 0)), inc=True, sem="tt")
                aA, aB = f"A{2 * par}", f"A{2 * par + 1}"
                for head, (sX, aX) in enumerate(((sA, aA), (sB, aB))):
                    gA = st["A_guard"].get(aX)
                    if gA is not None:
                        p.wait("scalar", gA[0], gA[1])
                    p.wait("scalar", "tt", smm)
                    if kind == "diag":
                        j = (ktok - Q0) // 128
                        assert 0 <= j < 4
                        p.wait("vector", "tt", smm)
                        if st.get("mskf_free") is not None:
                            p.wait("vector", "ss", st["mskf_free"])
                        mk = op("vector", lambda e, S, sX=sX, j=j:
                                e.tensor_tensor(
                                    out=A("msk_f"), in0=ps[sX][:, :],
                                    in1=Abf("mskA")[:, 512 * j:512 * (j + 1)],
                                    op=ALU.add), inc=True)
                        p.wait("scalar", "vv", mk)
                        ex = op("scalar", lambda e, S, aX=aX: e.activation(
                            out=Abf(aX)[:, :], in_=A("msk_f"), func=AF.Exp,
                            scale=0.125, bias=Asm("bm10")), inc=True)
                        st["mskf_free"] = ex
                    else:
                        ex = op("scalar", lambda e, S, sX=sX, aX=aX:
                                e.activation(
                                    out=Abf(aX)[:, :], in_=ps[sX][:, :],
                                    func=AF.Exp, scale=0.125,
                                    bias=Asm("bm10")), inc=True)
                    st[f"exp_{head}"] = ex
                st["S_guard"][par] = ("ss", st["exp_1"])
                st["last_smm"] = smm
                exA, exB = st["exp_0"], st["exp_1"]
                p.wait("tensor", "ss", exB)
                p.wait("tensor", "V0" if vb == "vp0" else "V1", vld)
                stt = (first and i == 0)
                stp = (last_flags and i == len(kcs) - 1)
                op("tensor", lambda e, S, vb=vb, aA=aA, stt=stt, stp=stp:
                   e.matmul(ps["ps4"][0:64, :], Abf(vb)[:, 0:64],
                            Abf(aA)[:, :], start=stt, stop=stp,
                            tile_position=(0, 0)), inc=True, sem="tt")
                op("tensor", lambda e, S, vb=vb, aB=aB, stt=stt, stp=stp:
                   e.matmul(ps["ps5"][64:128, :], Abf(vb)[:, 64:128],
                            Abf(aB)[:, :], start=stt, stop=stp,
                            tile_position=(0, 64)), inc=True, sem="tt")
                op("tensor", lambda e, S, aA=aA, stt=stt, stp=stp:
                   e.matmul(ps["ps6"][0:1, :], Asm("onescol"),
                            Abf(aA)[:, :], start=stt, stop=stp,
                            tile_position=(0, 0)), inc=True, sem="tt")
                avv = op("tensor", lambda e, S, aB=aB, stt=stt, stp=stp:
                         e.matmul(ps["ps7"][64:65, :], Asm("onescol"),
                                  Abf(aB)[:, :], start=stt, stop=stp,
                                  tile_position=(0, 64)), inc=True, sem="tt")
                st["vp_guard"][vb] = ("tt", avv)
                st["A_guard"][aA] = ("tt", avv)
                st["A_guard"][aB] = ("tt", avv)
            return avv

        def finish_pair(hp, qp, av_done):
            p.wait("scalar", "tt", av_done)
            se1 = op("scalar", lambda e, S: e.copy(
                out=Asm("serow")[0:1, 0:512], in_=ps["ps6"][0:1, :]),
                inc=True)
            se2 = op("scalar", lambda e, S: e.copy(
                out=Asm("serow")[64:65, 0:512], in_=ps["ps7"][64:65, :]),
                inc=True)
            p.wait("vector", "ss", se2)
            op("vector", lambda e, S: e.reciprocal(
                out=Asm("serow")[0:1, 512:1024], in_=Asm("serow")[0:1, 0:512]))
            op("vector", lambda e, S: e.reciprocal(
                out=Asm("serow")[64:65, 512:1024],
                in_=Asm("serow")[64:65, 0:512]))
            p.drain("vector")
            op("vector", lambda e, S: e.tensor_copy(
                out=Abf("selo")[0:1, 0:512], in_=Asm("serow")[0:1, 512:1024]))
            op("vector", lambda e, S: e.tensor_copy(
                out=Abf("selo")[64:65, 0:512],
                in_=Asm("serow")[64:65, 512:1024]))
            p.drain("vector")
            op("vector", lambda e, S: e.tensor_tensor(
                out=Asm("serow")[0:1, 0:512],
                in0=Asm("serow")[0:1, 512:1024], in1=Abf("selo")[0:1, 0:512],
                op=ALU.subtract))
            op("vector", lambda e, S: e.tensor_tensor(
                out=Asm("serow")[64:65, 0:512],
                in0=Asm("serow")[64:65, 512:1024],
                in1=Abf("selo")[64:65, 0:512], op=ALU.subtract))
            p.drain("vector")
            op("vector", lambda e, S: e.tensor_copy(
                out=Abf("selo")[0:1, 512:1024], in_=Asm("serow")[0:1, 0:512]))
            lov = op("vector", lambda e, S: e.tensor_copy(
                out=Abf("selo")[64:65, 512:1024],
                in_=Asm("serow")[64:65, 0:512]), inc=True)
            p.wait("tensor", "vv", lov)
            op("tensor", lambda e, S: e.matmul(
                ps["ps6"][0:64, :], Asm("onesrow")[0:1, :],
                Abf("selo")[0:1, 0:512], start=True, stop=False,
                tile_position=(0, 0)), inc=True, sem="tt")
            op("tensor", lambda e, S: e.matmul(
                ps["ps6"][0:64, :], Asm("onesrow")[0:1, :],
                Abf("selo")[0:1, 512:1024], start=False, stop=True,
                tile_position=(0, 0)), inc=True, sem="tt")
            op("tensor", lambda e, S: e.matmul(
                ps["ps7"][64:128, :], Asm("onesrow")[64:65, :],
                Abf("selo")[64:65, 0:512], start=True, stop=False,
                tile_position=(64, 64)), inc=True, sem="tt")
            bcv2 = op("tensor", lambda e, S: e.matmul(
                ps["ps7"][64:128, :], Asm("onesrow")[64:65, :],
                Abf("selo")[64:65, 512:1024], start=False, stop=True,
                tile_position=(64, 64)), inc=True, sem="tt")
            p.wait("scalar", "tt", bcv2)
            r1v = op("scalar", lambda e, S: e.copy(
                out=A("smrcp")[0:64, :], in_=ps["ps6"][0:64, :]), inc=True)
            r2v = op("scalar", lambda e, S: e.copy(
                out=A("smrcp")[64:128, :], in_=ps["ps7"][64:128, :]),
                inc=True)
            p.wait("vector", "ss", r2v)
            op("vector", lambda e, S, hp=hp, qp=qp: e.tensor_tensor(
                out=Abf("afm").rearrange("p (c f) -> p c f", f=1024)[
                    0:64, hp, 512 * qp:512 * (qp + 1)],
                in0=ps["ps4"][0:64, :], in1=A("smrcp")[0:64, :],
                op=ALU.mult))
            fin = op("vector", lambda e, S, hp=hp, qp=qp: e.tensor_tensor(
                out=Abf("afm").rearrange("p (c f) -> p c f", f=1024)[
                    64:128, hp, 512 * qp:512 * (qp + 1)],
                in0=ps["ps5"][64:128, :], in1=A("smrcp")[64:128, :],
                op=ALU.mult), inc=True)
            p.wait("tensor", "vv", fin)
            return fin

        for hp in range(15):
            S0 = max(0, B0 - 512)
            span_end = B0 + 1024
            st["koff"] = {tok: tok - S0 for tok in range(S0, span_end, 128)}
            kl = khp_load(hp, S0, span_end)
            p.wait("tensor", "Tk", kl)
            for qp in range(2):
                Q0 = B0 + 512 * qp
                kcs = []
                for ktok in range(S0, span_end, 128):
                    if ktok >= Q0 + 512:
                        continue
                    kcs.append((ktok, "diag" if ktok + 127 > Q0 else "full"))
                av = do_pair(hp, qp, kcs, True, True)
                finish_pair(hp, qp, av)
        hp = 15
        for qp in range(2):
            Q0 = B0 + 512 * qp
            hi_all = Q0 + 512
            scs = [(lo, min(lo + 2048, hi_all))
                   for lo in range(0, hi_all, 2048)]
            first = True
            for si, (lo, hi) in enumerate(scs):
                st["koff"] = {tok: tok - lo for tok in range(lo, hi, 128)}
                kl = khp_load(hp, lo, hi)
                p.wait("tensor", "Tk", kl)
                kcs = []
                for ktok in range(lo, hi, 128):
                    kcs.append((ktok,
                                "diag" if ktok + 127 > Q0 else "full"))
                av = do_pair(hp, qp, kcs, first, si == len(scs) - 1)
                first = False
            finish_pair(hp, qp, av)

    if UPTO >= 3:
        p.wait("tensor", "ag", p.val("ag"))
        p.wait("tensor", "Tl", qfm_done)
        p.wait("tensor", "vv", consts_v)
        p.wait("scalar", "Tl", qfm_done)
        p.wait("scalar", "ag", p.val("ag"))
        p.wait("vector", "ag", p.val("ag"))
        for _ln in EVL:
            p.wait("vector", _ln, p.val(_ln))
            p.wait("scalar", _ln, p.val(_ln))
        p.wait("sync", "ag", p.val("ag"))

        sw_engines = ("tensor", "vector", "scalar", "sync")
        snap = dict(p.cnt)
        snap_mw = dict(p.max_waiter)
        snap_gt = dict(p.gated_to)
        collected = {a: {eng: [] for eng in sw_engines} for a in range(8)}
        finals = []
        fin_mw, fin_gt = [], []
        for arm in range(8):
            p.cnt = dict(snap)
            p.max_waiter = dict(snap_mw)
            p.gated_to = dict(snap_gt)
            saved = {eng: p.ops[eng] for eng in sw_engines}
            for eng in sw_engines:
                p.ops[eng] = collected[arm][eng]
            emit_scores_arm(arm)
            for eng in sw_engines:
                p.ops[eng] = saved[eng]
            finals.append(dict(p.cnt))
            fin_mw.append(dict(p.max_waiter))
            fin_gt.append(dict(p.gated_to))
        canon = {}
        for f in finals + [snap]:
            for k, v in f.items():
                canon[k] = max(canon.get(k, 0), v)
        for arm in range(8):
            fin = finals[arm]
            for eng in sw_engines:
                tail = []
                for semn in ESEM[eng]:
                    fv = fin.get(semn, 0)
                    cv = canon.get(semn, 0)
                    tail.append((semn, fv, cv - fv))
                ops_list = collected[arm][eng]
                for semn, fv, pad in tail:
                    if fv > 0:
                        ops_list.append(
                            lambda e, S, n=semn, v=fv: e.wait_ge(S[n], v))
                    if pad > 0:
                        ops_list.append(
                            lambda e, S, n=semn, d=pad: e.sem_inc(S[n], d))
        p.cnt = canon
        p.max_waiter = {k: max(canon.get(k, 0),
                               max((m.get(k, 0) for m in fin_mw), default=0))
                        for k in set(canon) | set(snap_mw)}
        p.gated_to = {}
        for eng in sw_engines:
            def mk_switch(eng):
                arms_ops = [collected[a][eng] for a in range(8)]

                def fn(e, S):
                    pid = e.partition_id()
                    for arm in e.Switch(pid, 8):
                        for f in arms_ops[arm]:
                            f(e, S)
                return fn
            p.on(eng, mk_switch(eng))

        if DEBUG_PHASE >= 3:
            pass

    if UPTO >= 4:
        def afm_lhs(kc, tc):
            return Abf("afm").rearrange("p (c f) -> p c f", f=1024)[
                :, kc, 128 * tc:128 * (tc + 1)]

        ZEV = [("Z0", 0), ("Z1", 512), ("Z2", 1024), ("Z3", 1536)]
        zev_state = {"i": 0, "guard": {}}

        def wo_evac(pc):
            def f(tc, mm_target):
                zlane, off = ZEV[zev_state["i"] % 4]
                zev_state["i"] += 1
                g = zev_state["guard"].get(off)
                if g is not None:
                    p.wait("scalar", zlane, g)
                p.wait("scalar", "mm", mm_target)
                evv = op("scalar", lambda e, S, off=off, tc=tc: e.activation(
                    out=A("z_t")[:, off:off + 512], in_=ps[f"ps{tc}"][:, :],
                    func=AF.Identity), inc=True)
                p.wait("scalar", "ss", evv)
                st2 = dma(zlane,
                          lambda tc=tc, pc=pc: s_z[128 * tc:128 * (tc + 1),
                                                   512 * pc:512 * (pc + 1)],
                          lambda off=off: A("z_t")[:, off:off + 512])
                zev_state["guard"][off] = st2
                return ("ss", evv)
            return f

        pg2 = pg
        pg2 = [("vv", p.val("vv"))] * TC
        for pc in range(4):
            pg2 = mm_group8(afm_lhs,
                            lambda kc, pc=pc:
                            ag1_tile(OFF_O, D, 128 * kc, 512 * pc, 512),
                            DC, wo_evac(pc), psum_guards=pg2,
                            bufs=["wrhsB0", "wrhsB1", "wrhsB2", "wrhsB3"])
        wo_done = {zl: p.val(zl) for zl in ("Z0", "Z1", "Z2", "Z3")}

        if DEBUG_PHASE >= 3:
            for zl, zv in wo_done.items():
                p.wait("gpsimd", zl, zv)
            dma("Gd", lambda: dbg["z"][:, :], lambda: s_z[:, :])

        for nm, src in (("bo_b", bo_in), ("lnw_b", lnw_in), ("lnb_b", lnb_in)):
            t = dma("Gd", lambda nm=nm: A(nm),
                    lambda src=src: src[0:1, :].broadcast_to([128, D]))
        bco = p.val("Gd")
        p.wait("vector", "Gd", bco)
        p.wait("scalar", "Gd", bco)

        for zl, zv in wo_done.items():
            p.wait("sync", zl, zv)
        for tc in range(TC):
            tz = dma("Lt", lambda: A("t_t"),
                     lambda tc=tc: s_z[128 * tc:128 * (tc + 1), :])
            tx = dma("Lx", lambda: A("x_t"),
                     lambda tc=tc: x_in[128 * tc:128 * (tc + 1), :])
            p.wait("vector", "Lt", tz)
            p.wait("vector", "Lx", tx)
            zb = op("vector", lambda e, S: e.tensor_tensor(
                out=A("z_t"), in0=A("t_t"), in1=A("bo_b"), op=ALU.add),
                inc=True)
            p.wait("scalar", "vv", zb)
            ln_stats(lambda: A("z_t"))
            ap2 = op("scalar", lambda e, S: e.activation(
                out=A("t_t"), in_=A("z_t"), func=AF.Identity,
                scale=Asm("ln_r"), bias=Asm("ln_nmr")), inc=True)
            p.wait("vector", "ss", ap2)
            x1v = op("vector", lambda e, S: e.tensor_tensor(
                out=A("z_t"), in0=A("t_t"), in1=A("x_t"), op=ALU.add),
                inc=True)
            p.wait("sync", "vv", x1v)
            sx1 = dma("Ls", lambda tc=tc: s_x1[128 * tc:128 * (tc + 1), :],
                      lambda: A("z_t"))
            p.wait("scalar", "vv", x1v)
            ln_stats(lambda: A("z_t"))
            h0v = op("scalar", lambda e, S: e.activation(
                out=Abf("h0st"), in_=A("z_t"), func=AF.Identity,
                scale=Asm("ln_r"), bias=Asm("ln_nmr")), inc=True)
            p.wait("sync", "ss", h0v)
            dma("Ls", lambda tc=tc: s_h0[128 * tc:128 * (tc + 1), :],
                lambda: Abf("h0st"))
            p.wait("scalar", "ss", h0v)
            ln_stats(lambda: Abf("h0st"))
            t2v = op("scalar", lambda e, S: e.activation(
                out=A("t_t"), in_=Abf("h0st"), func=AF.Identity,
                scale=Asm("ln_r"), bias=Asm("ln_nmr")), inc=True)
            p.wait("vector", "ss", t2v)
            p.wait("vector", "Ls")
            op("vector", lambda e, S: e.tensor_tensor(
                out=A("z_t"), in0=A("t_t"), in1=A("lnw_b"), op=ALU.mult))
            p.drain("vector")
            hv = op("vector", lambda e, S: e.tensor_tensor(
                out=Abf("hst"), in0=A("z_t"), in1=A("lnb_b"), op=ALU.add),
                inc=True)
            p.wait("vector", "vv", hv)
            for half in range(2):
                qf = quant_chain(
                    lambda half=half: Abf("hst")[:, 1024 * half:1024 * (half + 1)],
                    lambda half=half: Abf("hqst")[:, 1024 * half:1024 * (half + 1)],
                    1024, vbuf="q2_v", rbuf="q2_r")
            p.wait("sync", "vv", qf)
            dma("Ls", lambda tc=tc: s_hq[128 * tc:128 * (tc + 1), :],
                lambda: Abf("hqst"))
            p.wait("sync", "vv", qf)
        hq_stored = p.val("Ls")

        if DEBUG_PHASE >= 4:
            p.wait("gpsimd", "Ls", hq_stored)
            dma("Gd", lambda: dbg["x1"][:, :], lambda: s_x1[:, :])
            dma("Gd", lambda: dbg["hq"][:, :], lambda: s_hq[:, :])

    if UPTO >= 5:
        p.wait("scalar", "Ls", hq_stored)
        for dc in range(DC):
            dmaT("Tl",
                 lambda dc=dc: Abf("hqfm").rearrange(
                     "p (c f) -> p c f", f=1024)[:, dc, :],
                 lambda dc=dc: s_hq[:, 128 * dc:128 * (dc + 1)])
        hqfm_done = p.val("Tl")

        def hq_lhs(kc, tc):
            return Abf("hqfm").rearrange("p (c f) -> p c f", f=1024)[
                :, kc, 128 * tc:128 * (tc + 1)]

        p.wait("gpsimd", "vv", p.val("vv"))
        p.wait("gpsimd", "ss", p.val("ss"))
        dma("Gd", lambda: A("b1full"),
            lambda: b1_in[0:1, :].broadcast_to([128, DFF]))
        tb2 = dma("Gd", lambda: A("b2full"),
                  lambda: b2_in[0:1, :].broadcast_to([128, D]))

        w1st = {"i": 0, "g_y1p": [None] * 6, "g_y1g": [None] * 6,
                "g_y1h": [None] * 6, "g_gqs": [None] * 6,
                "grp_st": [], "stash": []}

        def w1_flush():
            yvs = []
            for (tc, sl, evv, jc) in w1st["stash"]:
                p.wait("vector", "ss", evv)
                if w1st["g_y1g"][sl] is not None:
                    p.wait("vector", "ss", w1st["g_y1g"][sl])
                yv = op("vector", lambda e, S, sl=sl, jc=jc:
                        e.tensor_tensor(
                            out=Abf(f"y1g{sl}")[:, :], in0=A(f"y1p{sl}"),
                            in1=A("b1full")[:, 512 * jc:512 * (jc + 1)],
                            op=ALU.add), inc=True)
                w1st["g_y1p"][sl] = yv
                yvs.append(yv)
            for (tc, sl, evv, jc), yv in zip(w1st["stash"], yvs):
                el = EVL[w1st["i"] & 3]
                w1st["i"] += 1
                p.wait("scalar", "vv", yv)
                if w1st["g_y1h"][sl] is not None:
                    p.wait("scalar", "vv", w1st["g_y1h"][sl])
                gv = op("scalar", lambda e, S, sl=sl: e.activation(
                    out=Abf(f"y1h{sl}")[:, :], in_=Abf(f"y1g{sl}")[:, :],
                    func=AF.Gelu_apprx_tanh), inc=True)
                w1st["g_y1g"][sl] = gv
                p.wait("vector", "ss", gv)
                if w1st["g_gqs"][sl] is not None:
                    gl, gvv = w1st["g_gqs"][sl]
                    p.wait("vector", gl, p.val(gl))
                qv = quant_chain(lambda sl=sl: Abf(f"y1h{sl}")[:, :],
                                 lambda sl=sl: Abf(f"gqs{sl}")[:, :],
                                 512, vbuf="q2_v", rbuf="q2_r")
                w1st["g_y1h"][sl] = qv
                p.wait("gpsimd", "vv", qv)
                stv = dma("Vs",
                          lambda tc=tc, jc=jc: s_gq[128 * tc:128 * (tc + 1),
                                                    512 * jc:512 * (jc + 1)],
                          lambda sl=sl: Abf(f"gqs{sl}")[:, :])
                w1st["g_gqs"][sl] = ("Vs", stv)
                w1st["grp_st"].append(("Vs", stv))
            w1st["stash"] = []

        def w1_evac(jc):
            def f(tc, mm_target):
                sl = (w1st["i"] + len(w1st["stash"])) % 6
                p.wait("scalar", "mm", mm_target)
                if w1st["g_y1p"][sl] is not None:
                    p.wait("scalar", "vv", w1st["g_y1p"][sl])
                evv = op("scalar", lambda e, S, sl=sl, tc=tc: e.activation(
                    out=A(f"y1p{sl}"), in_=ps[f"ps{tc}"][:, :],
                    func=AF.Identity), inc=True)
                w1st["stash"].append((tc, sl, evv, jc))
                if len(w1st["stash"]) == 6 or tc == TC - 1:
                    w1_flush()
                return ("ss", evv)
            return f

        pg3 = [("ss", p.val("ss"))] * TC
        p.wait("tensor", "Tl", hqfm_done)
        p.wait("vector", "Gd", tb2)
        grp_done = []
        for jc in range(16):
            w1st["grp_st"] = []
            pg3 = mm_group8(hq_lhs,
                            lambda kc, jc=jc:
                            ag1_tile(OFF_W1, DFF, 128 * kc, 512 * jc, 512),
                            DC, w1_evac(jc), psum_guards=pg3)
            lanemax = {}
            for el, sv in w1st["grp_st"]:
                lanemax[el] = max(lanemax.get(el, 0), sv)
            grp_done.append(lanemax)
        w1_mm_done = p.val("mm")

        if DEBUG_PHASE >= 5:
            for el, sv in grp_done[-1].items():
                p.wait("gpsimd", el, sv)
            for lm in grp_done:
                for el, sv in lm.items():
                    p.wait("gpsimd", el, sv)
            dma("Gd", lambda: dbg["gq"][:, :], lambda: s_gq[:, :])

        w2st = {"guard": [None, None]}

        def w2_pre(kc):
            sl = kc & 1
            g = w2st["guard"][sl]
            if g is not None:
                p.wait("scalar", g[0], g[1])
            for el, sv in grp_done[kc // 4].items():
                p.wait("scalar", el, sv)
            t = dmaT("Tl",
                     lambda sl=sl: Abf(f"glhs{sl}")[:, :],
                     lambda kc=kc: s_gq[:, 128 * kc:128 * (kc + 1)])
            if kc == 63:
                w2st["guard"][sl] = ("mm", p.val("mm") + TC)
            else:
                w2st["guard"][sl] = ("mq", p.val("mq") + TC)
            p.wait("tensor", "Tl", t)

        def gq_lhs(kc, tc):
            return Abf(f"glhs{kc & 1}")[:, 128 * tc:128 * (tc + 1)]

        h0_state = {"g": [None, None], "fg": [None, None], "pend": {}}

        def w2_load_h0(oc, tc):
            sl = tc & 1
            g = h0_state["g"][sl]
            if g is not None:
                p.wait("gpsimd", "vv", g)
            th = dma("Vs", lambda sl=sl: Abf(f"h0t{sl}")[:, :],
                     lambda tc=tc, oc=oc: s_h0[128 * tc:128 * (tc + 1),
                                               512 * oc:512 * (oc + 1)])
            h0_state["pend"][(oc, tc)] = th

        def w2_evac(oc):
            def f(tc, mm_target):
                sl = tc & 1
                fsl = w1st["i"] & 1
                w1st["i"] += 1
                if (oc, tc) not in h0_state["pend"]:
                    w2_load_h0(oc, tc)
                th = h0_state["pend"].pop((oc, tc))
                nxt = (oc, tc + 1) if tc < TC - 1 else                     ((oc + 1, 0) if oc < 3 else None)
                if nxt is not None:
                    w2_load_h0(*nxt)
                p.wait("vector", "mm", mm_target)
                p.wait("vector", "Vs", p.val("Vs"))
                if h0_state.get("pfv") is not None:
                    p.wait("vector", "vv", h0_state["pfv"])
                yv = op("vector", lambda e, S, tc=tc, oc=oc: e.tensor_tensor(
                    out=A("y2_f"), in0=ps[f"ps{tc}"][:, :],
                    in1=A("b2full")[:, 512 * oc:512 * (oc + 1)],
                    op=ALU.add), inc=True)
                p.drain("vector")
                if h0_state["fg"][fsl] is not None:
                    p.wait("vector", "Vs", p.val("Vs"))
                fv = op("vector", lambda e, S, sl=sl, fsl=fsl:
                        e.tensor_tensor(
                            out=Abf(f"fot{fsl}")[:, :], in0=A("y2_f"),
                            in1=Abf(f"h0t{sl}")[:, :], op=ALU.add), inc=True)
                h0_state["g"][sl] = fv
                h0_state["pfv"] = fv
                p.wait("gpsimd", "vv", fv)
                sv = dma("Vs",
                         lambda tc=tc, oc=oc: s_fo[128 * tc:128 * (tc + 1),
                                                   512 * oc:512 * (oc + 1)],
                         lambda fsl=fsl: Abf(f"fot{fsl}")[:, :])
                h0_state["fg"][fsl] = sv
                return ("vv", yv)
            return f

        pg4 = pg3
        for oc in range(4):
            pg4 = mm_group8(gq_lhs,
                            lambda kc, oc=oc:
                            ag1_tile(OFF_W2, D, 128 * kc, 512 * oc, 512, 1024),
                            64, w2_evac(oc), psum_guards=pg4,
                            bufs=["wrhsD0", "wrhsD1"], pre_kc=w2_pre)
        fo_stored = p.val("Vs")

        p.wait("sync", "Vs", fo_stored)
        p.wait("scalar", "mm", p.val("mm"))
        fprev = None
        for tc in range(TC):
            if fprev is not None:
                p.wait("sync", "vv", fprev)
            tf = dma("Lg", lambda: Abf("fo_ld"),
                     lambda tc=tc: s_fo[128 * tc:128 * (tc + 1), :])
            tx1 = dma("Lt", lambda: A("t_t"),
                      lambda tc=tc: s_x1[128 * tc:128 * (tc + 1), :])
            p.wait("scalar", "Lg", tf)
            p.wait("scalar", "Lt", tx1)
            p.wait("vector", "Lt", tx1)
            ln_stats(lambda: Abf("fo_ld"))
            apf = op("scalar", lambda e, S: e.activation(
                out=A("z_t"), in_=Abf("fo_ld"), func=AF.Identity,
                scale=Asm("ln_r"), bias=Asm("ln_nmr")), inc=True)
            p.wait("vector", "ss", apf)
            if fprev is not None:
                p.wait("vector", "Ls", syp)
            yv = op("vector", lambda e, S: e.tensor_tensor(
                out=A("x_t"), in0=A("z_t"), in1=A("t_t"), op=ALU.add),
                inc=True)
            p.wait("sync", "vv", yv)
            syp = dma("Ls", lambda tc=tc: y_out[128 * tc:128 * (tc + 1), :],
                      lambda: A("x_t"))
            fprev = yv
    else:
        pv = None
        for tc in range(TC):
            if pv is not None:
                p.wait("sync", "Ls", pv)
            tl0 = dma("Lx", lambda: A("x_t"),
                      lambda tc=tc: x_in[128 * tc:128 * (tc + 1), :])
            p.wait("sync", "Lx", tl0)
            pv = dma("Ls", lambda tc=tc: y_out[128 * tc:128 * (tc + 1), :],
                     lambda: A("x_t"))

    for _ln in SYNC_LANES + SCAL_LANES + GP_LANES:
        if p.val(_ln) > 0:
            p.wait("gpsimd", _ln, p.val(_ln))

    import contextlib
    with contextlib.ExitStack() as stack:
        sb["AF32"] = stack.enter_context(
            nc.sbuf_tensor("AF32", [128, NF32], F32))
        sb["ABF"] = stack.enter_context(
            nc.sbuf_tensor("ABF", [128, NBF], BF16))
        for name, (shape, dt) in sm.items():
            sb[name] = stack.enter_context(nc.sbuf_tensor(name, shape, dt))
        for i in range(8):
            ps[f"ps{i}"] = stack.enter_context(
                nc.psum_tensor(f"ps{i}", [128, 512], F32))
        S = {n: stack.enter_context(nc.semaphore(n)) for n in SEM_NAMES}
        p.emit(nc, S)
    return nc


def kernel(L0_attn_hidden_states, L0_attn_Wq, L0_attn_Wk, L0_attn_Wv,
           L0_attn_Wo, L0_attn_bo, L0_ffn_ln_w, L0_ffn_ln_b, L0_ffn_W1,
           L0_ffn_b1, L0_ffn_W2, L0_ffn_b2):
    x = np.ascontiguousarray(np.asarray(L0_attn_hidden_states, np.float32))
    Wq = np.asarray(L0_attn_Wq, np.float32)
    Wk = np.asarray(L0_attn_Wk, np.float32)
    Wv = np.asarray(L0_attn_Wv, np.float32)
    Wo = np.asarray(L0_attn_Wo, np.float32)
    W1 = np.asarray(L0_ffn_W1, np.float32)
    W2 = np.asarray(L0_ffn_W2, np.float32)
    in_maps = []
    for s in range(NP):
        b, c = divmod(s, 4)
        in_maps.append({
            "x": np.ascontiguousarray(x[b, 1024 * c:1024 * (c + 1), :]),
            "pos": np.arange(1024 * c, 1024 * (c + 1),
                             dtype=np.float32)[None, :],
            "wq": np.ascontiguousarray(Wq[:, 256 * s:256 * (s + 1)]),
            "wk": np.ascontiguousarray(Wk[:, 256 * s:256 * (s + 1)]),
            "wv": np.ascontiguousarray(Wv[:, 256 * s:256 * (s + 1)]),
            "wo": np.ascontiguousarray(Wo[:, 256 * s:256 * (s + 1)]),
            "w1": np.ascontiguousarray(W1[:, 256 * s:256 * (s + 1)]),
            "w2": np.ascontiguousarray(W2[:, 1024 * s:1024 * (s + 1)]),
            "bo": np.asarray(L0_attn_bo, np.float32)[None, :],
            "lnw": np.asarray(L0_ffn_ln_w, np.float32)[None, :],
            "lnb": np.asarray(L0_ffn_ln_b, np.float32)[None, :],
            "b1": np.asarray(L0_ffn_b1, np.float32)[None, :],
            "b2": np.asarray(L0_ffn_b2, np.float32)[None, :],
        })
    nc = build_nc()
    res = run_bass_kernel_spmd(nc, in_maps, core_ids=list(range(NP)))
    out = np.empty((B, L, D), np.float32)
    for s in range(NP):
        b, c = divmod(s, 4)
        out[b, 1024 * c:1024 * (c + 1), :] = res.results[s]["y"]
    return out
